# version 12
# baseline (speedup 1.0000x reference)
import contextlib
import numpy as np
import ml_dtypes
import concourse.bass as bass
import concourse.mybir as mybir
from concourse.bass_utils import run_bass_kernel_spmd

F32 = mybir.dt.float32
BF16 = mybir.dt.bfloat16
AF = mybir.ActivationFunctionType
ALU = mybir.AluOpType
AX = mybir.AxisListType

NCORES = 8
B = 16
BPC = 2
T = 2048
NT = T // 128
D = 1024
DC = D // 128
NH = 16
HD = 64
HMoE = 2048
NE = 8
BF = ml_dtypes.bfloat16

ENGS = ("tensor", "vector", "scalar", "gpsimd", "sync")

L1DBG = {}
DEBUG_L0ATT = False


class Ctx:

    def __init__(self, nc):
        self.nc = nc
        self.q = {e: [] for e in ENGS}
        self.sems = {}
        self.cnt = {}
        self._stack = None

    def sem(self, name):
        h = self.nc.semaphore(name)
        handle = h.__enter__()
        self.sems[name] = handle
        self.cnt[name] = 0
        return handle

    def val(self, name):
        return self.cnt[name]

    def op(self, eng, fn, waits=(), inc=None, n=1):
        waits = list(waits)
        if inc is not None and eng in ("vector", "scalar", "gpsimd"):
            waits.append((inc, self.cnt[inc]))
        waits = tuple((self.sems[s], v) for (s, v) in waits)
        if inc is not None:
            self.cnt[inc] += n
            inc_h = (self.sems[inc], n)
        else:
            inc_h = None

        def emit(e, fn=fn, waits=waits, inc_h=inc_h):
            for (h, v) in waits:
                e.wait_ge(h, v)
            instr = fn(e)
            if inc_h is not None:
                assert instr is not None
                instr.then_inc(inc_h[0], inc_h[1])

        self.q[eng].append(emit)
        return self.cnt[inc] if inc is not None else None

    def dma(self, eng, out, in_, waits=(), inc="dma", **kw):
        waits = list(waits) + [(inc, self.cnt[inc])]
        return self.op(eng, lambda e: e.dma_start(out=out, in_=in_, **kw),
                       waits=waits, inc=inc, n=16)

    def wait(self, eng, semname, value=None):
        v = self.cnt[semname] if value is None else value
        h = self.sems[semname]
        self.q[eng].append(lambda e, h=h, v=v: e.wait_ge(h, v))

    def emit(self, block):
        for eng in ENGS:
            ops = self.q[eng]
            if not ops:
                continue
            fn = {"tensor": block.tensor, "vector": block.vector,
                  "scalar": block.scalar, "gpsimd": block.gpsimd,
                  "sync": block.sync}[eng]

            def body(e, ops=ops):
                for op in ops:
                    op(e)

            fn(body)

    def flush(self, block):
        self.emit(block)
        self.q = {e: [] for e in ENGS}



def emit_weight_transpose(ctx, w_hbm, wT, wstrip, idb, pT, nrows, ncols):
    nrc = nrows // 128
    ncc = ncols // 128
    ev = {}
    i = 0
    v0 = ctx.val("v")
    for r in range(nrc):
        waits = [("t_wt", ctx.val("t_wt"))] if r > 0 else []
        ctx.dma("gpsimd", wstrip[:], w_hbm[128 * r:128 * (r + 1), :],
                waits=waits, inc="dma_wt")
        for c in range(ncc):
            ctx.op("tensor",
                   lambda e, r=r, c=c: e.transpose(
                       pT[c % 2][:], wstrip[:, 128 * c:128 * (c + 1)], idb[:]),
                   waits=[("dma_wt", ctx.val("dma_wt"))] +
                         ([("v_wt", ev[i - 2])] if i >= 2 else
                          [("v", v0)]),
                   inc="t_wt")
            ctx.op("vector",
                   lambda e, r=r, c=c: e.tensor_copy(
                       wT[:, c * nrows + 128 * r: c * nrows + 128 * (r + 1)],
                       pT[c % 2][:]),
                   waits=[("t_wt", ctx.val("t_wt"))], inc="v_wt")
            ev[i] = ctx.val("v_wt")
            i += 1


def emit_col_from_row(ctx, row_sb, col_sb, ones1, psum1, n, name="bias"):
    for c in range(n // 128):
        ctx.op("tensor",
               lambda e, c=c: e.matmul(psum1, row_sb[:, 128 * c:128 * (c + 1)],
                                       ones1, start=True, stop=True),
               waits=[("v_misc", ctx.val("v_misc"))], inc="t_misc")
        ctx.op("vector",
               lambda e, c=c: e.tensor_copy(col_sb[:, c:c + 1], psum1),
               waits=[("t_misc", ctx.val("t_misc"))], inc="v_misc")


def emit_bcast_row(ctx, row_sb, out_sb, ones128, psumw, n):
    for g in range(0, n, 512):
        w = min(512, n - g)
        ctx.op("tensor",
               lambda e, g=g, w=w: e.matmul(psumw[:, 0:w], ones128,
                                            row_sb[:, g:g + w], start=True, stop=True),
               waits=[("v_misc", ctx.val("v_misc"))], inc="t_misc")
        ctx.op("vector",
               lambda e, g=g, w=w: e.tensor_copy(out_sb[:, g:g + w], psumw[:, 0:w]),
               waits=[("t_misc", ctx.val("t_misc"))], inc="v_misc")



def emit_l0ln(nc, ctx, block, io):
    x0 = io["x0"]
    wg = io["l0wg"]
    xnT = io["xnT"]
    scores_o = io["scores"]
    W = 4

    with (
        nc.sbuf_tensor("xt", [128, W * D], F32) as xt,
        nc.sbuf_tensor("xc", [128, W * D], F32) as xc,
        nc.sbuf_tensor("sq", [128, D], F32) as sq,
        nc.sbuf_tensor("xnb", [128, W * D], BF16) as xnb,
        nc.sbuf_tensor("stat", [128, W * 16], F32) as stat,
        nc.sbuf_tensor("wgb", [128, DC * NE], BF16) as wgb,
        nc.sbuf_tensor("xnT_sb", [128, 2 * D], BF16) as xnT_sb,
        nc.sbuf_tensor("idb", [128, 128], BF16) as idb,
        nc.sbuf_tensor("gat", [128, W * 6 * NE], F32) as gat,
        nc.sbuf_tensor("gcol", [128, W * 8], F32) as gcol,
        nc.psum_tensor("pT0", [128, 128], BF16) as pT0,
        nc.psum_tensor("pT1", [128, 128], BF16) as pT1,
        nc.psum_tensor("pT2", [128, 128], BF16) as pT2,
        nc.psum_tensor("pT3", [128, 128], BF16) as pT3,
        nc.psum_tensor("pg", [128, W * NE], F32) as pg,
    ):
        pT = [pT0, pT1, pT2, pT3]

        ctx.op("gpsimd", lambda e: e.memset(idb[:], 1.0), inc="g")
        ctx.op("gpsimd", lambda e: e.affine_select(
            idb[:], idb[:], pattern=[[1, 128]], compare_op=ALU.is_ge, fill=0.0,
            base=0, channel_multiplier=-1), waits=[("g", ctx.val("g"))], inc="g")
        ctx.op("gpsimd", lambda e: e.affine_select(
            idb[:], idb[:], pattern=[[-1, 128]], compare_op=ALU.is_ge, fill=0.0,
            base=0, channel_multiplier=1), waits=[("g", ctx.val("g"))], inc="g")
        ctx.dma("gpsimd", wgb[:].rearrange("p (c e) -> p c e", e=NE),
                wg.rearrange("(c p) e -> p c e", p=128), inc="dmag")

        id_done = ctx.val("g")
        wg_done = ctx.val("dmag")
        exps_prev = 0

        for b in range(BPC):
            for w in range(NT // W):
                ts0 = w * W
                prev_s = ctx.val("s")
                for i in range(W):
                    t = ts0 + i
                    ctx.dma("sync", xt[:, i * D:(i + 1) * D],
                            x0[b, 128 * t:128 * (t + 1), :],
                            waits=[("s", prev_s)] if (b or w) else [])
                x_ld = ctx.val("dma")
                d_scores_prev = ctx.val("dma_out")
                for i in range(W):
                    ctx.op("vector", lambda e, i=i: e.reduce_sum(
                        stat[:, 16 * i:16 * i + 1], xt[:, i * D:(i + 1) * D],
                        axis=AX.X), waits=[("dma", x_ld)], inc="v")
                for i in range(W):
                    ctx.op("vector", lambda e, i=i: e.tensor_scalar(
                        stat[:, 16 * i + 1:16 * i + 2], stat[:, 16 * i:16 * i + 1],
                        -1.0 / D, None, op0=ALU.mult),
                        waits=[("v", ctx.val("v"))], inc="v")
                vst = ctx.val("v")
                for i in range(W):
                    ctx.op("scalar", lambda e, i=i: e.activation(
                        xc[:, i * D:(i + 1) * D], xt[:, i * D:(i + 1) * D],
                        AF.Identity, bias=stat[:, 16 * i + 1:16 * i + 2], scale=1.0),
                        waits=[("v", vst)], inc="s")
                    ctx.op("scalar", lambda e, i=i: e.activation(
                        sq[:], xc[:, i * D:(i + 1) * D], AF.Square,
                        accum_out=stat[:, 16 * i + 2:16 * i + 3]), inc="s")
                sst = ctx.val("s")
                for i in range(W):
                    ctx.op("vector", lambda e, i=i: e.tensor_scalar(
                        stat[:, 16 * i + 3:16 * i + 4], stat[:, 16 * i + 2:16 * i + 3],
                        1.0 / D, 1e-5, op0=ALU.mult, op1=ALU.add),
                        waits=[("s", sst), ("v", ctx.val("v"))], inc="v")
                    ctx.op("vector", lambda e, i=i: e.reciprocal(
                        stat[:, 16 * i + 4:16 * i + 5], stat[:, 16 * i + 3:16 * i + 4]),
                        waits=[("v", ctx.val("v"))], inc="v")
                vrv = ctx.val("v")
                for i in range(W):
                    ctx.op("scalar", lambda e, i=i: e.activation(
                        stat[:, 16 * i + 5:16 * i + 6], stat[:, 16 * i + 4:16 * i + 5],
                        AF.Sqrt), waits=[("v", vrv)], inc="s")
                srs = ctx.val("s")
                for i in range(W):
                    ctx.op("vector", lambda e, i=i: e.tensor_scalar(
                        xnb[:, i * D:(i + 1) * D], xc[:, i * D:(i + 1) * D],
                        stat[:, 16 * i + 5:16 * i + 6], None, op0=ALU.mult),
                        waits=[("s", srs), ("v", ctx.val("v"))], inc="v")
                xn_done = ctx.val("v")

                for i in range(W):
                    t = ts0 + i
                    sl = i % 2
                    sbs = xnT_sb[:, sl * D:(sl + 1) * D]
                    ev = {}
                    d0 = ctx.val("dma_out")
                    for c in range(DC):
                        ctx.op("tensor", lambda e, i=i, c=c: e.transpose(
                            pT[c % 4][:], xnb[:, i * D + 128 * c:i * D + 128 * (c + 1)],
                            idb[:]),
                            waits=([("v", xn_done), ("g", id_done),
                                    ("dma_out", d0)] if c < 4 else
                                   [("v", ev[c - 4])]),
                            inc="t")
                        ctx.op("vector", lambda e, c=c, sbs=sbs: e.tensor_copy(
                            sbs[:, 128 * c:128 * (c + 1)], pT[c % 4][:]),
                            waits=[("t", ctx.val("t"))], inc="v")
                        ev[c] = ctx.val("v")
                    ctx.dma("sync", xnT[b, :, :, 128 * t:128 * (t + 1)]
                            .rearrange("c p q -> p c q"),
                            sbs[:].rearrange("p (c q) -> p c q", q=128),
                            waits=[("v", ctx.val("v"))], inc="dma_out")
                    for c in range(DC):
                        ctx.op("tensor", lambda e, i=i, c=c, sbs=sbs: e.matmul(
                            pg[:, NE * i:NE * (i + 1)],
                            sbs[:, 128 * c:128 * (c + 1)],
                            wgb[:, NE * c:NE * (c + 1)],
                            start=(c == 0), stop=(c == DC - 1)),
                            waits=[("dmag", wg_done), ("v", ev[DC - 1]),
                                   ("s", exps_prev)],
                            inc="t")
                tg = ctx.val("t")

                gs = lambda i, k: gat[:, (6 * i + k) * NE:(6 * i + k + 1) * NE]
                gc = lambda i, k: gcol[:, 8 * i + k:8 * i + k + 1]
                for i in range(W):
                    ctx.op("vector", lambda e, i=i: e.reduce_max(
                        gc(i, 0), pg[:, NE * i:NE * (i + 1)], axis=AX.X,
                        negate=True), waits=[("t", tg)], inc="v")
                vng = ctx.val("v")
                for i in range(W):
                    ctx.op("scalar", lambda e, i=i: e.activation(
                        gs(i, 0), pg[:, NE * i:NE * (i + 1)], AF.Exp,
                        bias=gc(i, 0), scale=1.0),
                        waits=[("v", vng)], inc="s")
                exps_prev = ctx.val("s")
                sxp = exps_prev
                for i in range(W):
                    ctx.op("vector", lambda e, i=i: e.reduce_max(
                        gc(i, 1), gs(i, 0), axis=AX.X),
                        waits=[("s", sxp), ("v", ctx.val("v"))], inc="v")
                for i in range(W):
                    ctx.op("vector", lambda e, i=i: e.tensor_scalar(
                        gs(i, 1), gs(i, 0), gc(i, 1), None, op0=ALU.is_ge),
                        waits=[("v", ctx.val("v"))], inc="v")
                for i in range(W):
                    ctx.op("vector", lambda e, i=i: e.tensor_scalar(
                        gs(i, 2), gs(i, 1), -1.0, 1.0, op0=ALU.mult, op1=ALU.add),
                        waits=[("v", ctx.val("v"))], inc="v")
                for i in range(W):
                    ctx.op("vector", lambda e, i=i: e.tensor_tensor(
                        gs(i, 3), gs(i, 0), gs(i, 2), op=ALU.mult),
                        waits=[("v", ctx.val("v"))], inc="v")
                for i in range(W):
                    ctx.op("vector", lambda e, i=i: e.reduce_max(
                        gc(i, 2), gs(i, 3), axis=AX.X),
                        waits=[("v", ctx.val("v"))], inc="v")
                for i in range(W):
                    ctx.op("vector", lambda e, i=i: e.tensor_scalar(
                        gs(i, 4), gs(i, 3), gc(i, 2), None, op0=ALU.is_ge),
                        waits=[("v", ctx.val("v"))], inc="v")
                for i in range(W):
                    ctx.op("vector", lambda e, i=i: e.tensor_tensor(
                        gc(i, 3), gc(i, 1), gc(i, 2), op=ALU.add),
                        waits=[("v", ctx.val("v"))], inc="v")
                for i in range(W):
                    ctx.op("vector", lambda e, i=i: e.reciprocal(
                        gc(i, 4), gc(i, 3)), waits=[("v", ctx.val("v"))], inc="v")
                for i in range(W):
                    ctx.op("vector", lambda e, i=i: e.tensor_tensor(
                        gs(i, 5), gs(i, 1), gs(i, 4), op=ALU.add),
                        waits=([("dma_out", d_scores_prev)] if i == 0 else []) +
                              [("v", ctx.val("v"))], inc="v")
                for i in range(W):
                    ctx.op("vector", lambda e, i=i: e.tensor_tensor(
                        gs(i, 5), gs(i, 5), gs(i, 0), op=ALU.mult),
                        waits=[("v", ctx.val("v"))], inc="v")
                for i in range(W):
                    ctx.op("vector", lambda e, i=i: e.tensor_scalar(
                        gs(i, 5), gs(i, 5), gc(i, 4), None, op0=ALU.mult),
                        waits=[("v", ctx.val("v"))], inc="v")
                vfin = ctx.val("v")
                for i in range(W):
                    t = ts0 + i
                    ctx.dma("sync", scores_o[b, :, t, :], gs(i, 5),
                            waits=[("v", vfin)], inc="dma_out")

        ctx.flush(block)



DEBUG_L0ATT = False


def emit_l0att(nc, ctx, block, io):
    x0 = io["x0"]
    xnT = io["xnT"]
    wq, wk, wv, wo = io["l0wq"], io["l0wk"], io["l0wv"], io["l0wo"]
    biases = io["biases0"]
    idn = io["idn"]
    ones1 = io["ones1"]
    part1 = io["part1"]
    SCALE = 0.25

    with contextlib.ExitStack() as es:
        sb = lambda *a: es.enter_context(nc.sbuf_tensor(*a))
        ps = lambda *a: es.enter_context(nc.psum_tensor(*a))
        idb = sb("idb", [128, 128], BF16)
        wT = sb("wT", [128, DC * D], BF16)
        woT = wT
        wstrip = sb("wstrip", [128, D], BF16)
        onesr = sb("onesr", [1, 128], F32)
        bqk_cols = sb("bqk_cols", [128, 2 * DC], F32)
        bvb = sb("bvb", [128, D], F32)
        bob = sb("bob", [128, D], F32)
        xnT_sb = sb("xnT_sb", [128, DC * T], BF16)
        QT = sb("QT", [128, DC * T], BF16)
        KT = sb("KT", [128, DC * T], BF16)
        vaug = sb("vaug", [128, NT * NH * 65], BF16)
        p1t = sb("p1t", [128, 2 * NT * 512], BF16)
        ao = sb("ao", [128, 4 * D], BF16)
        aoT = sb("aoT", [128, DC * 128], BF16)
        xres = sb("xres", [128, D], F32)
        po = sb("po", [128, D], F32)
        brow = po[0:1, :]
        rz = sb("rz", [128, 4], F32)
        pT0 = ps("pT0", [128, 128], BF16)
        pT1 = ps("pT1", [128, 128], BF16)
        pqk4 = ps("pqk4", [128, 2048], F32)
        pav0 = ps("pav0", [128, 65], F32)
        pav1 = ps("pav1", [128, 65], F32)
        pT = [pT0, pT1]
        pbc = pqk4[:, 0:512]
        pav = [pav0, pav1]

        ctx.dma("gpsimd", idb[:], idn[:], inc="dmag")
        ctx.dma("sync", onesr[:], ones1[:], inc="dma")
        const_dma = ctx.val("dma")
        ctx.wait("tensor", "dma", const_dma)
        ctx.wait("tensor", "dmag", ctx.val("dmag"))
        ctx.wait("vector", "dma", const_dma)
        for i, tgt in enumerate(("bq", "bk", "bv", "bo")):
            ctx.dma("sync", brow[:], biases[:, i * D:(i + 1) * D],
                    waits=[("t_misc", ctx.val("t_misc"))], inc="dma")
            ctx.wait("tensor", "dma", ctx.val("dma"))
            if i < 2:
                emit_col_from_row(ctx, brow[:],
                                  bqk_cols[:, i * DC:(i + 1) * DC],
                                  onesr[:, 0:1], pbc[:, 0:1], D)
            else:
                emit_bcast_row(ctx, brow[:], bvb if i == 2 else bob,
                               onesr[:], pbc, D)

        for b in range(BPC):
            ctx.dma("sync", xnT_sb[:].rearrange("p (c q) -> p c q", q=T),
                    xnT[b].rearrange("c p q -> p c q"),
                    waits=[("t", ctx.val("t"))], inc="dma")
            xn_ld = ctx.val("dma")

            ctx.op("gpsimd", lambda e: e.memset(
                vaug[:].rearrange("p (k h d) -> p k h d", h=NH, d=65)[:, :, :, 64:65],
                1.0), waits=[("t", ctx.val("t"))], inc="g")
            ones_set = ctx.val("g")

            for wi, (w_hbm, is_v) in enumerate([(wq, False), (wk, False), (wv, True)]):
                ctx.wait("gpsimd", "t", ctx.val("t"))
                emit_weight_transpose(ctx, w_hbm, wT, wstrip, idb, pT, D, D)
                wt_done = ctx.val("v_wt")
                if not is_v:
                    dst = QT if wi == 0 else KT
                    bcol = bqk_cols[:, wi * DC:(wi + 1) * DC]
                    for m in range(DC):
                        for g in range(4):
                            for c in range(DC):
                                ctx.op("tensor", lambda e, m=m, g=g, c=c, dst=dst: e.matmul(
                                    pqk4[:, 512 * (g % 4):512 * (g % 4 + 1)],
                                    wT[:, c * D + 128 * m: c * D + 128 * (m + 1)],
                                    xnT_sb[:, c * T + 512 * g: c * T + 512 * (g + 1)],
                                    start=(c == 0), stop=(c == DC - 1)),
                                    waits=([("v_wt", wt_done), ("dma", xn_ld),
                                            ("s", ctx.val("s")), ("v", ctx.val("v"))]
                                           if c == 0 else []),
                                    inc="t")
                            ctx.op("scalar", lambda e, m=m, g=g, dst=dst, bcol=bcol: e.activation(
                                dst[:, m * T + 512 * g: m * T + 512 * (g + 1)],
                                pqk4[:, 512 * (g % 4):512 * (g % 4 + 1)],
                                AF.Identity, bias=bcol[:, m:m + 1],
                                scale=1.0),
                                waits=[("t", ctx.val("t")), ("v_misc", ctx.val("v_misc"))],
                                inc="s")
                else:
                    for t in range(NT):
                        for g2 in range(2):
                            for c in range(DC):
                                ctx.op("tensor", lambda e, t=t, g2=g2, c=c: e.matmul(
                                    pqk4[:, 512 * ((2 * t + g2) % 4):
                                        512 * ((2 * t + g2) % 4 + 1)],
                                    xnT_sb[:, c * T + 128 * t: c * T + 128 * (t + 1)],
                                    wT[:, c * D + 512 * g2: c * D + 512 * (g2 + 1)],
                                    start=(c == 0), stop=(c == DC - 1)),
                                    waits=([("v_wt", wt_done), ("dma", xn_ld),
                                            ("s", ctx.val("s")),
                                            ("v", ctx.val("v"))] if c == 0 else []),
                                    inc="t")
                            dst3 = vaug[:, 1040 * t + 520 * g2: 1040 * t + 520 * (g2 + 1)] \
                                .rearrange("p (h d) -> p h d", d=65)[:, :, 0:64]
                            ctx.op("vector", lambda e, t=t, g2=g2, dst3=dst3: e.scalar_tensor_tensor(
                                dst3,
                                pqk4[:, 512 * ((2 * t + g2) % 4):
                                    512 * ((2 * t + g2) % 4 + 1)], 1.0,
                                bvb[:, 512 * g2:512 * (g2 + 1)]
                                .rearrange("p (h d) -> p h d", d=64),
                                op0=ALU.mult, op1=ALU.add),
                                waits=[("t", ctx.val("t")), ("v_misc", ctx.val("v_misc"))],
                                inc="v")

            if DEBUG_L0ATT and b == 0:
                ctx.dma("sync", dQT[:], QT[:], waits=[("s", ctx.val("s"))], inc="dma_out")
                ctx.dma("sync", dKT[:], KT[:], waits=[("s", ctx.val("s"))], inc="dma_out")
                ctx.dma("sync", dV[:], vaug[:], waits=[("v", ctx.val("v")), ("g", ctx.val("g"))], inc="dma_out")

            ctx.wait("gpsimd", "t", ctx.val("t"))
            emit_weight_transpose(ctx, wo, woT, wstrip, idb, pT, D, D)
            wo_done = ctx.val("v_wt")

            for g in range(4):
                g_v = ctx.val("v")
                exp_ends = {}
                cmb_ends = {-1: ctx.val("v")}
                exp_prev = [ctx.val("s")]

                def sched_st(h, g=g, g_v=g_v, exp_ends=exp_ends, exp_prev=exp_prev):
                    p0 = 64 * (h % 2)
                    mo = (h // 2) * T
                    base_s = ctx.val("s")
                    p1 = p1t[:, (h % 2) * NT * 512:(h % 2 + 1) * NT * 512]
                    for c in range(NT):
                        if c == 0:
                            w = [("s", exp_prev[0]), ("v", g_v)]
                        elif c % 4 == 0:
                            w = [("s", base_s + c // 4)]
                        else:
                            w = []
                        ctx.op("tensor", lambda e, h=h, c=c, p0=p0, mo=mo: e.matmul(
                            pqk4[:, 512 * (c % 4):512 * (c % 4 + 1)],
                            KT[p0:p0 + 64, mo + 128 * c: mo + 128 * (c + 1)],
                            QT[p0:p0 + 64, mo + 512 * g: mo + 512 * (g + 1)],
                            start=True, stop=True),
                            waits=w, inc="t")
                        if c % 4 == 3:
                            ctx.op("scalar", lambda e, c=c, p1=p1: e.activation(
                                p1[:, 512 * (c - 3):512 * (c + 1)], pqk4[:],
                                AF.Exp, scale=SCALE),
                                waits=[("t", ctx.val("t"))], inc="s")
                    exp_ends[h] = ctx.val("s")
                    exp_prev[0] = exp_ends[h]

                def sched_av(h, exp_ends=exp_ends, cmb_ends=cmb_ends):
                    p1 = p1t[:, (h % 2) * NT * 512:(h % 2 + 1) * NT * 512]
                    bank_free = [cmb_ends[h - 1], cmb_ends[h - 1]]
                    for t4 in range(4):
                        for c in range(NT):
                            ctx.op("tensor", lambda e, t4=t4, c=c, h=h, p1=p1: e.matmul(
                                pav[t4 % 2][:],
                                p1[:, 512 * c + 128 * t4: 512 * c + 128 * (t4 + 1)],
                                vaug[:, 1040 * c + 65 * h: 1040 * c + 65 * (h + 1)],
                                start=(c == 0), stop=(c == NT - 1)),
                                waits=([("s", exp_ends[h]), ("g", ones_set),
                                        ("v", bank_free[t4 % 2])]
                                       if c == 0 else []),
                                inc="t")
                        ctx.op("vector", lambda e, t4=t4: e.reciprocal(
                            rz[:, t4:t4 + 1], pav[t4 % 2][:, 64:65]),
                            waits=[("t", ctx.val("t"))], inc="v")
                        ctx.op("vector", lambda e, t4=t4, h=h: e.tensor_scalar(
                            ao[:, 1024 * t4 + 64 * h: 1024 * t4 + 64 * (h + 1)],
                            pav[t4 % 2][:, 0:64], rz[:, t4:t4 + 1], None,
                            op0=ALU.mult),
                            waits=[("v", ctx.val("v"))], inc="v")
                        bank_free[t4 % 2] = ctx.val("v")
                    cmb_ends[h] = ctx.val("v")

                for h in range(NH):
                    sched_st(h)
                    if h > 0:
                        sched_av(h - 1)
                sched_av(NH - 1)
                ao_done = ctx.val("v")
                for t4 in range(4):
                    t = 4 * g + t4
                    ctx.dma("sync", xres[:], x0[b, 128 * t:128 * (t + 1), :],
                            waits=[("v", ctx.val("v"))], inc="dma")
                    x_ld = ctx.val("dma")
                    for c in range(DC):
                        ctx.op("tensor", lambda e, t4=t4, c=c: e.transpose(
                            pT[c % 2][:],
                            ao[:, 1024 * t4 + 128 * c: 1024 * t4 + 128 * (c + 1)],
                            idb[:]),
                            waits=[("v", ctx.val("v")), ("v_wt", ctx.val("v_wt"))],
                            inc="t")
                        ctx.op("vector", lambda e, c=c: e.tensor_copy(
                            aoT[:, 128 * c:128 * (c + 1)], pT[c % 2][:]),
                            waits=[("t", ctx.val("t"))], inc="v")
                    aoT_done = ctx.val("v")
                    for g2 in range(2):
                        for c in range(DC):
                            ctx.op("tensor", lambda e, g2=g2, c=c: e.matmul(
                                pqk4[:, 512 * g2:512 * (g2 + 1)],
                                aoT[:, 128 * c:128 * (c + 1)],
                                woT[:, c * D + 512 * g2: c * D + 512 * (g2 + 1)],
                                start=(c == 0), stop=(c == DC - 1)),
                                waits=([("v_wt", wo_done), ("v", aoT_done),
                                        ("s", ctx.val("s"))] if c == 0 else []),
                                inc="t")
                        ctx.op("vector", lambda e, g2=g2: e.tensor_tensor(
                            po[:, 512 * g2:512 * (g2 + 1)],
                            pqk4[:, 512 * g2:512 * (g2 + 1)],
                            bob[:, 512 * g2:512 * (g2 + 1)], op=ALU.add),
                            waits=[("t", ctx.val("t")), ("dma", x_ld),
                                   ("dma_out", ctx.val("dma_out"))],
                            inc="v")
                        ctx.op("vector", lambda e, g2=g2: e.tensor_tensor(
                            po[:, 512 * g2:512 * (g2 + 1)],
                            po[:, 512 * g2:512 * (g2 + 1)],
                            xres[:, 512 * g2:512 * (g2 + 1)], op=ALU.add),
                            inc="v")
                    ctx.dma("sync", part1[b, 128 * t:128 * (t + 1), :], po[:],
                            waits=[("v", ctx.val("v"))], inc="dma_out")

        ctx.flush(block)


def emit_l0moe(nc, ctx, block, io):
    xnT = io["xnT"]
    scores = io["scores"]
    part1 = io["part1"]
    W1, b1, W2, b2 = io["W1"], io["b1"], io["W2"], io["b2"]
    idn = io["idn"]
    ones1 = io["ones1"]
    x1 = io["x1"]

    HC = HMoE // 128

    with contextlib.ExitStack() as es:
        sb = lambda *a: es.enter_context(nc.sbuf_tensor(*a))
        ps = lambda *a: es.enter_context(nc.psum_tensor(*a))
        idb = sb("idb", [128, 128], BF16)
        onesr = sb("onesr", [1, 128], F32)
        acc = sb("acc", [128, NT * D], F32)
        xnT_sb = sb("xnT_sb", [128, DC * T], BF16)
        W1sb = sb("W1sb", [128, DC * HMoE], BF16)
        W2sb = sb("W2sb", [128, HC * D], BF16)
        hT = sb("hT", [128, HC * 512], BF16)
        scs = sb("scs", [128, NT * NE], F32)
        scb = sb("scb", [128, NE], BF16)
        scT = sb("scT", [8, NT * 128], BF16)
        b2sb = sb("b2sb", [8, D], BF16)
        b1row = sb("b1row", [1, HMoE], F32)
        b1c = sb("b1c", [128, NE * HC], F32)
        pT0 = ps("pT0", [128, 128], BF16)
        pTs = ps("pTs", [8, 128], BF16)
        ph0 = ps("ph0", [128, 512], F32)
        ph1 = ps("ph1", [128, 512], F32)
        py4 = ps("py4", [128, 2048], F32)
        pb1 = ph0[:, 0:1]
        ph = [ph0, ph1]
        py = [py4[:, 0:512], py4[:, 512:1024], py4[:, 1024:1536], py4[:, 1536:2048]]

        ctx.dma("gpsimd", idb[:], idn[:], inc="dmag")
        ctx.dma("sync", onesr[:], ones1[:], inc="dma")
        ctx.dma("gpsimd", b2sb[:], b2[:], inc="dmag")
        const_dma = ctx.val("dma")
        const_dmag = ctx.val("dmag")
        ctx.wait("tensor", "dma", const_dma)
        ctx.wait("tensor", "dmag", const_dmag)
        for e_i in range(NE):
            ctx.dma("sync", b1row[:], b1[e_i:e_i + 1, :],
                    waits=[("t_misc", ctx.val("t_misc"))], inc="dma")
            ctx.wait("tensor", "dma", ctx.val("dma"))
            emit_col_from_row(ctx, b1row[:], b1c[:, e_i * HC:(e_i + 1) * HC],
                              onesr[:, 0:1], pb1[:, 0:1], HMoE)
        b1_done = ctx.val("v_misc")

        for b in range(BPC):
            ctx.dma("sync", xnT_sb[:].rearrange("p (c q) -> p c q", q=T),
                    xnT[b].rearrange("c p q -> p c q"),
                    waits=[("t", ctx.val("t"))], inc="dma")
            ctx.dma("sync", scs[:].rearrange("p (t e) -> p t e", e=NE),
                    scores[b], waits=[("v", ctx.val("v"))], inc="dma")
            in_ld = ctx.val("dma")
            for t in range(NT):
                ctx.dma("sync", acc[:, D * t:D * (t + 1)],
                        part1[b, 128 * t:128 * (t + 1), :],
                        waits=[("dma_out", ctx.val("dma_out"))], inc="dma")
            acc_ld = ctx.val("dma")

            for t in range(NT):
                ctx.op("vector", lambda e, t=t: e.tensor_copy(
                    scb[:], scs[:, NE * t:NE * (t + 1)]),
                    waits=[("dma", in_ld), ("t", ctx.val("t"))], inc="v")
                ctx.op("tensor", lambda e, t=t: e.transpose(
                    pTs[:], scb[:], idb[:]),
                    waits=[("v", ctx.val("v")), ("dma", const_dma)], inc="t")
                ctx.op("vector", lambda e, t=t: e.tensor_copy(
                    scT[:, 128 * t:128 * (t + 1)], pTs[:]),
                    waits=[("t", ctx.val("t"))], inc="v")
            scT_done = ctx.val("v")
            for t in range(NT):
                for g2 in range(2):
                    ctx.op("tensor", lambda e, t=t, g2=g2: e.matmul(
                        py[(2 * t + g2) % 4], scT[:, 128 * t:128 * (t + 1)],
                        b2sb[:, 512 * g2:512 * (g2 + 1)],
                        start=True, stop=True),
                        waits=[("v", scT_done), ("dmag", const_dmag),
                               ("v", ctx.val("v"))], inc="t")
                    ctx.op("vector", lambda e, t=t, g2=g2: e.tensor_tensor(
                        acc[:, D * t + 512 * g2: D * t + 512 * (g2 + 1)],
                        acc[:, D * t + 512 * g2: D * t + 512 * (g2 + 1)],
                        py[(2 * t + g2) % 4], op=ALU.add),
                        waits=[("t", ctx.val("t")), ("dma", acc_ld)], inc="v")

            def load_w1(e_i, waits):
                ctx.dma("gpsimd", W1sb[:].rearrange("p (c h) -> p c h", h=HMoE),
                        W1[e_i].rearrange("(c p) h -> p c h", p=128),
                        waits=waits, inc="dma_w")
                return ctx.val("dma_w")

            def load_w2(e_i, waits):
                ctx.dma("gpsimd", W2sb[:].rearrange("p (j d) -> p j d", d=D),
                        W2[e_i].rearrange("(j p) d -> p j d", p=128),
                        waits=waits, inc="dma_w")
                return ctx.val("dma_w")

            w1_ld = load_w1(0, [("t", ctx.val("t"))])
            w2_ld = load_w2(0, [("t", ctx.val("t"))])
            for e_i in range(NE):
                w1_cur, w2_cur = w1_ld, w2_ld
                for g in range(4):
                    for j in range(HC):
                        for c in range(DC):
                            ctx.op("tensor", lambda e, j=j, c=c, g=g: e.matmul(
                                ph[j % 2][:],
                                W1sb[:, c * HMoE + 128 * j: c * HMoE + 128 * (j + 1)],
                                xnT_sb[:, c * T + 512 * g: c * T + 512 * (g + 1)],
                                start=(c == 0), stop=(c == DC - 1)),
                                waits=([("dma_w", w1_cur), ("dma", in_ld),
                                        ("s", ctx.val("s") - (1 if j >= 2 else 0))]
                                       if c == 0 else []),
                                inc="t")
                        ctx.op("scalar", lambda e, j=j, e_i=e_i: e.activation(
                            hT[:, 512 * j:512 * (j + 1)], ph[j % 2][:], AF.Relu,
                            bias=b1c[:, e_i * HC + j: e_i * HC + j + 1], scale=1.0),
                            waits=[("t", ctx.val("t")), ("v_misc", b1_done)],
                            inc="s")
                    if g == 3 and e_i < NE - 1:
                        w1_ld = load_w1(e_i + 1, [("t", ctx.val("t"))])
                    h_done = ctx.val("s")
                    if g == 0 and e_i == 0:
                        bank_free = [ctx.val("v")] * 4
                    for t4 in range(4):
                        t = 4 * g + t4
                        for g2 in range(2):
                            bk = (2 * t4 + g2) % 4
                            for j in range(HC):
                                ctx.op("tensor", lambda e, t4=t4, g2=g2, j=j, bk=bk: e.matmul(
                                    py[bk],
                                    hT[:, 512 * j + 128 * t4: 512 * j + 128 * (t4 + 1)],
                                    W2sb[:, j * D + 512 * g2: j * D + 512 * (g2 + 1)],
                                    start=(j == 0), stop=(j == HC - 1)),
                                    waits=([("s", h_done), ("dma_w", w2_cur),
                                            ("v", bank_free[bk])]
                                           if j == 0 else []),
                                    inc="t")
                            ctx.op("vector", lambda e, t=t, g2=g2, e_i=e_i, bk=bk: e.scalar_tensor_tensor(
                                acc[:, D * t + 512 * g2: D * t + 512 * (g2 + 1)],
                                py[bk],
                                scs[:, NE * t + e_i: NE * t + e_i + 1],
                                acc[:, D * t + 512 * g2: D * t + 512 * (g2 + 1)],
                                op0=ALU.mult, op1=ALU.add),
                                waits=[("t", ctx.val("t")), ("dma", in_ld)],
                                inc="v")
                            bank_free[bk] = ctx.val("v")
                if e_i < NE - 1:
                    w2_ld = load_w2(e_i + 1, [("t", ctx.val("t"))])

            for t in range(NT):
                ctx.dma("sync", x1[b, 128 * t:128 * (t + 1), :],
                        acc[:, D * t:D * (t + 1)],
                        waits=[("v", ctx.val("v"))], inc="dma_out")

        ctx.flush(block)


def emit_quant(ctx, xsrc, xd_bf, am, sc, rsc, u, t0, t1, ic, fb, width=D, blk=32):
    nb = width // blk
    V = "v"

    def vop(fn, waits=()):
        ctx.op("vector", fn, waits=waits, inc=V)

    x3 = xsrc.rearrange("p (b i) -> p b i", i=blk)
    vop(lambda e: e.tensor_reduce(am, x3, axis=AX.X, op=ALU.max,
                                  apply_absolute_value=True))
    vop(lambda e: e.tensor_scalar(sc, am, 1.0 / 7.5, 1e-30, op0=ALU.mult,
                                  op1=ALU.max), waits=[("v", ctx.val("v"))])
    vop(lambda e: e.reciprocal(rsc, sc), waits=[("v", ctx.val("v"))])
    u3 = u.rearrange("p (b i) -> p b i", i=blk)
    rb = rsc.rearrange("p b -> p b").unsqueeze(2).broadcast_to([128, nb, blk])
    vop(lambda e: e.tensor_tensor(u3, x3, rb, op=ALU.mult),
        waits=[("v", ctx.val("v"))])
    u_done = ctx.val("v")
    ctx.op("scalar", lambda e: e.activation(fb, u, AF.Abs),
           waits=[("v", u_done)], inc="s")
    a_done = ctx.val("s")
    vop(lambda e: e.tensor_scalar(t1, fb, 2.0, None, op0=ALU.is_ge),
        waits=[("s", a_done)])
    vop(lambda e: e.tensor_scalar(t1, t1, -4.0, 8.0, op0=ALU.mult, op1=ALU.add))
    vop(lambda e: e.tensor_scalar(fb, fb, 4.0, None, op0=ALU.is_ge))
    vop(lambda e: e.scalar_tensor_tensor(t1, fb, -2.0, t1, op0=ALU.mult,
                                         op1=ALU.add))
    vop(lambda e: e.tensor_tensor(t0, u, t1, op=ALU.mult))
    vop(lambda e: e.tensor_scalar(t0, t0, -0.5, None, op0=ALU.add))
    vop(lambda e: e.tensor_copy(ic, t0))
    vop(lambda e: e.tensor_copy(fb, ic))
    vop(lambda e: e.tensor_tensor(u, fb, t0, op=ALU.is_lt))
    vop(lambda e: e.tensor_tensor(fb, fb, u, op=ALU.add))
    vop(lambda e: e.reciprocal(t1, t1))
    vop(lambda e: e.tensor_tensor(fb, fb, t1, op=ALU.mult))
    f3 = fb.rearrange("p (b i) -> p b i", i=blk)
    sb3 = sc.rearrange("p b -> p b").unsqueeze(2).broadcast_to([128, nb, blk])
    xd3 = xd_bf.rearrange("p (b i) -> p b i", i=blk)
    vop(lambda e: e.tensor_tensor(xd3, f3, sb3, op=ALU.mult),
        waits=[("v", ctx.val("v"))])
    return ctx.val("v")


def emit_l1ln(nc, ctx, block, io):
    x1 = io["x1"]
    idn = io["idn"]
    xn1T = io["xn1T"]
    xdT = io["xdT"]
    W = 4

    with contextlib.ExitStack() as es:
        sb = lambda *a: es.enter_context(nc.sbuf_tensor(*a))
        ps = lambda *a: es.enter_context(nc.psum_tensor(*a))
        idb = sb("idb", [128, 128], BF16)
        xt = sb("xt", [128, W * D], F32)
        xc = sb("xc", [128, W * D], F32)
        sq = sb("sq", [128, D], F32)
        xnf = sb("xnf", [128, W * D], F32)
        xnb = sb("xnb", [128, W * D], BF16)
        xdb = sb("xdb", [128, W * D], BF16)
        stat = sb("stat", [128, W * 16], F32)
        am = sb("am", [128, W * 32], F32)
        scq = sb("scq", [128, W * 32], F32)
        rscq = sb("rscq", [128, 32], F32)
        qu = sb("qu", [128, W * D], F32)
        qt0 = sb("qt0", [128, D], F32)
        qt1 = sb("qt1", [128, D], F32)
        qic = sb("qic", [128, D], mybir.dt.int32)
        qfb = sb("qfb", [128, W * D], F32)
        trT = sb("trT", [128, D], BF16)
        pT0 = ps("pT0", [128, 128], BF16)
        pT1 = ps("pT1", [128, 128], BF16)
        pT2 = ps("pT2", [128, 128], BF16)
        pT3 = ps("pT3", [128, 128], BF16)
        pT = [pT0, pT1, pT2, pT3]

        ctx.dma("gpsimd", idb[:], idn[:], inc="dmag")
        ctx.wait("tensor", "dmag", ctx.val("dmag"))

        for b in range(BPC):
            for w in range(NT // W):
                ts0 = w * W
                prev_s = ctx.val("s")
                for i in range(W):
                    t = ts0 + i
                    ctx.dma("sync", xt[:, i * D:(i + 1) * D],
                            x1[b, 128 * t:128 * (t + 1), :],
                            waits=[("s", prev_s)] if (b or w) else [])
                x_ld = ctx.val("dma")
                for i in range(W):
                    ctx.op("vector", lambda e, i=i: e.reduce_sum(
                        stat[:, 16 * i:16 * i + 1], xt[:, i * D:(i + 1) * D],
                        axis=AX.X), waits=[("dma", x_ld)], inc="v")
                for i in range(W):
                    ctx.op("vector", lambda e, i=i: e.tensor_scalar(
                        stat[:, 16 * i + 1:16 * i + 2], stat[:, 16 * i:16 * i + 1],
                        -1.0 / D, None, op0=ALU.mult),
                        waits=[("v", ctx.val("v"))], inc="v")
                vst = ctx.val("v")
                for i in range(W):
                    ctx.op("scalar", lambda e, i=i: e.activation(
                        xc[:, i * D:(i + 1) * D], xt[:, i * D:(i + 1) * D],
                        AF.Identity, bias=stat[:, 16 * i + 1:16 * i + 2], scale=1.0),
                        waits=[("v", vst)], inc="s")
                    ctx.op("scalar", lambda e, i=i: e.activation(
                        sq[:], xc[:, i * D:(i + 1) * D], AF.Square,
                        accum_out=stat[:, 16 * i + 2:16 * i + 3]), inc="s")
                sst = ctx.val("s")
                for i in range(W):
                    ctx.op("vector", lambda e, i=i: e.tensor_scalar(
                        stat[:, 16 * i + 3:16 * i + 4], stat[:, 16 * i + 2:16 * i + 3],
                        1.0 / D, 1e-5, op0=ALU.mult, op1=ALU.add),
                        waits=[("s", sst), ("v", ctx.val("v"))], inc="v")
                    ctx.op("vector", lambda e, i=i: e.reciprocal(
                        stat[:, 16 * i + 4:16 * i + 5], stat[:, 16 * i + 3:16 * i + 4]),
                        waits=[("v", ctx.val("v"))], inc="v")
                vrv = ctx.val("v")
                for i in range(W):
                    ctx.op("scalar", lambda e, i=i: e.activation(
                        stat[:, 16 * i + 5:16 * i + 6], stat[:, 16 * i + 4:16 * i + 5],
                        AF.Sqrt), waits=[("v", vrv)], inc="s")
                srs = ctx.val("s")
                for i in range(W):
                    ctx.op("vector", lambda e, i=i: e.tensor_scalar(
                        xnf[:, i * D:(i + 1) * D], xc[:, i * D:(i + 1) * D],
                        stat[:, 16 * i + 5:16 * i + 6], None, op0=ALU.mult),
                        waits=[("s", srs), ("v", ctx.val("v"))], inc="v")
                    ctx.op("vector", lambda e, i=i: e.tensor_copy(
                        xnb[:, i * D:(i + 1) * D], xnf[:, i * D:(i + 1) * D]),
                        inc="v")
                xn_done = ctx.val("v")

                for i in range(W):
                    t = ts0 + i
                    ev = {}
                    d0 = ctx.val("dma_out")
                    for c in range(DC):
                        ctx.op("tensor", lambda e, i=i, c=c: e.transpose(
                            pT[c % 4][:], xnb[:, i * D + 128 * c: i * D + 128 * (c + 1)],
                            idb[:]),
                            waits=([("v", xn_done), ("dma_out", d0)] if c < 4 else
                                   [("v", ev[c - 4])]),
                            inc="t")
                        ctx.op("vector", lambda e, c=c: e.tensor_copy(
                            trT[:, 128 * c:128 * (c + 1)], pT[c % 4][:]),
                            waits=[("t", ctx.val("t"))], inc="v")
                        ev[c] = ctx.val("v")
                    ctx.dma("sync", xn1T[b, :, :, 128 * t:128 * (t + 1)]
                            .rearrange("c p q -> p c q"),
                            trT[:].rearrange("p (c q) -> p c q", q=128),
                            waits=[("v", ctx.val("v"))], inc="dma_out")

                for i in range(W):
                    x3 = xnf[:, i * D:(i + 1) * D].rearrange("p (b i) -> p b i", i=32)
                    ctx.op("vector", lambda e, i=i, x3=x3: e.tensor_reduce(
                        am[:, 32 * i:32 * (i + 1)], x3, axis=AX.X, op=ALU.max,
                        apply_absolute_value=True),
                        waits=[("v", ctx.val("v"))], inc="v")
                    ctx.op("vector", lambda e, i=i: e.tensor_scalar(
                        scq[:, 32 * i:32 * (i + 1)], am[:, 32 * i:32 * (i + 1)],
                        1.0 / 7.5, 1e-30, op0=ALU.mult, op1=ALU.max),
                        waits=[("v", ctx.val("v"))], inc="v")
                    ctx.op("vector", lambda e, i=i: e.reciprocal(
                        rscq[:], scq[:, 32 * i:32 * (i + 1)]),
                        waits=[("v", ctx.val("v"))], inc="v")
                    ctx.op("vector", lambda e, i=i, x3=x3: e.tensor_tensor(
                        qu[:, i * D:(i + 1) * D].rearrange("p (b i) -> p b i", i=32),
                        x3, rscq[:].unsqueeze(2).broadcast_to([128, 32, 32]),
                        op=ALU.mult), waits=[("v", ctx.val("v"))], inc="v")
                u_done = ctx.val("v")
                for i in range(W):
                    ctx.op("scalar", lambda e, i=i: e.activation(
                        qfb[:, i * D:(i + 1) * D], qu[:, i * D:(i + 1) * D], AF.Abs),
                        waits=[("v", u_done)], inc="s")
                a_done = ctx.val("s")
                for i in range(W):
                    fbs = qfb[:, i * D:(i + 1) * D]
                    us = qu[:, i * D:(i + 1) * D]
                    ctx.op("vector", lambda e, fbs=fbs: e.tensor_scalar(
                        qt1[:], fbs, 2.0, None, op0=ALU.is_ge),
                        waits=[("s", a_done), ("v", ctx.val("v"))], inc="v")
                    ctx.op("vector", lambda e: e.tensor_scalar(
                        qt1[:], qt1[:], -4.0, 8.0, op0=ALU.mult, op1=ALU.add), inc="v")
                    ctx.op("vector", lambda e, fbs=fbs: e.tensor_scalar(
                        fbs, fbs, 4.0, None, op0=ALU.is_ge), inc="v")
                    ctx.op("vector", lambda e, fbs=fbs: e.scalar_tensor_tensor(
                        qt1[:], fbs, -2.0, qt1[:], op0=ALU.mult, op1=ALU.add), inc="v")
                    ctx.op("vector", lambda e, us=us: e.tensor_tensor(
                        qt0[:], us, qt1[:], op=ALU.mult), inc="v")
                    ctx.op("vector", lambda e: e.tensor_scalar(
                        qt0[:], qt0[:], -0.5, None, op0=ALU.add), inc="v")
                    ctx.op("vector", lambda e: e.tensor_copy(qic[:], qt0[:]), inc="v")
                    ctx.op("vector", lambda e, fbs=fbs: e.tensor_copy(fbs, qic[:]), inc="v")
                    ctx.op("vector", lambda e, us=us, fbs=fbs: e.tensor_tensor(
                        us, fbs, qt0[:], op=ALU.is_lt), inc="v")
                    ctx.op("vector", lambda e, us=us, fbs=fbs: e.tensor_tensor(
                        fbs, fbs, us, op=ALU.add), inc="v")
                    ctx.op("vector", lambda e: e.reciprocal(qt1[:], qt1[:]), inc="v")
                    ctx.op("vector", lambda e, fbs=fbs: e.tensor_tensor(
                        fbs, fbs, qt1[:], op=ALU.mult), inc="v")
                    ctx.op("vector", lambda e, i=i, fbs=fbs: e.tensor_tensor(
                        xdb[:, i * D:(i + 1) * D].rearrange("p (b i) -> p b i", i=32),
                        fbs.rearrange("p (b i) -> p b i", i=32),
                        scq[:, 32 * i:32 * (i + 1)].unsqueeze(2)
                        .broadcast_to([128, 32, 32]),
                        op=ALU.mult), waits=[("v", ctx.val("v"))], inc="v")

                for i in range(W):
                    t = ts0 + i
                    ev = {}
                    d0 = ctx.val("dma_out")
                    w0 = ctx.val("v")
                    for c in range(DC):
                        ctx.op("tensor", lambda e, i=i, c=c: e.transpose(
                            pT[c % 4][:], xdb[:, i * D + 128 * c: i * D + 128 * (c + 1)],
                            idb[:]),
                            waits=([("v", w0), ("dma_out", d0)] if c < 4 else
                                   [("v", ev[c - 4])]),
                            inc="t")
                        ctx.op("vector", lambda e, c=c: e.tensor_copy(
                            trT[:, 128 * c:128 * (c + 1)], pT[c % 4][:]),
                            waits=[("t", ctx.val("t"))], inc="v")
                        ev[c] = ctx.val("v")
                    ctx.dma("sync", xdT[b, :, :, 128 * t:128 * (t + 1)]
                            .rearrange("c p q -> p c q"),
                            trT[:].rearrange("p (c q) -> p c q", q=128),
                            waits=[("v", ctx.val("v"))], inc="dma_out")

        ctx.flush(block)



def emit_l1att(nc, ctx, block, io):
    nb = L1DBG.get("nb", BPC)
    qs = L1DBG.get("qs", 99)
    nt = L1DBG.get("nt", NT)
    nh = L1DBG.get("nh", NH)
    x1 = io["x1"]
    xn1T = io["xn1T"]
    wq, wk, wv, wo = io["l1wq"], io["l1wk"], io["l1wv"], io["l1wo"]
    biases = io["biases1"]
    tkbig = io["tkbig"]
    tvbig = io["tvbig"]
    tv0r = io["tv0r"]
    tv8r = io["tv8r"]
    emsk = io["emsk"]
    zpad = io["zpad"]
    idn = io["idn"]
    ones1 = io["ones1"]
    part2 = io["part2"]
    ktok_d = io["ktok_d"]
    vtok_d = io["vtok_d"]

    SC8 = 0.125

    with contextlib.ExitStack() as es:
        sb = lambda *a: es.enter_context(nc.sbuf_tensor(*a))
        ps = lambda *a: es.enter_context(nc.psum_tensor(*a))
        idb = sb("idb", [128, 128], BF16)
        onesr = sb("onesr", [1, 128], F32)
        bqk_cols = sb("bqk_cols", [128, 2 * DC], F32)
        bvob = sb("bvob", [128, 2 * D], F32)
        tkTs = sb("tkTs", [128, DC * NH * 9], BF16)
        tvbigs = sb("tvbigs", [128, D], BF16)
        tv0b = sb("tv0b", [128, D], BF16)
        tv8b = sb("tv8b", [128, D], BF16)
        emsks = sb("emsks", [128, 16], BF16)
        w_sb = sb("w_sb", [128, DC * D], BF16)
        xnT_sb = sb("xnT_sb", [128, DC * T], BF16)
        QT = sb("QT", [128, DC * T], BF16)
        KT = sb("KT", [128, DC * T], BF16)
        vaug = sb("vaug", [128, NT * NH * 65], BF16)
        qtok2 = sb("qtok2", [128, 2 * D], BF16)
        win = xnT_sb[:, 0:7 * D]
        p1t = [xnT_sb[:, 7 * D:7 * D + NT * 256],
               xnT_sb[:, 7 * D + NT * 256:7 * D + 2 * NT * 256]]
        prod = sb("prod", [128, D], BF16)
        accb2 = sb("accb2", [128, 2 * D], BF16)
        bd2 = sb("bd2", [128, 256], F32)
        warg = sb("warg", [128, 128], F32)
        w8_2 = sb("w8_2", [128, 256], BF16)
        p1d8 = sb("p1d8", [128, 128], BF16)
        w8T = sb("w8T", [128, 128], BF16)
        qdT2 = sb("qdT2", [128, 288], F32)
        Acol2 = sb("Acol2", [128, 32], F32)
        Bcol2 = sb("Bcol2", [128, 32], F32)
        Ab = sb("Ab", [128, 16], BF16)
        tg8_2 = sb("tg8_2", [128, 256], BF16)
        Ssum2 = sb("Ssum2", [128, 128], F32)
        Fsb2 = sb("Fsb2", [128, 2 * NH * 65], F32)
        Usb2 = sb("Usb2", [128, 2 * NH * 65], F32)
        msk1 = sb("msk1", [128, 128], BF16)
        msk2 = sb("msk2", [128, 128], BF16)
        numb = sb("numb", [128, D], BF16)
        tb = sb("tb", [128, D], BF16)
        tvb_sb = sb("tvb_sb", [128, D], BF16)
        den = sb("den", [128, 16], F32)
        rden = sb("rden", [128, 16], F32)
        ao = sb("ao", [128, D], BF16)
        aoT = sb("aoT", [128, D], BF16)
        po = sb("po", [128, D], F32)
        t0f = po
        brow = po[0:1, :]
        xres = Fsb2
        pT0 = ps("pT0", [128, 128], BF16)
        pT1 = ps("pT1", [128, 128], BF16)
        pst = ps("pst", [128, 2048], F32)
        pFU = ps("pFU", [128, 512], F32)
        pF = pFU[:, 0:130]
        pU = pFU[:, 256:386]
        pOP = ps("pOP", [128, 512], F32)
        pqd = pst[:, 1024:1536]
        pT = [pT0, pT1]

        ctx.dma("gpsimd", idb[:], idn[:], inc="dmag")
        ctx.dma("sync", onesr[:], ones1[:], inc="dma")
        ctx.dma("gpsimd", tkTs[:].rearrange("p (c e) -> p c e", e=NH * 9),
                tkbig.rearrange("(c p) e -> p c e", p=128), inc="dmag")
        ctx.dma("gpsimd", tvbigs[:], tvbig[:], inc="dmag")
        ctx.dma("gpsimd", emsks[:].rearrange("p (i o) -> p i o", o=8),
                emsk.rearrange("i p o -> p i o"), inc="dmag")
        for dst in (ktok_d, vtok_d):
            ctx.dma("gpsimd", dst[0:3, :], zpad[0:3, :], inc="dmag")
            ctx.dma("gpsimd", dst[T + 3:T + 6, :], zpad[0:3, :], inc="dmag")
        const_ld = ctx.val("dma")
        ctx.wait("tensor", "dma", const_ld)
        ctx.wait("tensor", "dmag", ctx.val("dmag"))
        ctx.wait("vector", "dma", const_ld)
        ctx.wait("vector", "dmag", ctx.val("dmag"))
        for row, dstb in ((tv0r, tv0b), (tv8r, tv8b)):
            ctx.dma("sync", brow, row[:],
                    waits=[("t_misc", ctx.val("t_misc"))], inc="dma")
            ctx.wait("tensor", "dma", ctx.val("dma"))
            for gg in range(2):
                ctx.op("tensor", lambda e, gg=gg: e.matmul(
                    pqd[:, 0:512], onesr[:], brow[:, 512 * gg:512 * (gg + 1)],
                    start=True, stop=True),
                    waits=[("v_misc", ctx.val("v_misc"))], inc="t_misc")
                ctx.op("vector", lambda e, gg=gg, dstb=dstb: e.tensor_copy(
                    dstb[:, 512 * gg:512 * (gg + 1)], pqd[:]),
                    waits=[("t_misc", ctx.val("t_misc"))], inc="v_misc")
        for i in range(4):
            ctx.dma("sync", brow, biases[:, i * D:(i + 1) * D],
                    waits=[("t_misc", ctx.val("t_misc"))], inc="dma")
            ctx.wait("tensor", "dma", ctx.val("dma"))
            if i < 2:
                emit_col_from_row(ctx, brow, bqk_cols[:, i * DC:(i + 1) * DC],
                                  onesr[:, 0:1], pqd[:, 0:1], D)
            else:
                emit_bcast_row(ctx, brow, bvob[:, (i - 2) * D:(i - 1) * D],
                               onesr[:], pqd, D)

        for b in range(nb):
            ctx.dma("sync", xnT_sb[:].rearrange("p (c q) -> p c q", q=T),
                    xn1T[b].rearrange("c p q -> p c q"),
                    waits=[("t", ctx.val("t"))], inc="dma")
            xn_ld = ctx.val("dma")
            ctx.op("gpsimd", lambda e: e.memset(
                vaug[:].rearrange("p (k h d) -> p k h d", h=NH, d=65)[:, :, :, 64:65],
                1.0), waits=[("t", ctx.val("t"))], inc="g")
            ones_set = ctx.val("g")

            for wi, w_hbm in enumerate([wq, wk, wv, wo]):
                ctx.dma("gpsimd", w_sb[:].rearrange("p (c e) -> p c e", e=D),
                        w_hbm.rearrange("(c p) e -> p c e", p=128),
                        waits=[("t", ctx.val("t"))], inc="dma_w")
                w_ld = ctx.val("dma_w")
                if wi < 2:
                    dst = QT if wi == 0 else KT
                    bcol = bqk_cols[:, wi * DC:(wi + 1) * DC]
                    for m in range(DC):
                        for g in range(4):
                            for c in range(DC):
                                ctx.op("tensor", lambda e, m=m, g=g, c=c: e.matmul(
                                    pst[:, 512 * (g % 4):512 * (g % 4 + 1)],
                                    w_sb[:, c * D + 128 * m: c * D + 128 * (m + 1)],
                                    xnT_sb[:, c * T + 512 * g: c * T + 512 * (g + 1)],
                                    start=(c == 0), stop=(c == DC - 1)),
                                    waits=([("dma_w", w_ld), ("dma", xn_ld),
                                            ("s", ctx.val("s")), ("v", ctx.val("v"))]
                                           if c == 0 else []),
                                    inc="t")
                            ctx.op("scalar", lambda e, m=m, g=g, dst=dst, bcol=bcol: e.activation(
                                dst[:, m * T + 512 * g: m * T + 512 * (g + 1)],
                                pst[:, 512 * (g % 4):512 * (g % 4 + 1)],
                                AF.Identity, bias=bcol[:, m:m + 1], scale=1.0),
                                waits=[("t", ctx.val("t")),
                                       ("v_misc", ctx.val("v_misc"))], inc="s")
                elif wi == 2:
                    for t in range(NT):
                        for g2 in range(2):
                            for c in range(DC):
                                ctx.op("tensor", lambda e, t=t, g2=g2, c=c: e.matmul(
                                    pst[:, 512 * ((2 * t + g2) % 4):
                                        512 * ((2 * t + g2) % 4 + 1)],
                                    xnT_sb[:, c * T + 128 * t: c * T + 128 * (t + 1)],
                                    w_sb[:, c * D + 512 * g2: c * D + 512 * (g2 + 1)],
                                    start=(c == 0), stop=(c == DC - 1)),
                                    waits=([("dma_w", w_ld), ("dma", xn_ld),
                                            ("s", ctx.val("s")),
                                            ("v", ctx.val("v"))] if c == 0 else []),
                                    inc="t")
                            dst3 = vaug[:, 1040 * t + 520 * g2: 1040 * t + 520 * (g2 + 1)] \
                                .rearrange("p (h d) -> p h d", d=65)[:, :, 0:64]
                            ctx.op("vector", lambda e, t=t, g2=g2, dst3=dst3: e.scalar_tensor_tensor(
                                dst3,
                                pst[:, 512 * ((2 * t + g2) % 4):
                                    512 * ((2 * t + g2) % 4 + 1)], 1.0,
                                bvob[:, 512 * g2:512 * (g2 + 1)]
                                .rearrange("p (h d) -> p h d", d=64),
                                op0=ALU.mult, op1=ALU.add),
                                waits=[("t", ctx.val("t")),
                                       ("v_misc", ctx.val("v_misc"))], inc="v")
                    v_done = ctx.val("v")
                    for cc in range(NT):
                        ctx.dma("sync",
                                vtok_d[3 + 128 * cc:3 + 128 * (cc + 1), :]
                                .rearrange("p (h d) -> p h d", d=64),
                                vaug[:, 1040 * cc:1040 * (cc + 1)]
                                .rearrange("p (h d) -> p h d", d=65)[:, :, 0:64],
                                waits=[("v", v_done),
                                       ("dma_win", ctx.val("dma_win"))],
                                inc="dma_out")
                else:
                    wo_ld = ctx.val("dma_w")

            for t in range(NT):
                ev = {}
                w0 = ctx.val("v")
                s0 = ctx.val("s")
                for c in range(DC):
                    ctx.op("tensor", lambda e, t=t, c=c: e.transpose(
                        pT[c % 2][:], KT[:, c * T + 128 * t: c * T + 128 * (t + 1)],
                        idb[:]),
                        waits=([("v", w0), ("s", s0)] if c < 2 else
                               [("v", ev[c - 2])]),
                        inc="t")
                    ctx.op("vector", lambda e, c=c: e.tensor_copy(
                        qtok2[:, 128 * c:128 * (c + 1)], pT[c % 2][:]),
                        waits=[("t", ctx.val("t"))], inc="v")
                    ev[c] = ctx.val("v")
                ctx.dma("sync", ktok_d[3 + 128 * t:3 + 128 * (t + 1), :],
                        qtok2[:, 0:D],
                        waits=[("v", ctx.val("v")), ("dma_win", ctx.val("dma_win"))],
                        inc="dma_out")
                ctx.wait("tensor", "dma_out", ctx.val("dma_out"))
            ktok_done = ctx.val("dma_out")

            exp_prev = [ctx.val("s")]
            tvb_free = [ctx.val("v")]
            pT_free = [ctx.val("v")]
            qtok_free = [ctx.val("v")]
            for gq in range(nt // 2):
                tp0 = 2 * gq
                ev_s = {}
                pi = 0
                for t4 in range(2):
                    t = tp0 + t4
                    w0 = max(pT_free[0], qtok_free[0])
                    g0 = ctx.val("g")
                    for c in range(DC):
                        ctx.op("tensor", lambda e, t=t, c=c: e.transpose(
                            pT[c % 2][:], QT[:, c * T + 128 * t: c * T + 128 * (t + 1)],
                            idb[:]),
                            waits=([("v", w0), ("g", g0)] if pi < 2 else
                                   [("s", ev_s[pi - 2])]),
                            inc="t")
                        ctx.op("scalar", lambda e, c=c, t4=t4: e.copy(
                            qtok2[:, D * t4 + 128 * c: D * t4 + 128 * (c + 1)],
                            pT[c % 2][:]),
                            waits=[("t", ctx.val("t"))], inc="s")
                        ev_s[pi] = ctx.val("s")
                        pi += 1
                    for c in range(DC):
                        ctx.op("tensor", lambda e, c=c, t=t: e.matmul(
                            pqd[:, 0:144],
                            QT[:, c * T + 128 * t: c * T + 128 * (t + 1)],
                            tkTs[:, 144 * c:144 * (c + 1)],
                            start=(c == 0), stop=(c == DC - 1)),
                            waits=([("v", tvb_free[0])] if c == 0 else []),
                            inc="t")
                    ctx.op("scalar", lambda e, t4=t4: e.copy(
                        qdT2[:, 144 * t4:144 * (t4 + 1)], pqd[:, 0:144]),
                        waits=[("t", ctx.val("t"))], inc="s")
                    ctx.op("scalar", lambda e, t4=t4: e.activation(
                        Acol2[:, 16 * t4:16 * (t4 + 1)],
                        qdT2[:, 144 * t4:144 * (t4 + 1)]
                        .rearrange("p (h j) -> p h j", j=9)[:, :, 0:1].squeeze(2),
                        AF.Exp, scale=SC8), inc="s")
                    ctx.op("scalar", lambda e, t4=t4: e.activation(
                        Bcol2[:, 16 * t4:16 * (t4 + 1)],
                        qdT2[:, 144 * t4:144 * (t4 + 1)]
                        .rearrange("p (h j) -> p h j", j=9)[:, :, 8:9].squeeze(2),
                        AF.Exp, scale=SC8), inc="s")
                qtok_done_s = ctx.val("s")
                AB_done = ctx.val("s")
                pair_v = tvb_free[0]
                pair_g = ctx.val("g")
                exp_prev[0] = max(exp_prev[0], ctx.val("s"))

                bd_done = [0, 0]
                for t4 in range(2):
                    t = tp0 + t4
                    ksrc = bass.AP(ktok_d.tensor, (128 * t) * D, [[D, 128], [D, 7], [1, D]])
                    ctx.dma("gpsimd", win.rearrange("p (o d) -> p o d", d=D), ksrc,
                            waits=[("dma_out", ktok_done), ("v", ctx.val("v"))],
                            inc="dma_win")
                    kwin_ld = ctx.val("dma_win")
                    ctx.op("vector", lambda e, t4=t4: e.memset(
                        bd2[:, 128 * t4:128 * (t4 + 1)], -1.0e30),
                        waits=[("s", ctx.val("s"))], inc="v")
                    for o in range(7):
                        ctx.op("vector", lambda e, o=o, t4=t4: e.tensor_tensor(
                            prod[:], qtok2[:, D * t4:D * (t4 + 1)],
                            win[:, D * o:D * (o + 1)], op=ALU.mult),
                            waits=[("dma_win", kwin_ld), ("s", qtok_done_s),
                                   ("v", ctx.val("v"))], inc="v")
                        ctx.op("vector", lambda e, o=o, t4=t4: e.tensor_reduce(
                            bd2[:, 128 * t4:128 * (t4 + 1)]
                            .rearrange("p (h o) -> p h o", o=8)[:, :, o:o + 1].squeeze(2),
                            prod[:].rearrange("p (h d) -> p h d", d=64),
                            axis=AX.X, op=ALU.add),
                            waits=[("v", ctx.val("v"))], inc="v")
                    bd_done[t4] = ctx.val("v")
                    qtok_free[0] = ctx.val("v")

                exp_ends = {}
                evac_end = {-1: 0}
                fu_t_end = {-1: 0, -2: 0}
                g8_done = [0, 0]
                accb_done = [0, 0]

                def sched_st(h):
                    p0 = 64 * (h % 2)
                    mo = (h // 2) * T
                    base_s = ctx.val("s")
                    p1 = p1t[h % 2]
                    for c in range(NT):
                        if c == 0:
                            w = [("s", exp_prev[0]), ("g", pair_g), ("v", pair_v)]
                        elif c % 4 == 0:
                            w = [("s", base_s + c // 4)]
                        else:
                            w = []
                        ctx.op("tensor", lambda e, h=h, c=c, p0=p0, mo=mo: e.matmul(
                            pst[:, 512 * (c % 4):512 * (c % 4) + 256],
                            KT[p0:p0 + 64, mo + 128 * c: mo + 128 * (c + 1)],
                            QT[p0:p0 + 64, mo + 256 * gq: mo + 256 * (gq + 1)],
                            start=True, stop=True),
                            waits=w, inc="t")
                        if c % 4 == 3:
                            ctx.op("scalar", lambda e, c=c, p1=p1: e.activation(
                                p1[:, 256 * (c - 3):256 * (c + 1)],
                                pst[:].rearrange("p (i x) -> p i x", x=512)
                                [:, :, 0:256],
                                AF.Exp, scale=SC8),
                                waits=[("t", ctx.val("t"))], inc="s")
                    exp_ends[h] = ctx.val("s")
                    exp_prev[0] = exp_ends[h]

                def sched_fu(h):
                    p1 = p1t[h % 2]
                    swait = max(exp_ends[h], evac_end[h - 1])
                    for t4 in range(2):
                        t = tp0 + t4
                        ctx.op("gpsimd", lambda e, t=t, t4=t4, p1=p1: e.affine_select(
                            msk1[:], p1[:, 256 * t + 128 * t4: 256 * t + 128 * (t4 + 1)],
                            pattern=[[-1, 128]], compare_op=ALU.is_ge, fill=0.0,
                            base=-4, channel_multiplier=1),
                            waits=[("s", exp_ends[h]),
                                   ("t", fu_t_end[h - 1] if t4 == 0 else ctx.val("t"))],
                            inc="g")
                        if t < NT - 1:
                            ctx.op("gpsimd", lambda e, t=t, t4=t4, p1=p1: e.affine_select(
                                msk2[:], p1[:, 256 * (t + 1) + 128 * t4:
                                            256 * (t + 1) + 128 * (t4 + 1)],
                                pattern=[[-1, 128]], compare_op=ALU.is_ge, fill=0.0,
                                base=124, channel_multiplier=1), inc="g")
                        msk_done = ctx.val("g")
                        for c in range(NT):
                            ctx.op("tensor", lambda e, c=c, h=h, t4=t4, p1=p1: e.matmul(
                                pF[:, 65 * t4:65 * (t4 + 1)],
                                p1[:, 256 * c + 128 * t4: 256 * c + 128 * (t4 + 1)],
                                vaug[:, 1040 * c + 65 * h: 1040 * c + 65 * (h + 1)],
                                start=(c == 0), stop=(c == NT - 1)),
                                waits=([("s", swait), ("g", pair_g)]
                                       if c == 0 else []),
                                inc="t")
                        nU = (1 if t == NT - 1 else 2) + max(0, NT - (t + 2))
                        iU = 0
                        ctx.op("tensor", lambda e, h=h, t=t, t4=t4, nU=nU: e.matmul(
                            pU[:, 65 * t4:65 * (t4 + 1)], msk1[:],
                            vaug[:, 1040 * t + 65 * h: 1040 * t + 65 * (h + 1)],
                            start=True, stop=(nU == 1)),
                            waits=[("g", msk_done), ("s", swait)], inc="t")
                        iU += 1
                        if t < NT - 1:
                            ctx.op("tensor", lambda e, h=h, t=t, t4=t4, nU=nU, iU=iU: e.matmul(
                                pU[:, 65 * t4:65 * (t4 + 1)], msk2[:],
                                vaug[:, 1040 * (t + 1) + 65 * h: 1040 * (t + 1) + 65 * (h + 1)],
                                start=False, stop=(iU == nU - 1)), inc="t")
                            iU += 1
                        for c in range(t + 2, NT):
                            ctx.op("tensor", lambda e, c=c, h=h, t4=t4, iU=iU, nU=nU, p1=p1: e.matmul(
                                pU[:, 65 * t4:65 * (t4 + 1)],
                                p1[:, 256 * c + 128 * t4: 256 * c + 128 * (t4 + 1)],
                                vaug[:, 1040 * c + 65 * h: 1040 * c + 65 * (h + 1)],
                                start=False, stop=(iU == nU - 1)), inc="t")
                            iU += 1
                    fu_t_end[h] = ctx.val("t")
                    ctx.op("scalar", lambda e, h=h: e.copy(
                        Fsb2[:].rearrange("p (q x) -> p q x", x=1040)[:, :, 65 * h:65 * (h + 1)],
                        pF.rearrange("p (q x) -> p q x", x=65)),
                        waits=[("t", ctx.val("t"))], inc="s")
                    ctx.op("scalar", lambda e, h=h: e.copy(
                        Usb2[:].rearrange("p (q x) -> p q x", x=1040)[:, :, 65 * h:65 * (h + 1)],
                        pU.rearrange("p (q x) -> p q x", x=65)), inc="s")
                    evac_end[h] = ctx.val("s")

                def sched_band2(t4):
                    t = tp0 + t4
                    bslice = bd2[:, 128 * t4:128 * (t4 + 1)]
                    qdT = qdT2[:, 144 * t4:144 * (t4 + 1)]
                    ctx.op("vector", lambda e: e.memset(warg[:], -1.0e30),
                           waits=[("s", ctx.val("s")), ("v", bd_done[t4])], inc="v")
                    ctx.op("vector", lambda e, bslice=bslice, qdT=qdT: e.tensor_tensor(
                        warg[:].rearrange("p (h o) -> p h o", o=8)[:, :, 0:7],
                        bslice.rearrange("p (h o) -> p h o", o=8)[:, :, 0:7],
                        qdT.rearrange("p (h j) -> p h j", j=9)[:, :, 1:8],
                        op=ALU.add), waits=[("v", ctx.val("v"))], inc="v")
                    warg_done = ctx.val("v")
                    ctx.op("scalar", lambda e, t4=t4: e.activation(
                        w8_2[:, 128 * t4:128 * (t4 + 1)], warg[:], AF.Exp, scale=SC8),
                        waits=[("v", warg_done)], inc="s")
                    ctx.op("scalar", lambda e, t4=t4, bslice=bslice: e.activation(
                        p1d8[:], bslice, AF.Exp, scale=SC8), inc="s")
                    exp8_done = ctx.val("s")
                    if t == 0 or t == NT - 1:
                        ei = 0 if t == 0 else 1
                        for buf in (w8_2[:, 128 * t4:128 * (t4 + 1)], p1d8[:]):
                            ctx.op("vector", lambda e, buf=buf, ei=ei: e.tensor_tensor(
                                buf.rearrange("p (h o) -> p h o", o=8),
                                buf.rearrange("p (h o) -> p h o", o=8),
                                emsks[:, 8 * ei:8 * (ei + 1)].unsqueeze(1)
                                .broadcast_to([128, NH, 8]), op=ALU.mult),
                                waits=[("s", exp8_done)], inc="v")
                    ctx.op("vector", lambda e, t4=t4: e.tensor_reduce(
                        Ssum2[:, 64 * t4:64 * t4 + 16],
                        w8_2[:, 128 * t4:128 * (t4 + 1)]
                        .rearrange("p (h o) -> p h o", o=8),
                        axis=AX.X, op=ALU.add),
                        waits=[("s", exp8_done), ("v", ctx.val("v"))], inc="v")
                    ctx.op("vector", lambda e, t4=t4: e.tensor_reduce(
                        Ssum2[:, 64 * t4 + 16:64 * t4 + 32],
                        p1d8[:].rearrange("p (h o) -> p h o", o=8),
                        axis=AX.X, op=ALU.add),
                        waits=[("v", ctx.val("v"))], inc="v")
                    ctx.op("vector", lambda e, t4=t4: e.tensor_copy(
                        Ab[:], Acol2[:, 16 * t4:16 * (t4 + 1)]),
                        waits=[("s", AB_done), ("v", ctx.val("v"))], inc="v")
                    ctx.op("vector", lambda e, t4=t4: e.tensor_tensor(
                        tg8_2[:, 128 * t4:128 * (t4 + 1)]
                        .rearrange("p (h o) -> p h o", o=8),
                        p1d8[:].rearrange("p (h o) -> p h o", o=8),
                        Ab[:].unsqueeze(2).broadcast_to([128, NH, 8]), op=ALU.mult),
                        waits=[("v", ctx.val("v"))], inc="v")
                    ctx.op("vector", lambda e, t4=t4: e.tensor_tensor(
                        tg8_2[:, 128 * t4:128 * (t4 + 1)],
                        w8_2[:, 128 * t4:128 * (t4 + 1)],
                        tg8_2[:, 128 * t4:128 * (t4 + 1)], op=ALU.subtract),
                        waits=[("v", ctx.val("v"))], inc="v")
                    g8_done[t4] = ctx.val("v")

                def sched_band3(t4):
                    t = tp0 + t4
                    vsrc = bass.AP(vtok_d.tensor, (128 * t) * D, [[D, 128], [D, 7], [1, D]])
                    ctx.dma("gpsimd", win.rearrange("p (o d) -> p o d", d=D), vsrc,
                            waits=[("v", g8_done[t4]), ("v", ctx.val("v")),
                                   ("dma_out", ctx.val("dma_out"))],
                            inc="dma_win")
                    vwin_ld = ctx.val("dma_win")
                    for o in range(7):
                        ctx.op("vector", lambda e, o=o, t4=t4: e.tensor_tensor(
                            prod[:].rearrange("p (h d) -> p h d", d=64),
                            win[:, D * o:D * (o + 1)].rearrange("p (h d) -> p h d", d=64),
                            tg8_2[:, 128 * t4:128 * (t4 + 1)]
                            .rearrange("p (h o) -> p h o", o=8)[:, :, o:o + 1]
                            .broadcast_to([128, NH, 64]), op=ALU.mult),
                            waits=[("dma_win", vwin_ld), ("v", ctx.val("v"))], inc="v")
                        if o == 0:
                            ctx.op("vector", lambda e, t4=t4: e.tensor_copy(
                                accb2[:, D * t4:D * (t4 + 1)], prod[:]),
                                waits=[("v", ctx.val("v"))], inc="v")
                        else:
                            ctx.op("vector", lambda e, t4=t4: e.tensor_tensor(
                                accb2[:, D * t4:D * (t4 + 1)],
                                accb2[:, D * t4:D * (t4 + 1)], prod[:], op=ALU.add),
                                waits=[("v", ctx.val("v"))], inc="v")
                    accb_done[t4] = ctx.val("v")

                for h in range(nh):
                    sched_st(h)
                    if h == 6:
                        sched_band2(0)
                        sched_band2(1)
                    if h == 9:
                        sched_band3(0)
                        sched_band3(1)
                    if h > 0:
                        sched_fu(h - 1)
                sched_fu(nh - 1)
                FU_done = ctx.val("s")

                for t4 in range(2):
                    t = tp0 + t4
                    qdT = qdT2[:, 144 * t4:144 * (t4 + 1)]
                    Acol = Acol2[:, 16 * t4:16 * (t4 + 1)]
                    Bcol = Bcol2[:, 16 * t4:16 * (t4 + 1)]
                    Fsb = Fsb2[:, 1040 * t4:1040 * (t4 + 1)]
                    Usb = Usb2[:, 1040 * t4:1040 * (t4 + 1)]
                    Ssum = Ssum2[:, 64 * t4:64 * (t4 + 1)]
                    accb = accb2[:, D * t4:D * (t4 + 1)]
                    ctx.op("tensor", lambda e, t4=t4: e.transpose(
                        pT[0][:], w8_2[:, 128 * t4:128 * (t4 + 1)], idb[:]),
                        waits=[("v", ctx.val("v"))], inc="t")
                    ctx.op("vector", lambda e: e.tensor_copy(w8T[:], pT[0][:]),
                           waits=[("t", ctx.val("t"))], inc="v")
                    w8T_done = ctx.val("v")
                    for gg in range(2):
                        ctx.op("tensor", lambda e, gg=gg: e.matmul(
                            pqd, w8T[:], tvbigs[:, 512 * gg:512 * (gg + 1)],
                            start=True, stop=True),
                            waits=[("v", w8T_done), ("v", ctx.val("v"))], inc="t")
                        ctx.op("vector", lambda e, gg=gg: e.tensor_copy(
                            tvb_sb[:, 512 * gg:512 * (gg + 1)], pqd),
                            waits=[("t", ctx.val("t"))], inc="v")
                    tvb_done = ctx.val("v")
                    tvb_free[0] = tvb_done
                    F3 = Fsb.rearrange("p (h x) -> p h x", x=65)
                    U3 = Usb.rearrange("p (h x) -> p h x", x=65)
                    ctx.op("vector", lambda e, F3=F3, U3=U3: e.tensor_tensor(
                        t0f[:].rearrange("p (h d) -> p h d", d=64),
                        F3[:, :, 0:64], U3[:, :, 0:64], op=ALU.subtract),
                        waits=[("s", FU_done), ("v", ctx.val("v")),
                               ("dma_out", ctx.val("dma_out"))], inc="v")
                    ctx.op("vector", lambda e, Acol=Acol: e.tensor_tensor(
                        numb[:].rearrange("p (h d) -> p h d", d=64),
                        t0f[:].rearrange("p (h d) -> p h d", d=64),
                        Acol.unsqueeze(2).broadcast_to([128, NH, 64]), op=ALU.mult),
                        waits=[("v", ctx.val("v")), ("s", AB_done)], inc="v")
                    ctx.op("vector", lambda e, U3=U3, Bcol=Bcol: e.tensor_tensor(
                        tb[:].rearrange("p (h d) -> p h d", d=64),
                        U3[:, :, 0:64],
                        Bcol.unsqueeze(2).broadcast_to([128, NH, 64]), op=ALU.mult),
                        waits=[("v", ctx.val("v"))], inc="v")
                    ctx.op("vector", lambda e: e.tensor_tensor(
                        numb[:], numb[:], tb[:], op=ALU.add),
                        waits=[("v", ctx.val("v"))], inc="v")
                    ctx.op("vector", lambda e, accb=accb, t4=t4: e.tensor_tensor(
                        numb[:], numb[:], accb, op=ALU.add),
                        waits=[("v", accb_done[t4])], inc="v")
                    ctx.op("vector", lambda e: e.tensor_tensor(
                        numb[:], numb[:], tvb_sb[:], op=ALU.add),
                        waits=[("v", tvb_done)], inc="v")
                    ctx.op("vector", lambda e, F3=F3, U3=U3, Ssum=Ssum: e.tensor_tensor(
                        Ssum[:, 32:48], F3[:, :, 64:65].squeeze(2),
                        U3[:, :, 64:65].squeeze(2), op=ALU.subtract),
                        waits=[("v", ctx.val("v")), ("s", FU_done)], inc="v")
                    ctx.op("vector", lambda e, Ssum=Ssum: e.tensor_tensor(
                        Ssum[:, 32:48], Ssum[:, 32:48], Ssum[:, 16:32],
                        op=ALU.subtract), waits=[("v", ctx.val("v"))], inc="v")
                    ctx.op("vector", lambda e, Ssum=Ssum, Acol=Acol: e.tensor_tensor(
                        den[:], Acol, Ssum[:, 32:48], op=ALU.mult),
                        waits=[("v", ctx.val("v"))], inc="v")
                    ctx.op("vector", lambda e, Ssum=Ssum, U3=U3, Bcol=Bcol: e.tensor_tensor(
                        Ssum[:, 48:64], Bcol, U3[:, :, 64:65].squeeze(2),
                        op=ALU.mult), waits=[("v", ctx.val("v")), ("s", AB_done)], inc="v")
                    ctx.op("vector", lambda e, Ssum=Ssum: e.tensor_tensor(
                        den[:], den[:], Ssum[:, 48:64], op=ALU.add),
                        waits=[("v", ctx.val("v"))], inc="v")
                    ctx.op("vector", lambda e, Ssum=Ssum: e.tensor_tensor(
                        den[:], den[:], Ssum[:, 0:16], op=ALU.add),
                        waits=[("v", ctx.val("v"))], inc="v")
                    ctx.op("vector", lambda e, Ssum=Ssum, Acol=Acol: e.tensor_tensor(
                        Ssum[:, 32:48], Acol, Ssum[:, 32:48], op=ALU.mult),
                        waits=[("v", ctx.val("v"))], inc="v")
                    ctx.op("vector", lambda e, Ssum=Ssum: e.tensor_tensor(
                        tb[:].rearrange("p (h d) -> p h d", d=64),
                        tv0b[:].rearrange("p (h d) -> p h d", d=64),
                        Ssum[:, 32:48].unsqueeze(2).broadcast_to([128, NH, 64]),
                        op=ALU.mult), waits=[("v", ctx.val("v"))], inc="v")
                    ctx.op("vector", lambda e: e.tensor_tensor(
                        numb[:], numb[:], tb[:], op=ALU.add), inc="v")
                    ctx.op("vector", lambda e, Ssum=Ssum: e.tensor_tensor(
                        tb[:].rearrange("p (h d) -> p h d", d=64),
                        tv8b[:].rearrange("p (h d) -> p h d", d=64),
                        Ssum[:, 48:64].unsqueeze(2).broadcast_to([128, NH, 64]),
                        op=ALU.mult), waits=[("v", ctx.val("v"))], inc="v")
                    ctx.op("vector", lambda e: e.tensor_tensor(
                        numb[:], numb[:], tb[:], op=ALU.add), inc="v")
                    ctx.op("vector", lambda e: e.reciprocal(rden[:], den[:]),
                           waits=[("v", ctx.val("v"))], inc="v")
                    ctx.op("vector", lambda e: e.tensor_tensor(
                        ao[:].rearrange("p (h d) -> p h d", d=64),
                        numb[:].rearrange("p (h d) -> p h d", d=64),
                        rden[:].unsqueeze(2).broadcast_to([128, NH, 64]), op=ALU.mult),
                        waits=[("v", ctx.val("v"))], inc="v")
                    ao_done = ctx.val("v")
                    ctx.dma("sync", xres[:, 0:D], x1[b, 128 * t:128 * (t + 1), :],
                            waits=[("v", ctx.val("v"))], inc="dma")
                    x_ld = ctx.val("dma")
                    ev = {}
                    w0 = ctx.val("v")
                    for c in range(DC):
                        ctx.op("tensor", lambda e, c=c: e.transpose(
                            pT[c % 2][:], ao[:, 128 * c:128 * (c + 1)], idb[:]),
                            waits=([("v", w0)] if c < 2 else [("v", ev[c - 2])]),
                            inc="t")
                        ctx.op("vector", lambda e, c=c: e.tensor_copy(
                            aoT[:, 128 * c:128 * (c + 1)], pT[c % 2][:]),
                            waits=[("t", ctx.val("t"))], inc="v")
                        ev[c] = ctx.val("v")
                    aoT_done = ctx.val("v")
                    pT_free[0] = aoT_done
                    op_free = ctx.val("v")
                    for g2 in range(2):
                        for c in range(DC):
                            ctx.op("tensor", lambda e, g2=g2, c=c: e.matmul(
                                pOP[:],
                                aoT[:, 128 * c:128 * (c + 1)],
                                w_sb[:, c * D + 512 * g2: c * D + 512 * (g2 + 1)],
                                start=(c == 0), stop=(c == DC - 1)),
                                waits=([("v", aoT_done if g2 == 0 else op_free),
                                        ("s", ctx.val("s")),
                                        ("dma_w", wo_ld)]
                                       if c == 0 else []),
                                inc="t")
                        ctx.op("vector", lambda e, g2=g2: e.tensor_tensor(
                            po[:, 512 * g2:512 * (g2 + 1)],
                            pOP[:],
                            bvob[:, D + 512 * g2: D + 512 * (g2 + 1)], op=ALU.add),
                            waits=[("t", ctx.val("t")), ("dma", x_ld),
                                   ("dma_out", ctx.val("dma_out"))], inc="v")
                        ctx.op("vector", lambda e, g2=g2: e.tensor_tensor(
                            po[:, 512 * g2:512 * (g2 + 1)],
                            po[:, 512 * g2:512 * (g2 + 1)],
                            xres[:, 512 * g2:512 * (g2 + 1)], op=ALU.add), inc="v")
                        op_free = ctx.val("v")
                    ctx.dma("sync", part2[b, 128 * t:128 * (t + 1), :], po[:],
                            waits=[("v", ctx.val("v"))], inc="dma_out")

        ctx.flush(block)


def l1att_consts(inputs):
    biases = np.concatenate([np.asarray(inputs["L1_attn_b" + n], np.float32)
                             for n in "qkvo"]).reshape(1, 4 * D)
    tk = np.asarray(inputs["L1_attn_table_k"], np.float32)
    tv = np.asarray(inputs["L1_attn_table_v"], np.float32)
    tkbig = np.zeros((D, NH * 9), np.float32)
    for h in range(NH):
        tkbig[64 * h:64 * (h + 1), 9 * h:9 * (h + 1)] = tk.T
    tvbig = np.zeros((128, D), np.float32)
    for h in range(NH):
        for o in range(7):
            tvbig[h * 8 + o, h * 64:(h + 1) * 64] = tv[o + 1]
    emsk = np.ones((2, 128, 8), np.float32)
    emsk[:, :, 7] = 0.0
    for p in range(128):
        for o in range(7):
            if p + (o - 3) < 0:
                emsk[0, p, o] = 0.0
            if (NT - 1) * 128 + p + (o - 3) > T - 1:
                emsk[1, p, o] = 0.0
    return {
        "wq": np.asarray(inputs["L1_attn_Wq"], np.float32),
        "wk": np.asarray(inputs["L1_attn_Wk"], np.float32),
        "wv": np.asarray(inputs["L1_attn_Wv"], np.float32),
        "wo": np.asarray(inputs["L1_attn_Wo"], np.float32),
        "biases": biases, "tkbig": tkbig, "tvbig": tvbig,
        "tv0r": np.tile(tv[0], NH).reshape(1, D),
        "tv8r": np.tile(tv[8], NH).reshape(1, D),
        "emsk": emsk, "zpad": np.zeros((4, D), np.float32),
        "idn": np.eye(128, dtype=np.float32),
        "ones1": np.ones((1, 128), np.float32),
    }


def emit_l1ffn(nc, ctx, block, io):
    xdT = io["xdT"]
    part2 = io["part2"]
    wf = io["l1fw"]
    bias = io["l1fb"]
    idn = io["idn"]
    ones1 = io["ones1"]
    out = io["out"]

    with contextlib.ExitStack() as es:
        sb = lambda *a: es.enter_context(nc.sbuf_tensor(*a))
        ps = lambda *a: es.enter_context(nc.psum_tensor(*a))
        idb = sb("idb", [128, 128], BF16)
        onesr = sb("onesr", [1, 128], F32)
        brow = sb("brow", [1, D], F32)
        bb = sb("bb", [128, D], F32)
        wrow = sb("wrow", [128, D], F32)
        wdq = sb("wdq", [128, D], BF16)
        wdT = sb("wdT", [128, DC * D], BF16)
        am = sb("am", [128, 32], F32)
        scq = sb("scq", [128, 32], F32)
        rscq = sb("rscq", [128, 32], F32)
        qu = sb("qu", [128, D], F32)
        qt0 = sb("qt0", [128, D], F32)
        qt1 = sb("qt1", [128, D], F32)
        qic = sb("qic", [128, D], mybir.dt.int32)
        qfb = sb("qfb", [128, D], F32)
        xdT_sb = sb("xdT_sb", [128, DC * T], BF16)
        po = sb("po", [128, D], F32)
        pT0 = ps("pT0", [128, 128], BF16)
        pT1 = ps("pT1", [128, 128], BF16)
        pmm0 = ps("pmm0", [128, 512], F32)
        pmm1 = ps("pmm1", [128, 512], F32)
        pbc = ps("pbc", [128, 512], F32)
        pT = [pT0, pT1]
        pmm = [pmm0, pmm1]

        ctx.dma("gpsimd", idb[:], idn[:], inc="dmag")
        ctx.dma("sync", onesr[:], ones1[:], inc="dma")
        ctx.dma("sync", brow[:], bias[:], inc="dma")
        const_ld = ctx.val("dma")
        ctx.wait("tensor", "dma", const_ld)
        ctx.wait("tensor", "dmag", ctx.val("dmag"))
        ctx.wait("vector", "dma", const_ld)
        emit_bcast_row(ctx, brow[:], bb, onesr[:], pbc, D)

        for r in range(DC):
            ctx.dma("sync", wrow[:], wf[128 * r:128 * (r + 1), :],
                    waits=[("v", ctx.val("v"))], inc="dma")
            ctx.wait("vector", "dma", ctx.val("dma"))
            ctx.wait("vector", "t", ctx.val("t"))
            emit_quant(ctx, wrow[:], wdq[:], am[:], scq[:], rscq[:],
                       qu[:], qt0[:], qt1[:], qic[:], qfb[:])
            for c in range(DC):
                ctx.op("tensor", lambda e, r=r, c=c: e.transpose(
                    pT[c % 2][:], wdq[:, 128 * c:128 * (c + 1)], idb[:]),
                    waits=[("v", ctx.val("v"))], inc="t")
                ctx.op("vector", lambda e, r=r, c=c: e.tensor_copy(
                    wdT[:, c * D + 128 * r: c * D + 128 * (r + 1)], pT[c % 2][:]),
                    waits=[("t", ctx.val("t"))], inc="v")
        wd_done = ctx.val("v")

        for b in range(BPC):
            ctx.dma("sync", xdT_sb[:].rearrange("p (c q) -> p c q", q=T),
                    xdT[b].rearrange("c p q -> p c q"),
                    waits=[("t", ctx.val("t"))], inc="dma")
            xd_ld = ctx.val("dma")
            for t in range(NT):
                for g2 in range(2):
                    for c in range(DC):
                        ctx.op("tensor", lambda e, t=t, g2=g2, c=c: e.matmul(
                            pmm[g2][:],
                            xdT_sb[:, c * T + 128 * t: c * T + 128 * (t + 1)],
                            wdT[:, c * D + 512 * g2: c * D + 512 * (g2 + 1)],
                            start=(c == 0), stop=(c == DC - 1)),
                            waits=([("v", wd_done), ("dma", xd_ld),
                                    ("v", ctx.val("v"))] if c == 0 else []),
                            inc="t")
                ctx.dma("sync", po[:], part2[b, 128 * t:128 * (t + 1), :],
                        waits=[("dma_out", ctx.val("dma_out"))], inc="dma")
                p_ld = ctx.val("dma")
                for g2 in range(2):
                    ctx.op("vector", lambda e, g2=g2: e.tensor_tensor(
                        po[:, 512 * g2:512 * (g2 + 1)],
                        po[:, 512 * g2:512 * (g2 + 1)],
                        pmm[g2][:], op=ALU.add),
                        waits=[("t", ctx.val("t")), ("dma", p_ld)], inc="v")
                    ctx.op("vector", lambda e, g2=g2: e.tensor_tensor(
                        po[:, 512 * g2:512 * (g2 + 1)],
                        po[:, 512 * g2:512 * (g2 + 1)],
                        bb[:, 512 * g2:512 * (g2 + 1)], op=ALU.add), inc="v")
                ctx.dma("sync", out[b, 128 * t:128 * (t + 1), :], po[:],
                        waits=[("v", ctx.val("v"))], inc="dma_out")

        ctx.flush(block)


ALL_SEMS = ("dma", "dmag", "dma_out", "dma_w", "dma_win", "dma_wt",
            "t", "v", "s", "g", "t_wt", "v_wt", "t_misc", "v_misc")

IO_SPEC = {
    "x0": ([BPC, T, D], F32),
    "l0wg": ([D, NE], F32),
    "l0wq": ([D, D], F32), "l0wk": ([D, D], F32),
    "l0wv": ([D, D], F32), "l0wo": ([D, D], F32),
    "biases0": ([1, 4 * D], F32),
    "W1": ([NE, D, HMoE], F32), "b1": ([NE, HMoE], F32),
    "W2": ([NE, HMoE, D], F32), "b2": ([NE, D], F32),
    "l1wq": ([D, D], F32), "l1wk": ([D, D], F32),
    "l1wv": ([D, D], F32), "l1wo": ([D, D], F32),
    "biases1": ([1, 4 * D], F32),
    "tkbig": ([D, NH * 9], F32), "tvbig": ([128, D], F32),
    "tv0r": ([1, D], F32), "tv8r": ([1, D], F32),
    "emsk": ([2, 128, 8], F32), "zpad": ([4, D], F32),
    "l1fw": ([D, D], F32), "l1fb": ([1, D], F32),
    "idn": ([128, 128], F32), "ones1": ([1, 128], F32),
    "xnT": ([BPC, DC, 128, T], BF16),
    "scores": ([BPC, 128, NT, NE], F32),
    "part1": ([BPC, T, D], F32),
    "x1": ([BPC, T, D], F32),
    "xn1T": ([BPC, DC, 128, T], BF16),
    "xdT": ([BPC, DC, 128, T], BF16),
    "part2": ([BPC, T, D], F32),
    "ktok_d": ([T + 6, D], BF16),
    "vtok_d": ([T + 6, D], BF16),
    "out": ([BPC, T, D], F32),
}

PHASES = {
    "l0ln": (["x0", "l0wg"], ["xnT", "scores"], []),
    "l0att": (["x0", "xnT", "l0wq", "l0wk", "l0wv", "l0wo", "biases0",
               "idn", "ones1"], ["part1"], []),
    "l0moe": (["xnT", "scores", "part1", "W1", "b1", "W2", "b2",
               "idn", "ones1"], ["x1"], []),
    "l1ln": (["x1", "idn"], ["xn1T", "xdT"], []),
    "l1att": (["x1", "xn1T", "l1wq", "l1wk", "l1wv", "l1wo", "biases1",
               "tkbig", "tvbig", "tv0r", "tv8r", "emsk", "zpad",
               "idn", "ones1"], ["part2"], ["ktok_d", "vtok_d"]),
    "l1ffn": (["xdT", "part2", "l1fw", "l1fb", "idn", "ones1"], ["out"], []),
}


def _declare(nc, names, kind):
    return {n: nc.dram_tensor(n, IO_SPEC[n][0], IO_SPEC[n][1], kind=kind).ap()
            for n in names}


def build_single(phase):
    ins, outs, internals = PHASES[phase]
    nc = bass.Bass("TRN2", target_bir_lowering=False)
    io = {}
    io.update(_declare(nc, ins, "ExternalInput"))
    io.update(_declare(nc, outs, "ExternalOutput"))
    io.update(_declare(nc, internals, "Internal"))
    with nc.Block() as block:
        ctx = Ctx(nc)
        for sname in ALL_SEMS:
            ctx.sem(sname)
        EMITTERS[phase](nc, ctx, block, io)
        ctx.wait("sync", "dma_out")
        ctx.flush(block)
    return nc


FUSED_ORDER = ["l0ln", "l0att", "l0moe", "l1ln", "l1att", "l1ffn"]

MODULE_INS = ["x0", "l0wg", "l0wq", "l0wk", "l0wv", "l0wo", "biases0",
              "W1", "b1", "W2", "b2",
              "l1wq", "l1wk", "l1wv", "l1wo", "biases1",
              "tkbig", "tvbig", "tv0r", "tv8r", "emsk", "zpad",
              "l1fw", "l1fb", "idn", "ones1"]
HANDOFFS = ["xnT", "scores", "part1", "x1", "xn1T", "xdT", "part2",
            "ktok_d", "vtok_d"]


class _NcProxy:

    def __init__(self, nc, prefix):
        self._nc = nc
        self._p = prefix

    def sbuf_tensor(self, name, *a, **k):
        return self._nc.sbuf_tensor(self._p + name, *a, **k)

    def psum_tensor(self, name, *a, **k):
        return self._nc.psum_tensor(self._p + name, *a, **k)

    def __getattr__(self, item):
        return getattr(self._nc, item)


def build_fused():
    nc = bass.Bass("TRN2", target_bir_lowering=False)
    io = {}
    io.update(_declare(nc, MODULE_INS, "ExternalInput"))
    io.update(_declare(nc, ["out"], "ExternalOutput"))
    io.update(_declare(nc, HANDOFFS, "Internal"))
    with nc.Block() as block:
        ctx = Ctx(nc)
        for sname in ALL_SEMS:
            ctx.sem(sname)
        for i, phase in enumerate(FUSED_ORDER):
            if i > 0:
                for eng in ENGS:
                    for sname in ALL_SEMS:
                        ctx.wait(eng, sname)
            EMITTERS[phase](_NcProxy(nc, phase + "_"), ctx, block, io)
        ctx.wait("sync", "dma_out")
        ctx.flush(block)
    return nc


def _shard_inputs(inputs):
    x = np.ascontiguousarray(inputs["L0_attn_x"])
    return [np.ascontiguousarray(x[c * BPC:(c + 1) * BPC]) for c in range(NCORES)]


def module_consts(inputs):
    d = l1att_consts(inputs)
    consts = {
        "l0wg": np.ascontiguousarray(np.asarray(inputs["L0_ffn_Wg"], np.float32)),
        "l0wq": np.asarray(inputs["L0_attn_wq"], np.float32),
        "l0wk": np.asarray(inputs["L0_attn_wk"], np.float32),
        "l0wv": np.asarray(inputs["L0_attn_wv"], np.float32),
        "l0wo": np.asarray(inputs["L0_attn_wo"], np.float32),
        "biases0": np.stack([np.asarray(inputs["L0_attn_b" + n], np.float32)
                             for n in "qkvo"]).reshape(1, 4 * D),
        "W1": np.asarray(inputs["L0_ffn_W1"], np.float32),
        "b1": np.asarray(inputs["L0_ffn_b1"], np.float32),
        "W2": np.asarray(inputs["L0_ffn_W2"], np.float32),
        "b2": np.asarray(inputs["L0_ffn_b2"], np.float32),
        "l1wq": d["wq"], "l1wk": d["wk"], "l1wv": d["wv"], "l1wo": d["wo"],
        "biases1": d["biases"],
        "tkbig": d["tkbig"], "tvbig": d["tvbig"],
        "tv0r": d["tv0r"], "tv8r": d["tv8r"],
        "emsk": d["emsk"], "zpad": d["zpad"],
        "l1fw": np.asarray(inputs["L1_ffn_weight"], np.float32),
        "l1fb": np.asarray(inputs["L1_ffn_bias"], np.float32).reshape(1, D),
        "idn": d["idn"], "ones1": d["ones1"],
    }
    return consts


def kernel(**inputs):
    nc = build_fused()
    shards = _shard_inputs(inputs)
    consts = module_consts(inputs)
    in_maps = [dict(consts, x0=shards[c]) for c in range(NCORES)]
    res = run_bass_kernel_spmd(nc, in_maps, core_ids=list(range(NCORES)))
    return np.concatenate([np.asarray(r["out"]) for r in res.results], axis=0)


def kernel_unfused(**inputs):
    res1 = run_l0ln(inputs)
    xnT = [np.asarray(r["xnT"]) for r in res1]
    scores = [np.asarray(r["scores"]) for r in res1]
    res2 = run_l0att(inputs, xnT)
    part1 = [np.asarray(r["part1"]) for r in res2]
    res3 = run_l0moe(inputs, xnT, scores, part1)
    x1 = [np.asarray(r["x1"]) for r in res3]
    res4 = run_l1ln(x1)
    xn1T = [np.asarray(r["xn1T"]) for r in res4]
    xdT = [np.asarray(r["xdT"]) for r in res4]
    res5 = run_l1att(inputs, x1, xn1T)
    part2 = [np.asarray(r["part2"]) for r in res5]
    res6 = run_l1ffn(inputs, xdT, part2)
    return np.concatenate([np.asarray(r["out"]) for r in res6], axis=0)


def run_l0ln(inputs):
    nc = build_single("l0ln")
    shards = _shard_inputs(inputs)
    wg = np.ascontiguousarray(np.asarray(inputs["L0_ffn_Wg"], np.float32))
    in_maps = [{"x0": shards[c], "l0wg": wg} for c in range(NCORES)]
    return run_bass_kernel_spmd(nc, in_maps, core_ids=list(range(NCORES))).results


def run_l0att(inputs, xnT_np):
    nc = build_single("l0att")
    shards = _shard_inputs(inputs)
    cc = module_consts(inputs)
    keys = ["l0wq", "l0wk", "l0wv", "l0wo", "biases0", "idn", "ones1"]
    in_maps = [dict({k: cc[k] for k in keys}, x0=shards[c], xnT=xnT_np[c])
               for c in range(NCORES)]
    return run_bass_kernel_spmd(nc, in_maps, core_ids=list(range(NCORES))).results


def run_l0moe(inputs, xnT_np, scores_np, part1_np):
    nc = build_single("l0moe")
    cc = module_consts(inputs)
    keys = ["W1", "b1", "W2", "b2", "idn", "ones1"]
    in_maps = [dict({k: cc[k] for k in keys}, xnT=xnT_np[c], scores=scores_np[c],
                    part1=part1_np[c]) for c in range(NCORES)]
    return run_bass_kernel_spmd(nc, in_maps, core_ids=list(range(NCORES))).results


def run_l1ln(x1_np):
    nc = build_single("l1ln")
    idn = np.eye(128, dtype=np.float32)
    in_maps = [{"x1": x1_np[c], "idn": idn} for c in range(NCORES)]
    return run_bass_kernel_spmd(nc, in_maps, core_ids=list(range(NCORES))).results


def run_l1att(inputs, x1_np, xn1T_np):
    nc = build_single("l1att")
    cc = module_consts(inputs)
    keys = ["l1wq", "l1wk", "l1wv", "l1wo", "biases1", "tkbig", "tvbig",
            "tv0r", "tv8r", "emsk", "zpad", "idn", "ones1"]
    in_maps = [dict({k: cc[k] for k in keys}, x1=x1_np[c], xn1T=xn1T_np[c])
               for c in range(NCORES)]
    return run_bass_kernel_spmd(nc, in_maps, core_ids=list(range(NCORES))).results


def run_l1ffn(inputs, xdT_np, part2_np):
    nc = build_single("l1ffn")
    cc = module_consts(inputs)
    keys = ["l1fw", "l1fb", "idn", "ones1"]
    in_maps = [dict({k: cc[k] for k in keys}, xdT=xdT_np[c], part2=part2_np[c])
               for c in range(NCORES)]
    return run_bass_kernel_spmd(nc, in_maps, core_ids=list(range(NCORES))).results


EMITTERS = {
    "l0ln": emit_l0ln, "l0att": emit_l0att, "l0moe": emit_l0moe,
    "l1ln": emit_l1ln, "l1att": emit_l1att, "l1ffn": emit_l1ffn,
}


# version 13
# speedup vs baseline: 1.0021x; 1.0021x over previous
import contextlib
import numpy as np
import ml_dtypes
import concourse.bass as bass
import concourse.mybir as mybir
from concourse.bass_utils import run_bass_kernel_spmd

F32 = mybir.dt.float32
BF16 = mybir.dt.bfloat16
AF = mybir.ActivationFunctionType
ALU = mybir.AluOpType
AX = mybir.AxisListType

NCORES = 8
B = 16
BPC = 2
T = 2048
NT = T // 128
D = 1024
DC = D // 128
NH = 16
HD = 64
HMoE = 2048
NE = 8
BF = ml_dtypes.bfloat16

ENGS = ("tensor", "vector", "scalar", "gpsimd", "sync")

L1DBG = {}
DEBUG_L0ATT = False


class Ctx:

    def __init__(self, nc):
        self.nc = nc
        self.q = {e: [] for e in ENGS}
        self.sems = {}
        self.cnt = {}
        self._stack = None

    def sem(self, name):
        h = self.nc.semaphore(name)
        handle = h.__enter__()
        self.sems[name] = handle
        self.cnt[name] = 0
        return handle

    def val(self, name):
        return self.cnt[name]

    def op(self, eng, fn, waits=(), inc=None, n=1):
        waits = list(waits)
        if inc is not None and eng in ("vector", "scalar", "gpsimd"):
            waits.append((inc, self.cnt[inc]))
        waits = tuple((self.sems[s], v) for (s, v) in waits)
        if inc is not None:
            self.cnt[inc] += n
            inc_h = (self.sems[inc], n)
        else:
            inc_h = None

        def emit(e, fn=fn, waits=waits, inc_h=inc_h):
            for (h, v) in waits:
                e.wait_ge(h, v)
            instr = fn(e)
            if inc_h is not None:
                assert instr is not None
                instr.then_inc(inc_h[0], inc_h[1])

        self.q[eng].append(emit)
        return self.cnt[inc] if inc is not None else None

    def dma(self, eng, out, in_, waits=(), inc="dma", **kw):
        waits = list(waits) + [(inc, self.cnt[inc])]
        return self.op(eng, lambda e: e.dma_start(out=out, in_=in_, **kw),
                       waits=waits, inc=inc, n=16)

    def wait(self, eng, semname, value=None):
        v = self.cnt[semname] if value is None else value
        h = self.sems[semname]
        self.q[eng].append(lambda e, h=h, v=v: e.wait_ge(h, v))

    def emit(self, block):
        for eng in ENGS:
            ops = self.q[eng]
            if not ops:
                continue
            fn = {"tensor": block.tensor, "vector": block.vector,
                  "scalar": block.scalar, "gpsimd": block.gpsimd,
                  "sync": block.sync}[eng]

            def body(e, ops=ops):
                for op in ops:
                    op(e)

            fn(body)

    def flush(self, block):
        self.emit(block)
        self.q = {e: [] for e in ENGS}



def emit_weight_transpose(ctx, w_hbm, wT, wstrip, idb, pT, nrows, ncols):
    nrc = nrows // 128
    ncc = ncols // 128
    ev = {}
    i = 0
    v0 = ctx.val("v")
    for r in range(nrc):
        waits = [("t_wt", ctx.val("t_wt"))] if r > 0 else []
        ctx.dma("gpsimd", wstrip[:], w_hbm[128 * r:128 * (r + 1), :],
                waits=waits, inc="dma_wt")
        for c in range(ncc):
            ctx.op("tensor",
                   lambda e, r=r, c=c: e.transpose(
                       pT[c % 2][:], wstrip[:, 128 * c:128 * (c + 1)], idb[:]),
                   waits=[("dma_wt", ctx.val("dma_wt"))] +
                         ([("v_wt", ev[i - 2])] if i >= 2 else
                          [("v", v0)]),
                   inc="t_wt")
            ctx.op("vector",
                   lambda e, r=r, c=c: e.tensor_copy(
                       wT[:, c * nrows + 128 * r: c * nrows + 128 * (r + 1)],
                       pT[c % 2][:]),
                   waits=[("t_wt", ctx.val("t_wt"))], inc="v_wt")
            ev[i] = ctx.val("v_wt")
            i += 1


def emit_col_from_row(ctx, row_sb, col_sb, ones1, psum1, n, name="bias"):
    for c in range(n // 128):
        ctx.op("tensor",
               lambda e, c=c: e.matmul(psum1, row_sb[:, 128 * c:128 * (c + 1)],
                                       ones1, start=True, stop=True),
               waits=[("v_misc", ctx.val("v_misc"))], inc="t_misc")
        ctx.op("vector",
               lambda e, c=c: e.tensor_copy(col_sb[:, c:c + 1], psum1),
               waits=[("t_misc", ctx.val("t_misc"))], inc="v_misc")


def emit_bcast_row(ctx, row_sb, out_sb, ones128, psumw, n):
    for g in range(0, n, 512):
        w = min(512, n - g)
        ctx.op("tensor",
               lambda e, g=g, w=w: e.matmul(psumw[:, 0:w], ones128,
                                            row_sb[:, g:g + w], start=True, stop=True),
               waits=[("v_misc", ctx.val("v_misc"))], inc="t_misc")
        ctx.op("vector",
               lambda e, g=g, w=w: e.tensor_copy(out_sb[:, g:g + w], psumw[:, 0:w]),
               waits=[("t_misc", ctx.val("t_misc"))], inc="v_misc")



def emit_l0ln(nc, ctx, block, io):
    x0 = io["x0"]
    wg = io["l0wg"]
    xnT = io["xnT"]
    scores_o = io["scores"]

    with (
        nc.sbuf_tensor("xt", [128, 2 * D], F32) as xt,
        nc.sbuf_tensor("xc", [128, 2 * D], F32) as xc,
        nc.sbuf_tensor("sq", [128, D], F32) as sq,
        nc.sbuf_tensor("xnb", [128, 2 * D], BF16) as xnb,
        nc.sbuf_tensor("stat", [128, 16], F32) as stat,
        nc.sbuf_tensor("wgb", [128, DC * NE], BF16) as wgb,
        nc.sbuf_tensor("xnT_sb", [128, D], BF16) as xnT_sb,
        nc.sbuf_tensor("idb", [128, 128], BF16) as idb,
        nc.sbuf_tensor("gat", [128, 6 * NE], F32) as gat,
        nc.sbuf_tensor("gcol", [128, 8], F32) as gcol,
        nc.psum_tensor("pT0", [128, 128], BF16) as pT0,
        nc.psum_tensor("pT1", [128, 128], BF16) as pT1,
        nc.psum_tensor("pg", [128, NE], F32) as pg,
    ):
        pT = [pT0, pT1]

        ctx.op("gpsimd", lambda e: e.memset(idb[:], 1.0), inc="g")
        ctx.op("gpsimd", lambda e: e.affine_select(
            idb[:], idb[:], pattern=[[1, 128]], compare_op=ALU.is_ge, fill=0.0,
            base=0, channel_multiplier=-1), waits=[("g", ctx.val("g"))], inc="g")
        ctx.op("gpsimd", lambda e: e.affine_select(
            idb[:], idb[:], pattern=[[-1, 128]], compare_op=ALU.is_ge, fill=0.0,
            base=0, channel_multiplier=1), waits=[("g", ctx.val("g"))], inc="g")
        ctx.dma("gpsimd", wgb[:].rearrange("p (c e) -> p c e", e=NE),
                wg.rearrange("(c p) e -> p c e", p=128), inc="dmag")

        id_done = ctx.val("g")
        wg_done = ctx.val("dmag")

        for b in range(BPC):
            for t in range(NT):
                sl = t % 2
                xts = xt[:, sl * D:(sl + 1) * D]
                xcs = xc[:, sl * D:(sl + 1) * D]
                xnbs = xnb[:, sl * D:(sl + 1) * D]
                waits = [("s", ctx.val("s"))] if (b or t) else []
                ctx.dma("sync", xts, x0[b, 128 * t:128 * (t + 1), :], waits=waits)
                dmav = ctx.val("dma_out" if False else "dma")
                ctx.op("vector", lambda e, xts=xts: e.reduce_sum(
                    stat[:, 0:1], xts, axis=AX.X),
                    waits=[("dma", dmav)], inc="v")
                ctx.op("vector", lambda e: e.tensor_scalar(
                    stat[:, 1:2], stat[:, 0:1], -1.0 / D, None, op0=ALU.mult),
                    waits=[("v", ctx.val("v"))], inc="v")
                ctx.op("scalar", lambda e, xts=xts, xcs=xcs: e.activation(
                    xcs, xts, AF.Identity, bias=stat[:, 1:2], scale=1.0),
                    waits=[("v", ctx.val("v"))], inc="s")
                ctx.op("scalar", lambda e, xcs=xcs: e.activation(
                    sq[:], xcs, AF.Square, accum_out=stat[:, 2:3]), inc="s")
                ctx.op("vector", lambda e: e.tensor_scalar(
                    stat[:, 3:4], stat[:, 2:3], 1.0 / D, 1e-5, op0=ALU.mult,
                    op1=ALU.add), waits=[("s", ctx.val("s"))], inc="v")
                ctx.op("vector", lambda e: e.reciprocal(stat[:, 4:5], stat[:, 3:4]),
                       waits=[("v", ctx.val("v"))], inc="v")
                ctx.op("scalar", lambda e: e.activation(
                    stat[:, 5:6], stat[:, 4:5], AF.Sqrt),
                    waits=[("v", ctx.val("v"))], inc="s")
                ctx.op("vector", lambda e, xcs=xcs, xnbs=xnbs: e.tensor_scalar(
                    xnbs, xcs, stat[:, 5:6], None, op0=ALU.mult),
                    waits=[("s", ctx.val("s"))], inc="v")
                xn_done = ctx.val("v")

                for c in range(DC):
                    ctx.op("tensor", lambda e, c=c, xnbs=xnbs: e.transpose(
                        pT[c % 2][:], xnbs[:, 128 * c:128 * (c + 1)], idb[:]),
                        waits=[("v", xn_done), ("g", id_done)] +
                              ([("v", ctx.val("v") - 1)] if c >= 2 else
                               ([("dma_out", ctx.val("dma_out"))] if (b or t) else [])),
                        inc="t")
                    ctx.op("vector", lambda e, c=c: e.tensor_copy(
                        xnT_sb[:, 128 * c:128 * (c + 1)], pT[c % 2][:]),
                        waits=[("t", ctx.val("t"))], inc="v")
                ctx.dma("sync", xnT[b, :, :, 128 * t:128 * (t + 1)]
                        .rearrange("c p q -> p c q"),
                        xnT_sb[:].rearrange("p (c q) -> p c q", q=128),
                        waits=[("v", ctx.val("v"))], inc="dma_out")

                for c in range(DC):
                    ctx.op("tensor", lambda e, c=c: e.matmul(
                        pg[:], xnT_sb[:, 128 * c:128 * (c + 1)],
                        wgb[:, NE * c:NE * (c + 1)],
                        start=(c == 0), stop=(c == DC - 1)),
                        waits=[("dmag", wg_done), ("v", ctx.val("v"))],
                        inc="t")
                tg = ctx.val("t")
                ctx.op("vector", lambda e: e.reduce_max(
                    gcol[:, 0:1], pg[:], axis=AX.X, negate=True),
                    waits=[("t", tg)], inc="v")
                ctx.op("scalar", lambda e: e.activation(
                    gat[:, 0:NE], pg[:], AF.Exp, bias=gcol[:, 0:1], scale=1.0),
                    waits=[("v", ctx.val("v"))], inc="s")
                ctx.op("vector", lambda e: e.reduce_max(
                    gcol[:, 1:2], gat[:, 0:NE], axis=AX.X),
                    waits=[("s", ctx.val("s"))], inc="v")
                ctx.op("vector", lambda e: e.tensor_scalar(
                    gat[:, NE:2 * NE], gat[:, 0:NE], gcol[:, 1:2], None,
                    op0=ALU.is_ge), waits=[("v", ctx.val("v"))], inc="v")
                ctx.op("vector", lambda e: e.tensor_scalar(
                    gat[:, 2 * NE:3 * NE], gat[:, NE:2 * NE], -1.0, 1.0,
                    op0=ALU.mult, op1=ALU.add), waits=[("v", ctx.val("v"))], inc="v")
                ctx.op("vector", lambda e: e.tensor_tensor(
                    gat[:, 3 * NE:4 * NE], gat[:, 0:NE], gat[:, 2 * NE:3 * NE],
                    op=ALU.mult), waits=[("v", ctx.val("v"))], inc="v")
                ctx.op("vector", lambda e: e.reduce_max(
                    gcol[:, 2:3], gat[:, 3 * NE:4 * NE], axis=AX.X),
                    waits=[("v", ctx.val("v"))], inc="v")
                ctx.op("vector", lambda e: e.tensor_scalar(
                    gat[:, 4 * NE:5 * NE], gat[:, 3 * NE:4 * NE], gcol[:, 2:3],
                    None, op0=ALU.is_ge), waits=[("v", ctx.val("v"))], inc="v")
                ctx.op("vector", lambda e: e.tensor_tensor(
                    gcol[:, 3:4], gcol[:, 1:2], gcol[:, 2:3], op=ALU.add),
                    waits=[("v", ctx.val("v"))], inc="v")
                ctx.op("vector", lambda e: e.reciprocal(gcol[:, 4:5], gcol[:, 3:4]),
                       waits=[("v", ctx.val("v"))], inc="v")
                ctx.op("vector", lambda e: e.tensor_tensor(
                    gat[:, 5 * NE:6 * NE], gat[:, NE:2 * NE], gat[:, 4 * NE:5 * NE],
                    op=ALU.add), waits=[("v", ctx.val("v"))], inc="v")
                ctx.op("vector", lambda e: e.tensor_tensor(
                    gat[:, 5 * NE:6 * NE], gat[:, 5 * NE:6 * NE], gat[:, 0:NE],
                    op=ALU.mult), waits=[("v", ctx.val("v"))], inc="v")
                ctx.op("vector", lambda e, b=b, t=t: e.tensor_scalar(
                    gat[:, 5 * NE:6 * NE], gat[:, 5 * NE:6 * NE], gcol[:, 4:5],
                    None, op0=ALU.mult), waits=[("v", ctx.val("v"))], inc="v")
                ctx.dma("sync", scores_o[b, :, t, :], gat[:, 5 * NE:6 * NE],
                        waits=[("v", ctx.val("v"))], inc="dma_out")

        ctx.flush(block)



DEBUG_L0ATT = False


def emit_l0att(nc, ctx, block, io):
    x0 = io["x0"]
    xnT = io["xnT"]
    wq, wk, wv, wo = io["l0wq"], io["l0wk"], io["l0wv"], io["l0wo"]
    biases = io["biases0"]
    idn = io["idn"]
    ones1 = io["ones1"]
    part1 = io["part1"]
    SCALE = 0.25

    with contextlib.ExitStack() as es:
        sb = lambda *a: es.enter_context(nc.sbuf_tensor(*a))
        ps = lambda *a: es.enter_context(nc.psum_tensor(*a))
        idb = sb("idb", [128, 128], BF16)
        wT = sb("wT", [128, DC * D], BF16)
        woT = wT
        wstrip = sb("wstrip", [128, D], BF16)
        onesr = sb("onesr", [1, 128], F32)
        bqk_cols = sb("bqk_cols", [128, 2 * DC], F32)
        bvb = sb("bvb", [128, D], F32)
        bob = sb("bob", [128, D], F32)
        xnT_sb = sb("xnT_sb", [128, DC * T], BF16)
        QT = sb("QT", [128, DC * T], BF16)
        KT = sb("KT", [128, DC * T], BF16)
        vaug = sb("vaug", [128, NT * NH * 65], BF16)
        p1t = sb("p1t", [128, 2 * NT * 512], BF16)
        ao = sb("ao", [128, 4 * D], BF16)
        aoT = sb("aoT", [128, DC * 128], BF16)
        xres = sb("xres", [128, D], F32)
        po = sb("po", [128, D], F32)
        brow = po[0:1, :]
        rz = sb("rz", [128, 4], F32)
        pT0 = ps("pT0", [128, 128], BF16)
        pT1 = ps("pT1", [128, 128], BF16)
        pqk4 = ps("pqk4", [128, 2048], F32)
        pav0 = ps("pav0", [128, 65], F32)
        pav1 = ps("pav1", [128, 65], F32)
        pT = [pT0, pT1]
        pbc = pqk4[:, 0:512]
        pav = [pav0, pav1]

        ctx.dma("gpsimd", idb[:], idn[:], inc="dmag")
        ctx.dma("sync", onesr[:], ones1[:], inc="dma")
        const_dma = ctx.val("dma")
        ctx.wait("tensor", "dma", const_dma)
        ctx.wait("tensor", "dmag", ctx.val("dmag"))
        ctx.wait("vector", "dma", const_dma)
        for i, tgt in enumerate(("bq", "bk", "bv", "bo")):
            ctx.dma("sync", brow[:], biases[:, i * D:(i + 1) * D],
                    waits=[("t_misc", ctx.val("t_misc"))], inc="dma")
            ctx.wait("tensor", "dma", ctx.val("dma"))
            if i < 2:
                emit_col_from_row(ctx, brow[:],
                                  bqk_cols[:, i * DC:(i + 1) * DC],
                                  onesr[:, 0:1], pbc[:, 0:1], D)
            else:
                emit_bcast_row(ctx, brow[:], bvb if i == 2 else bob,
                               onesr[:], pbc, D)

        for b in range(BPC):
            ctx.dma("sync", xnT_sb[:].rearrange("p (c q) -> p c q", q=T),
                    xnT[b].rearrange("c p q -> p c q"),
                    waits=[("t", ctx.val("t"))], inc="dma")
            xn_ld = ctx.val("dma")

            ctx.op("gpsimd", lambda e: e.memset(
                vaug[:].rearrange("p (k h d) -> p k h d", h=NH, d=65)[:, :, :, 64:65],
                1.0), waits=[("t", ctx.val("t"))], inc="g")
            ones_set = ctx.val("g")

            for wi, (w_hbm, is_v) in enumerate([(wq, False), (wk, False), (wv, True)]):
                ctx.wait("gpsimd", "t", ctx.val("t"))
                emit_weight_transpose(ctx, w_hbm, wT, wstrip, idb, pT, D, D)
                wt_done = ctx.val("v_wt")
                if not is_v:
                    dst = QT if wi == 0 else KT
                    bcol = bqk_cols[:, wi * DC:(wi + 1) * DC]
                    for m in range(DC):
                        for g in range(4):
                            for c in range(DC):
                                ctx.op("tensor", lambda e, m=m, g=g, c=c, dst=dst: e.matmul(
                                    pqk4[:, 512 * (g % 4):512 * (g % 4 + 1)],
                                    wT[:, c * D + 128 * m: c * D + 128 * (m + 1)],
                                    xnT_sb[:, c * T + 512 * g: c * T + 512 * (g + 1)],
                                    start=(c == 0), stop=(c == DC - 1)),
                                    waits=([("v_wt", wt_done), ("dma", xn_ld),
                                            ("s", ctx.val("s")), ("v", ctx.val("v"))]
                                           if c == 0 else []),
                                    inc="t")
                            ctx.op("scalar", lambda e, m=m, g=g, dst=dst, bcol=bcol: e.activation(
                                dst[:, m * T + 512 * g: m * T + 512 * (g + 1)],
                                pqk4[:, 512 * (g % 4):512 * (g % 4 + 1)],
                                AF.Identity, bias=bcol[:, m:m + 1],
                                scale=1.0),
                                waits=[("t", ctx.val("t")), ("v_misc", ctx.val("v_misc"))],
                                inc="s")
                else:
                    for t in range(NT):
                        for g2 in range(2):
                            for c in range(DC):
                                ctx.op("tensor", lambda e, t=t, g2=g2, c=c: e.matmul(
                                    pqk4[:, 512 * ((2 * t + g2) % 4):
                                        512 * ((2 * t + g2) % 4 + 1)],
                                    xnT_sb[:, c * T + 128 * t: c * T + 128 * (t + 1)],
                                    wT[:, c * D + 512 * g2: c * D + 512 * (g2 + 1)],
                                    start=(c == 0), stop=(c == DC - 1)),
                                    waits=([("v_wt", wt_done), ("dma", xn_ld),
                                            ("s", ctx.val("s")),
                                            ("v", ctx.val("v"))] if c == 0 else []),
                                    inc="t")
                            dst3 = vaug[:, 1040 * t + 520 * g2: 1040 * t + 520 * (g2 + 1)] \
                                .rearrange("p (h d) -> p h d", d=65)[:, :, 0:64]
                            ctx.op("vector", lambda e, t=t, g2=g2, dst3=dst3: e.scalar_tensor_tensor(
                                dst3,
                                pqk4[:, 512 * ((2 * t + g2) % 4):
                                    512 * ((2 * t + g2) % 4 + 1)], 1.0,
                                bvb[:, 512 * g2:512 * (g2 + 1)]
                                .rearrange("p (h d) -> p h d", d=64),
                                op0=ALU.mult, op1=ALU.add),
                                waits=[("t", ctx.val("t")), ("v_misc", ctx.val("v_misc"))],
                                inc="v")

            if DEBUG_L0ATT and b == 0:
                ctx.dma("sync", dQT[:], QT[:], waits=[("s", ctx.val("s"))], inc="dma_out")
                ctx.dma("sync", dKT[:], KT[:], waits=[("s", ctx.val("s"))], inc="dma_out")
                ctx.dma("sync", dV[:], vaug[:], waits=[("v", ctx.val("v")), ("g", ctx.val("g"))], inc="dma_out")

            ctx.wait("gpsimd", "t", ctx.val("t"))
            emit_weight_transpose(ctx, wo, woT, wstrip, idb, pT, D, D)
            wo_done = ctx.val("v_wt")

            for g in range(4):
                g_v = ctx.val("v")
                exp_ends = {}
                cmb_ends = {-1: ctx.val("v")}
                exp_prev = [ctx.val("s")]

                def sched_st(h, g=g, g_v=g_v, exp_ends=exp_ends, exp_prev=exp_prev):
                    p0 = 64 * (h % 2)
                    mo = (h // 2) * T
                    base_s = ctx.val("s")
                    p1 = p1t[:, (h % 2) * NT * 512:(h % 2 + 1) * NT * 512]
                    for c in range(NT):
                        if c == 0:
                            w = [("s", exp_prev[0]), ("v", g_v)]
                        elif c % 4 == 0:
                            w = [("s", base_s + c // 4)]
                        else:
                            w = []
                        ctx.op("tensor", lambda e, h=h, c=c, p0=p0, mo=mo: e.matmul(
                            pqk4[:, 512 * (c % 4):512 * (c % 4 + 1)],
                            KT[p0:p0 + 64, mo + 128 * c: mo + 128 * (c + 1)],
                            QT[p0:p0 + 64, mo + 512 * g: mo + 512 * (g + 1)],
                            start=True, stop=True),
                            waits=w, inc="t")
                        if c % 4 == 3:
                            ctx.op("scalar", lambda e, c=c, p1=p1: e.activation(
                                p1[:, 512 * (c - 3):512 * (c + 1)], pqk4[:],
                                AF.Exp, scale=SCALE),
                                waits=[("t", ctx.val("t"))], inc="s")
                    exp_ends[h] = ctx.val("s")
                    exp_prev[0] = exp_ends[h]

                def sched_av(h, exp_ends=exp_ends, cmb_ends=cmb_ends):
                    p1 = p1t[:, (h % 2) * NT * 512:(h % 2 + 1) * NT * 512]
                    bank_free = [cmb_ends[h - 1], cmb_ends[h - 1]]
                    for t4 in range(4):
                        for c in range(NT):
                            ctx.op("tensor", lambda e, t4=t4, c=c, h=h, p1=p1: e.matmul(
                                pav[t4 % 2][:],
                                p1[:, 512 * c + 128 * t4: 512 * c + 128 * (t4 + 1)],
                                vaug[:, 1040 * c + 65 * h: 1040 * c + 65 * (h + 1)],
                                start=(c == 0), stop=(c == NT - 1)),
                                waits=([("s", exp_ends[h]), ("g", ones_set),
                                        ("v", bank_free[t4 % 2])]
                                       if c == 0 else []),
                                inc="t")
                        ctx.op("vector", lambda e, t4=t4: e.reciprocal(
                            rz[:, t4:t4 + 1], pav[t4 % 2][:, 64:65]),
                            waits=[("t", ctx.val("t"))], inc="v")
                        ctx.op("vector", lambda e, t4=t4, h=h: e.tensor_scalar(
                            ao[:, 1024 * t4 + 64 * h: 1024 * t4 + 64 * (h + 1)],
                            pav[t4 % 2][:, 0:64], rz[:, t4:t4 + 1], None,
                            op0=ALU.mult),
                            waits=[("v", ctx.val("v"))], inc="v")
                        bank_free[t4 % 2] = ctx.val("v")
                    cmb_ends[h] = ctx.val("v")

                for h in range(NH):
                    sched_st(h)
                    if h > 0:
                        sched_av(h - 1)
                sched_av(NH - 1)
                ao_done = ctx.val("v")
                for t4 in range(4):
                    t = 4 * g + t4
                    ctx.dma("sync", xres[:], x0[b, 128 * t:128 * (t + 1), :],
                            waits=[("v", ctx.val("v"))], inc="dma")
                    x_ld = ctx.val("dma")
                    for c in range(DC):
                        ctx.op("tensor", lambda e, t4=t4, c=c: e.transpose(
                            pT[c % 2][:],
                            ao[:, 1024 * t4 + 128 * c: 1024 * t4 + 128 * (c + 1)],
                            idb[:]),
                            waits=[("v", ctx.val("v")), ("v_wt", ctx.val("v_wt"))],
                            inc="t")
                        ctx.op("vector", lambda e, c=c: e.tensor_copy(
                            aoT[:, 128 * c:128 * (c + 1)], pT[c % 2][:]),
                            waits=[("t", ctx.val("t"))], inc="v")
                    aoT_done = ctx.val("v")
                    for g2 in range(2):
                        for c in range(DC):
                            ctx.op("tensor", lambda e, g2=g2, c=c: e.matmul(
                                pqk4[:, 512 * g2:512 * (g2 + 1)],
                                aoT[:, 128 * c:128 * (c + 1)],
                                woT[:, c * D + 512 * g2: c * D + 512 * (g2 + 1)],
                                start=(c == 0), stop=(c == DC - 1)),
                                waits=([("v_wt", wo_done), ("v", aoT_done),
                                        ("s", ctx.val("s"))] if c == 0 else []),
                                inc="t")
                        ctx.op("vector", lambda e, g2=g2: e.tensor_tensor(
                            po[:, 512 * g2:512 * (g2 + 1)],
                            pqk4[:, 512 * g2:512 * (g2 + 1)],
                            bob[:, 512 * g2:512 * (g2 + 1)], op=ALU.add),
                            waits=[("t", ctx.val("t")), ("dma", x_ld),
                                   ("dma_out", ctx.val("dma_out"))],
                            inc="v")
                        ctx.op("vector", lambda e, g2=g2: e.tensor_tensor(
                            po[:, 512 * g2:512 * (g2 + 1)],
                            po[:, 512 * g2:512 * (g2 + 1)],
                            xres[:, 512 * g2:512 * (g2 + 1)], op=ALU.add),
                            inc="v")
                    ctx.dma("sync", part1[b, 128 * t:128 * (t + 1), :], po[:],
                            waits=[("v", ctx.val("v"))], inc="dma_out")

        ctx.flush(block)


def emit_l0moe(nc, ctx, block, io):
    xnT = io["xnT"]
    scores = io["scores"]
    part1 = io["part1"]
    W1, b1, W2, b2 = io["W1"], io["b1"], io["W2"], io["b2"]
    idn = io["idn"]
    ones1 = io["ones1"]
    x1 = io["x1"]

    HC = HMoE // 128

    with contextlib.ExitStack() as es:
        sb = lambda *a: es.enter_context(nc.sbuf_tensor(*a))
        ps = lambda *a: es.enter_context(nc.psum_tensor(*a))
        idb = sb("idb", [128, 128], BF16)
        onesr = sb("onesr", [1, 128], F32)
        acc = sb("acc", [128, NT * D], F32)
        xnT_sb = sb("xnT_sb", [128, DC * T], BF16)
        W1sb = sb("W1sb", [128, DC * HMoE], BF16)
        W2sb = sb("W2sb", [128, HC * D], BF16)
        hT = sb("hT", [128, HC * 512], BF16)
        scs = sb("scs", [128, NT * NE], F32)
        scb = sb("scb", [128, NE], BF16)
        scT = sb("scT", [8, NT * 128], BF16)
        b2sb = sb("b2sb", [8, D], BF16)
        b1row = sb("b1row", [1, HMoE], F32)
        b1c = sb("b1c", [128, NE * HC], F32)
        pT0 = ps("pT0", [128, 128], BF16)
        pTs = ps("pTs", [8, 128], BF16)
        ph0 = ps("ph0", [128, 512], F32)
        ph1 = ps("ph1", [128, 512], F32)
        py4 = ps("py4", [128, 2048], F32)
        pb1 = ph0[:, 0:1]
        ph = [ph0, ph1]
        py = [py4[:, 0:512], py4[:, 512:1024], py4[:, 1024:1536], py4[:, 1536:2048]]

        ctx.dma("gpsimd", idb[:], idn[:], inc="dmag")
        ctx.dma("sync", onesr[:], ones1[:], inc="dma")
        ctx.dma("gpsimd", b2sb[:], b2[:], inc="dmag")
        const_dma = ctx.val("dma")
        const_dmag = ctx.val("dmag")
        ctx.wait("tensor", "dma", const_dma)
        ctx.wait("tensor", "dmag", const_dmag)
        for e_i in range(NE):
            ctx.dma("sync", b1row[:], b1[e_i:e_i + 1, :],
                    waits=[("t_misc", ctx.val("t_misc"))], inc="dma")
            ctx.wait("tensor", "dma", ctx.val("dma"))
            emit_col_from_row(ctx, b1row[:], b1c[:, e_i * HC:(e_i + 1) * HC],
                              onesr[:, 0:1], pb1[:, 0:1], HMoE)
        b1_done = ctx.val("v_misc")

        for b in range(BPC):
            ctx.dma("sync", xnT_sb[:].rearrange("p (c q) -> p c q", q=T),
                    xnT[b].rearrange("c p q -> p c q"),
                    waits=[("t", ctx.val("t"))], inc="dma")
            ctx.dma("sync", scs[:].rearrange("p (t e) -> p t e", e=NE),
                    scores[b], waits=[("v", ctx.val("v"))], inc="dma")
            in_ld = ctx.val("dma")
            for t in range(NT):
                ctx.dma("sync", acc[:, D * t:D * (t + 1)],
                        part1[b, 128 * t:128 * (t + 1), :],
                        waits=[("dma_out", ctx.val("dma_out"))], inc="dma")
            acc_ld = ctx.val("dma")

            for t in range(NT):
                ctx.op("vector", lambda e, t=t: e.tensor_copy(
                    scb[:], scs[:, NE * t:NE * (t + 1)]),
                    waits=[("dma", in_ld), ("t", ctx.val("t"))], inc="v")
                ctx.op("tensor", lambda e, t=t: e.transpose(
                    pTs[:], scb[:], idb[:]),
                    waits=[("v", ctx.val("v")), ("dma", const_dma)], inc="t")
                ctx.op("vector", lambda e, t=t: e.tensor_copy(
                    scT[:, 128 * t:128 * (t + 1)], pTs[:]),
                    waits=[("t", ctx.val("t"))], inc="v")
            scT_done = ctx.val("v")
            for t in range(NT):
                for g2 in range(2):
                    ctx.op("tensor", lambda e, t=t, g2=g2: e.matmul(
                        py[(2 * t + g2) % 4], scT[:, 128 * t:128 * (t + 1)],
                        b2sb[:, 512 * g2:512 * (g2 + 1)],
                        start=True, stop=True),
                        waits=[("v", scT_done), ("dmag", const_dmag),
                               ("v", ctx.val("v"))], inc="t")
                    ctx.op("vector", lambda e, t=t, g2=g2: e.tensor_tensor(
                        acc[:, D * t + 512 * g2: D * t + 512 * (g2 + 1)],
                        acc[:, D * t + 512 * g2: D * t + 512 * (g2 + 1)],
                        py[(2 * t + g2) % 4], op=ALU.add),
                        waits=[("t", ctx.val("t")), ("dma", acc_ld)], inc="v")

            def load_w1(e_i, waits):
                ctx.dma("gpsimd", W1sb[:].rearrange("p (c h) -> p c h", h=HMoE),
                        W1[e_i].rearrange("(c p) h -> p c h", p=128),
                        waits=waits, inc="dma_w")
                return ctx.val("dma_w")

            def load_w2(e_i, waits):
                ctx.dma("gpsimd", W2sb[:].rearrange("p (j d) -> p j d", d=D),
                        W2[e_i].rearrange("(j p) d -> p j d", p=128),
                        waits=waits, inc="dma_w")
                return ctx.val("dma_w")

            w1_ld = load_w1(0, [("t", ctx.val("t"))])
            w2_ld = load_w2(0, [("t", ctx.val("t"))])
            for e_i in range(NE):
                w1_cur, w2_cur = w1_ld, w2_ld
                for g in range(4):
                    for j in range(HC):
                        for c in range(DC):
                            ctx.op("tensor", lambda e, j=j, c=c, g=g: e.matmul(
                                ph[j % 2][:],
                                W1sb[:, c * HMoE + 128 * j: c * HMoE + 128 * (j + 1)],
                                xnT_sb[:, c * T + 512 * g: c * T + 512 * (g + 1)],
                                start=(c == 0), stop=(c == DC - 1)),
                                waits=([("dma_w", w1_cur), ("dma", in_ld),
                                        ("s", ctx.val("s") - (1 if j >= 2 else 0))]
                                       if c == 0 else []),
                                inc="t")
                        ctx.op("scalar", lambda e, j=j, e_i=e_i: e.activation(
                            hT[:, 512 * j:512 * (j + 1)], ph[j % 2][:], AF.Relu,
                            bias=b1c[:, e_i * HC + j: e_i * HC + j + 1], scale=1.0),
                            waits=[("t", ctx.val("t")), ("v_misc", b1_done)],
                            inc="s")
                    if g == 3 and e_i < NE - 1:
                        w1_ld = load_w1(e_i + 1, [("t", ctx.val("t"))])
                    h_done = ctx.val("s")
                    if g == 0 and e_i == 0:
                        bank_free = [ctx.val("v")] * 4
                    for t4 in range(4):
                        t = 4 * g + t4
                        for g2 in range(2):
                            bk = (2 * t4 + g2) % 4
                            for j in range(HC):
                                ctx.op("tensor", lambda e, t4=t4, g2=g2, j=j, bk=bk: e.matmul(
                                    py[bk],
                                    hT[:, 512 * j + 128 * t4: 512 * j + 128 * (t4 + 1)],
                                    W2sb[:, j * D + 512 * g2: j * D + 512 * (g2 + 1)],
                                    start=(j == 0), stop=(j == HC - 1)),
                                    waits=([("s", h_done), ("dma_w", w2_cur),
                                            ("v", bank_free[bk])]
                                           if j == 0 else []),
                                    inc="t")
                            ctx.op("vector", lambda e, t=t, g2=g2, e_i=e_i, bk=bk: e.scalar_tensor_tensor(
                                acc[:, D * t + 512 * g2: D * t + 512 * (g2 + 1)],
                                py[bk],
                                scs[:, NE * t + e_i: NE * t + e_i + 1],
                                acc[:, D * t + 512 * g2: D * t + 512 * (g2 + 1)],
                                op0=ALU.mult, op1=ALU.add),
                                waits=[("t", ctx.val("t")), ("dma", in_ld)],
                                inc="v")
                            bank_free[bk] = ctx.val("v")
                if e_i < NE - 1:
                    w2_ld = load_w2(e_i + 1, [("t", ctx.val("t"))])

            for t in range(NT):
                ctx.dma("sync", x1[b, 128 * t:128 * (t + 1), :],
                        acc[:, D * t:D * (t + 1)],
                        waits=[("v", ctx.val("v"))], inc="dma_out")

        ctx.flush(block)


def emit_quant(ctx, xsrc, xd_bf, am, sc, rsc, u, t0, t1, ic, fb, width=D, blk=32):
    nb = width // blk
    V = "v"

    def vop(fn, waits=()):
        ctx.op("vector", fn, waits=waits, inc=V)

    x3 = xsrc.rearrange("p (b i) -> p b i", i=blk)
    vop(lambda e: e.tensor_reduce(am, x3, axis=AX.X, op=ALU.max,
                                  apply_absolute_value=True))
    vop(lambda e: e.tensor_scalar(sc, am, 1.0 / 7.5, 1e-30, op0=ALU.mult,
                                  op1=ALU.max), waits=[("v", ctx.val("v"))])
    vop(lambda e: e.reciprocal(rsc, sc), waits=[("v", ctx.val("v"))])
    u3 = u.rearrange("p (b i) -> p b i", i=blk)
    rb = rsc.rearrange("p b -> p b").unsqueeze(2).broadcast_to([128, nb, blk])
    vop(lambda e: e.tensor_tensor(u3, x3, rb, op=ALU.mult),
        waits=[("v", ctx.val("v"))])
    u_done = ctx.val("v")
    ctx.op("scalar", lambda e: e.activation(fb, u, AF.Abs),
           waits=[("v", u_done)], inc="s")
    a_done = ctx.val("s")
    vop(lambda e: e.tensor_scalar(t1, fb, 2.0, None, op0=ALU.is_ge),
        waits=[("s", a_done)])
    vop(lambda e: e.tensor_scalar(t1, t1, -4.0, 8.0, op0=ALU.mult, op1=ALU.add))
    vop(lambda e: e.tensor_scalar(fb, fb, 4.0, None, op0=ALU.is_ge))
    vop(lambda e: e.scalar_tensor_tensor(t1, fb, -2.0, t1, op0=ALU.mult,
                                         op1=ALU.add))
    vop(lambda e: e.tensor_tensor(t0, u, t1, op=ALU.mult))
    vop(lambda e: e.tensor_scalar(t0, t0, -0.5, None, op0=ALU.add))
    vop(lambda e: e.tensor_copy(ic, t0))
    vop(lambda e: e.tensor_copy(fb, ic))
    vop(lambda e: e.tensor_tensor(u, fb, t0, op=ALU.is_lt))
    vop(lambda e: e.tensor_tensor(fb, fb, u, op=ALU.add))
    vop(lambda e: e.reciprocal(t1, t1))
    vop(lambda e: e.tensor_tensor(fb, fb, t1, op=ALU.mult))
    f3 = fb.rearrange("p (b i) -> p b i", i=blk)
    sb3 = sc.rearrange("p b -> p b").unsqueeze(2).broadcast_to([128, nb, blk])
    xd3 = xd_bf.rearrange("p (b i) -> p b i", i=blk)
    vop(lambda e: e.tensor_tensor(xd3, f3, sb3, op=ALU.mult),
        waits=[("v", ctx.val("v"))])
    return ctx.val("v")


def emit_l1ln(nc, ctx, block, io):
    x1 = io["x1"]
    idn = io["idn"]
    xn1T = io["xn1T"]
    xdT = io["xdT"]
    wf = io["l1fw"]
    wdT_d = io["wdT_d"]
    W = 4

    with contextlib.ExitStack() as es:
        sb = lambda *a: es.enter_context(nc.sbuf_tensor(*a))
        ps = lambda *a: es.enter_context(nc.psum_tensor(*a))
        idb = sb("idb", [128, 128], BF16)
        xt = sb("xt", [128, W * D], F32)
        xc = sb("xc", [128, W * D], F32)
        sq = sb("sq", [128, D], F32)
        xnf = sb("xnf", [128, W * D], F32)
        xnb = sb("xnb", [128, W * D], BF16)
        xdb = sb("xdb", [128, W * D], BF16)
        stat = sb("stat", [128, W * 16], F32)
        am = sb("am", [128, W * 32], F32)
        scq = sb("scq", [128, W * 32], F32)
        rscq = sb("rscq", [128, 32], F32)
        qu = sb("qu", [128, W * D], F32)
        qt0 = sb("qt0", [128, D], F32)
        qt1 = sb("qt1", [128, D], F32)
        qic = sb("qic", [128, D], mybir.dt.int32)
        qfb = sb("qfb", [128, W * D], F32)
        trT = sb("trT", [128, D], BF16)
        wrow = sb("wrow", [128, D], F32)
        wdq = sb("wdq", [128, D], BF16)
        wu = sb("wu", [128, D], F32)
        wfb = sb("wfb", [128, D], F32)
        wam = sb("wam", [128, 32], F32)
        wsc = sb("wsc", [128, 32], F32)
        wrsc = sb("wrsc", [128, 32], F32)
        pT0 = ps("pT0", [128, 128], BF16)
        pT1 = ps("pT1", [128, 128], BF16)
        pT2 = ps("pT2", [128, 128], BF16)
        pT3 = ps("pT3", [128, 128], BF16)
        pT = [pT0, pT1, pT2, pT3]

        ctx.dma("gpsimd", idb[:], idn[:], inc="dmag")
        ctx.wait("tensor", "dmag", ctx.val("dmag"))
        strip_pre_v = 0
        strip_abs_s = 0
        strip_t = 0

        for b in range(BPC):
            for w in range(NT // W):
                ts0 = w * W
                prev_s = ctx.val("s")
                for i in range(W):
                    t = ts0 + i
                    ctx.dma("sync", xt[:, i * D:(i + 1) * D],
                            x1[b, 128 * t:128 * (t + 1), :],
                            waits=[("s", prev_s)] if (b or w) else [])
                x_ld = ctx.val("dma")
                r = b * (NT // W) + w
                ctx.dma("sync", wrow[:], wf[128 * r:128 * (r + 1), :],
                        waits=[("v", strip_pre_v), ("s", strip_abs_s)])
                wrow_ld = ctx.val("dma")
                for i in range(W):
                    ctx.op("vector", lambda e, i=i: e.reduce_sum(
                        stat[:, 16 * i:16 * i + 1], xt[:, i * D:(i + 1) * D],
                        axis=AX.X), waits=[("dma", x_ld)], inc="v")
                for i in range(W):
                    ctx.op("vector", lambda e, i=i: e.tensor_scalar(
                        stat[:, 16 * i + 1:16 * i + 2], stat[:, 16 * i:16 * i + 1],
                        -1.0 / D, None, op0=ALU.mult),
                        waits=[("v", ctx.val("v"))], inc="v")
                vst = ctx.val("v")
                for i in range(W):
                    ctx.op("scalar", lambda e, i=i: e.activation(
                        xc[:, i * D:(i + 1) * D], xt[:, i * D:(i + 1) * D],
                        AF.Identity, bias=stat[:, 16 * i + 1:16 * i + 2], scale=1.0),
                        waits=[("v", vst)], inc="s")
                    ctx.op("scalar", lambda e, i=i: e.activation(
                        sq[:], xc[:, i * D:(i + 1) * D], AF.Square,
                        accum_out=stat[:, 16 * i + 2:16 * i + 3]), inc="s")
                sst = ctx.val("s")
                for i in range(W):
                    ctx.op("vector", lambda e, i=i: e.tensor_scalar(
                        stat[:, 16 * i + 3:16 * i + 4], stat[:, 16 * i + 2:16 * i + 3],
                        1.0 / D, 1e-5, op0=ALU.mult, op1=ALU.add),
                        waits=[("s", sst), ("v", ctx.val("v"))], inc="v")
                    ctx.op("vector", lambda e, i=i: e.reciprocal(
                        stat[:, 16 * i + 4:16 * i + 5], stat[:, 16 * i + 3:16 * i + 4]),
                        waits=[("v", ctx.val("v"))], inc="v")
                vrv = ctx.val("v")
                for i in range(W):
                    ctx.op("scalar", lambda e, i=i: e.activation(
                        stat[:, 16 * i + 5:16 * i + 6], stat[:, 16 * i + 4:16 * i + 5],
                        AF.Sqrt), waits=[("v", vrv)], inc="s")
                srs = ctx.val("s")
                for i in range(W):
                    ctx.op("vector", lambda e, i=i: e.tensor_scalar(
                        xnf[:, i * D:(i + 1) * D], xc[:, i * D:(i + 1) * D],
                        stat[:, 16 * i + 5:16 * i + 6], None, op0=ALU.mult),
                        waits=[("s", srs), ("v", ctx.val("v"))], inc="v")
                    ctx.op("vector", lambda e, i=i: e.tensor_copy(
                        xnb[:, i * D:(i + 1) * D], xnf[:, i * D:(i + 1) * D]),
                        inc="v")
                xn_done = ctx.val("v")

                for i in range(W):
                    t = ts0 + i
                    ev = {}
                    d0 = ctx.val("dma_out")
                    for c in range(DC):
                        ctx.op("tensor", lambda e, i=i, c=c: e.transpose(
                            pT[c % 4][:], xnb[:, i * D + 128 * c: i * D + 128 * (c + 1)],
                            idb[:]),
                            waits=([("v", xn_done), ("dma_out", d0)] if c < 4 else
                                   [("v", ev[c - 4])]),
                            inc="t")
                        ctx.op("vector", lambda e, c=c: e.tensor_copy(
                            trT[:, 128 * c:128 * (c + 1)], pT[c % 4][:]),
                            waits=[("t", ctx.val("t"))], inc="v")
                        ev[c] = ctx.val("v")
                    ctx.dma("sync", xn1T[b, :, :, 128 * t:128 * (t + 1)]
                            .rearrange("c p q -> p c q"),
                            trT[:].rearrange("p (c q) -> p c q", q=128),
                            waits=[("v", ctx.val("v"))], inc="dma_out")

                for i in range(W):
                    x3 = xnf[:, i * D:(i + 1) * D].rearrange("p (b i) -> p b i", i=32)
                    ctx.op("vector", lambda e, i=i, x3=x3: e.tensor_reduce(
                        am[:, 32 * i:32 * (i + 1)], x3, axis=AX.X, op=ALU.max,
                        apply_absolute_value=True),
                        waits=[("v", ctx.val("v"))], inc="v")
                    ctx.op("vector", lambda e, i=i: e.tensor_scalar(
                        scq[:, 32 * i:32 * (i + 1)], am[:, 32 * i:32 * (i + 1)],
                        1.0 / 7.5, 1e-30, op0=ALU.mult, op1=ALU.max),
                        waits=[("v", ctx.val("v"))], inc="v")
                    ctx.op("vector", lambda e, i=i: e.reciprocal(
                        rscq[:], scq[:, 32 * i:32 * (i + 1)]),
                        waits=[("v", ctx.val("v"))], inc="v")
                    ctx.op("vector", lambda e, i=i, x3=x3: e.tensor_tensor(
                        qu[:, i * D:(i + 1) * D].rearrange("p (b i) -> p b i", i=32),
                        x3, rscq[:].unsqueeze(2).broadcast_to([128, 32, 32]),
                        op=ALU.mult), waits=[("v", ctx.val("v"))], inc="v")
                u_done = ctx.val("v")
                for i in range(W):
                    ctx.op("scalar", lambda e, i=i: e.activation(
                        qfb[:, i * D:(i + 1) * D], qu[:, i * D:(i + 1) * D], AF.Abs),
                        waits=[("v", u_done)], inc="s")
                a_done = ctx.val("s")
                for i in range(W):
                    fbs = qfb[:, i * D:(i + 1) * D]
                    us = qu[:, i * D:(i + 1) * D]
                    ctx.op("vector", lambda e, fbs=fbs: e.tensor_scalar(
                        qt1[:], fbs, 2.0, None, op0=ALU.is_ge),
                        waits=[("s", a_done), ("v", ctx.val("v"))], inc="v")
                    ctx.op("vector", lambda e: e.tensor_scalar(
                        qt1[:], qt1[:], -4.0, 8.0, op0=ALU.mult, op1=ALU.add), inc="v")
                    ctx.op("vector", lambda e, fbs=fbs: e.tensor_scalar(
                        fbs, fbs, 4.0, None, op0=ALU.is_ge), inc="v")
                    ctx.op("vector", lambda e, fbs=fbs: e.scalar_tensor_tensor(
                        qt1[:], fbs, -2.0, qt1[:], op0=ALU.mult, op1=ALU.add), inc="v")
                    ctx.op("vector", lambda e, us=us: e.tensor_tensor(
                        qt0[:], us, qt1[:], op=ALU.mult), inc="v")
                    ctx.op("vector", lambda e: e.tensor_scalar(
                        qt0[:], qt0[:], -0.5, None, op0=ALU.add), inc="v")
                    ctx.op("vector", lambda e: e.tensor_copy(qic[:], qt0[:]), inc="v")
                    ctx.op("vector", lambda e, fbs=fbs: e.tensor_copy(fbs, qic[:]), inc="v")
                    ctx.op("vector", lambda e, us=us, fbs=fbs: e.tensor_tensor(
                        us, fbs, qt0[:], op=ALU.is_lt), inc="v")
                    ctx.op("vector", lambda e, us=us, fbs=fbs: e.tensor_tensor(
                        fbs, fbs, us, op=ALU.add), inc="v")
                    ctx.op("vector", lambda e: e.reciprocal(qt1[:], qt1[:]), inc="v")
                    ctx.op("vector", lambda e, fbs=fbs: e.tensor_tensor(
                        fbs, fbs, qt1[:], op=ALU.mult), inc="v")
                    ctx.op("vector", lambda e, i=i, fbs=fbs: e.tensor_tensor(
                        xdb[:, i * D:(i + 1) * D].rearrange("p (b i) -> p b i", i=32),
                        fbs.rearrange("p (b i) -> p b i", i=32),
                        scq[:, 32 * i:32 * (i + 1)].unsqueeze(2)
                        .broadcast_to([128, 32, 32]),
                        op=ALU.mult), waits=[("v", ctx.val("v"))], inc="v")

                for i in range(W):
                    t = ts0 + i
                    ev = {}
                    d0 = ctx.val("dma_out")
                    w0 = ctx.val("v")
                    for c in range(DC):
                        ctx.op("tensor", lambda e, i=i, c=c: e.transpose(
                            pT[c % 4][:], xdb[:, i * D + 128 * c: i * D + 128 * (c + 1)],
                            idb[:]),
                            waits=([("v", w0), ("dma_out", d0)] if c < 4 else
                                   [("v", ev[c - 4])]),
                            inc="t")
                        ctx.op("vector", lambda e, c=c: e.tensor_copy(
                            trT[:, 128 * c:128 * (c + 1)], pT[c % 4][:]),
                            waits=[("t", ctx.val("t"))], inc="v")
                        ev[c] = ctx.val("v")
                    ctx.dma("sync", xdT[b, :, :, 128 * t:128 * (t + 1)]
                            .rearrange("c p q -> p c q"),
                            trT[:].rearrange("p (c q) -> p c q", q=128),
                            waits=[("v", ctx.val("v"))], inc="dma_out")

                ctx.wait("vector", "dma", wrow_ld)
                ctx.wait("vector", "t", strip_t)
                ctx.wait("vector", "s", strip_abs_s)
                strip_pre_v = emit_quant(ctx, wrow[:], wdq[:], wam[:], wsc[:],
                                         wrsc[:], wu[:], qt0[:], qt1[:],
                                         qic[:], wfb[:])
                strip_abs_s = ctx.val("s")
                ev = {}
                d0 = ctx.val("dma_out")
                for c in range(DC):
                    ctx.op("tensor", lambda e, c=c: e.transpose(
                        pT[c % 4][:], wdq[:, 128 * c:128 * (c + 1)], idb[:]),
                        waits=([("v", strip_pre_v), ("dma_out", d0)] if c < 4
                               else [("v", ev[c - 4])]), inc="t")
                    ctx.op("vector", lambda e, c=c: e.tensor_copy(
                        trT[:, 128 * c:128 * (c + 1)], pT[c % 4][:]),
                        waits=[("t", ctx.val("t"))], inc="v")
                    ev[c] = ctx.val("v")
                strip_t = ctx.val("t")
                ctx.dma("sync", wdT_d[:, :, 128 * r:128 * (r + 1)]
                        .rearrange("c p q -> p c q"),
                        trT[:].rearrange("p (c q) -> p c q", q=128),
                        waits=[("v", ctx.val("v"))], inc="dma_out")

        ctx.flush(block)



def emit_l1att(nc, ctx, block, io):
    nb = L1DBG.get("nb", BPC)
    qs = L1DBG.get("qs", 99)
    nt = L1DBG.get("nt", NT)
    nh = L1DBG.get("nh", NH)
    x1 = io["x1"]
    xn1T = io["xn1T"]
    wq, wk, wv, wo = io["l1wq"], io["l1wk"], io["l1wv"], io["l1wo"]
    biases = io["biases1"]
    tkbig = io["tkbig"]
    tvbig = io["tvbig"]
    tv0r = io["tv0r"]
    tv8r = io["tv8r"]
    emsk = io["emsk"]
    zpad = io["zpad"]
    idn = io["idn"]
    ones1 = io["ones1"]
    part2 = io["part2"]
    ktok_d = io["ktok_d"]
    vtok_d = io["vtok_d"]

    SC8 = 0.125

    with contextlib.ExitStack() as es:
        sb = lambda *a: es.enter_context(nc.sbuf_tensor(*a))
        ps = lambda *a: es.enter_context(nc.psum_tensor(*a))
        idb = sb("idb", [128, 128], BF16)
        onesr = sb("onesr", [1, 128], F32)
        bqk_cols = sb("bqk_cols", [128, 2 * DC], F32)
        bvob = sb("bvob", [128, 2 * D], F32)
        tkTs = sb("tkTs", [128, DC * NH * 9], BF16)
        tvbigs = sb("tvbigs", [128, D], BF16)
        tv0b = sb("tv0b", [128, D], BF16)
        tv8b = sb("tv8b", [128, D], BF16)
        emsks = sb("emsks", [128, 16], BF16)
        w_sb = sb("w_sb", [128, DC * D], BF16)
        xnT_sb = sb("xnT_sb", [128, DC * T], BF16)
        QT = sb("QT", [128, DC * T], BF16)
        KT = sb("KT", [128, DC * T], BF16)
        vaug = sb("vaug", [128, NT * NH * 65], BF16)
        qtok2 = sb("qtok2", [128, 2 * D], BF16)
        win = xnT_sb[:, 0:7 * D]
        p1t = [xnT_sb[:, 7 * D:7 * D + NT * 256],
               xnT_sb[:, 7 * D + NT * 256:7 * D + 2 * NT * 256]]
        prod = sb("prod", [128, D], BF16)
        accb2 = sb("accb2", [128, 2 * D], BF16)
        bd2 = sb("bd2", [128, 256], F32)
        warg = sb("warg", [128, 128], F32)
        w8_2 = sb("w8_2", [128, 256], BF16)
        p1d8 = sb("p1d8", [128, 128], BF16)
        w8T = sb("w8T", [128, 128], BF16)
        qdT2 = sb("qdT2", [128, 288], F32)
        Acol2 = sb("Acol2", [128, 32], F32)
        Bcol2 = sb("Bcol2", [128, 32], F32)
        Ab = sb("Ab", [128, 16], BF16)
        tg8_2 = sb("tg8_2", [128, 256], BF16)
        Ssum2 = sb("Ssum2", [128, 128], F32)
        Fsb2 = sb("Fsb2", [128, 2 * NH * 65], F32)
        Usb2 = sb("Usb2", [128, 2 * NH * 65], F32)
        msk1 = sb("msk1", [128, 128], BF16)
        msk2 = sb("msk2", [128, 128], BF16)
        numb = sb("numb", [128, D], BF16)
        tb = sb("tb", [128, D], BF16)
        tvb_sb = sb("tvb_sb", [128, D], BF16)
        den = sb("den", [128, 16], F32)
        rden = sb("rden", [128, 16], F32)
        ao = sb("ao", [128, D], BF16)
        aoT = sb("aoT", [128, D], BF16)
        po = sb("po", [128, D], F32)
        t0f = po
        brow = po[0:1, :]
        xres = Fsb2
        pT0 = ps("pT0", [128, 128], BF16)
        pT1 = ps("pT1", [128, 128], BF16)
        pst = ps("pst", [128, 2048], F32)
        pFU = ps("pFU", [128, 512], F32)
        pF = pFU[:, 0:130]
        pU = pFU[:, 256:386]
        pOP = ps("pOP", [128, 512], F32)
        pqd = pst[:, 1024:1536]
        pT = [pT0, pT1]

        ctx.dma("gpsimd", idb[:], idn[:], inc="dmag")
        ctx.dma("sync", onesr[:], ones1[:], inc="dma")
        ctx.dma("gpsimd", tkTs[:].rearrange("p (c e) -> p c e", e=NH * 9),
                tkbig.rearrange("(c p) e -> p c e", p=128), inc="dmag")
        ctx.dma("gpsimd", tvbigs[:], tvbig[:], inc="dmag")
        ctx.dma("gpsimd", emsks[:].rearrange("p (i o) -> p i o", o=8),
                emsk.rearrange("i p o -> p i o"), inc="dmag")
        for dst in (ktok_d, vtok_d):
            ctx.dma("gpsimd", dst[0:3, :], zpad[0:3, :], inc="dmag")
            ctx.dma("gpsimd", dst[T + 3:T + 6, :], zpad[0:3, :], inc="dmag")
        const_ld = ctx.val("dma")
        ctx.wait("tensor", "dma", const_ld)
        ctx.wait("tensor", "dmag", ctx.val("dmag"))
        ctx.wait("vector", "dma", const_ld)
        ctx.wait("vector", "dmag", ctx.val("dmag"))
        for row, dstb in ((tv0r, tv0b), (tv8r, tv8b)):
            ctx.dma("sync", brow, row[:],
                    waits=[("t_misc", ctx.val("t_misc"))], inc="dma")
            ctx.wait("tensor", "dma", ctx.val("dma"))
            for gg in range(2):
                ctx.op("tensor", lambda e, gg=gg: e.matmul(
                    pqd[:, 0:512], onesr[:], brow[:, 512 * gg:512 * (gg + 1)],
                    start=True, stop=True),
                    waits=[("v_misc", ctx.val("v_misc"))], inc="t_misc")
                ctx.op("vector", lambda e, gg=gg, dstb=dstb: e.tensor_copy(
                    dstb[:, 512 * gg:512 * (gg + 1)], pqd[:]),
                    waits=[("t_misc", ctx.val("t_misc"))], inc="v_misc")
        for i in range(4):
            ctx.dma("sync", brow, biases[:, i * D:(i + 1) * D],
                    waits=[("t_misc", ctx.val("t_misc"))], inc="dma")
            ctx.wait("tensor", "dma", ctx.val("dma"))
            if i < 2:
                emit_col_from_row(ctx, brow, bqk_cols[:, i * DC:(i + 1) * DC],
                                  onesr[:, 0:1], pqd[:, 0:1], D)
            else:
                emit_bcast_row(ctx, brow, bvob[:, (i - 2) * D:(i - 1) * D],
                               onesr[:], pqd, D)

        for b in range(nb):
            ctx.dma("sync", xnT_sb[:].rearrange("p (c q) -> p c q", q=T),
                    xn1T[b].rearrange("c p q -> p c q"),
                    waits=[("t", ctx.val("t"))], inc="dma")
            xn_ld = ctx.val("dma")
            ctx.op("gpsimd", lambda e: e.memset(
                vaug[:].rearrange("p (k h d) -> p k h d", h=NH, d=65)[:, :, :, 64:65],
                1.0), waits=[("t", ctx.val("t"))], inc="g")
            ones_set = ctx.val("g")

            for wi, w_hbm in enumerate([wq, wk, wv, wo]):
                ctx.dma("gpsimd", w_sb[:].rearrange("p (c e) -> p c e", e=D),
                        w_hbm.rearrange("(c p) e -> p c e", p=128),
                        waits=[("t", ctx.val("t"))], inc="dma_w")
                w_ld = ctx.val("dma_w")
                if wi < 2:
                    dst = QT if wi == 0 else KT
                    bcol = bqk_cols[:, wi * DC:(wi + 1) * DC]
                    for m in range(DC):
                        for g in range(4):
                            for c in range(DC):
                                ctx.op("tensor", lambda e, m=m, g=g, c=c: e.matmul(
                                    pst[:, 512 * (g % 4):512 * (g % 4 + 1)],
                                    w_sb[:, c * D + 128 * m: c * D + 128 * (m + 1)],
                                    xnT_sb[:, c * T + 512 * g: c * T + 512 * (g + 1)],
                                    start=(c == 0), stop=(c == DC - 1)),
                                    waits=([("dma_w", w_ld), ("dma", xn_ld),
                                            ("s", ctx.val("s")), ("v", ctx.val("v"))]
                                           if c == 0 else []),
                                    inc="t")
                            ctx.op("scalar", lambda e, m=m, g=g, dst=dst, bcol=bcol: e.activation(
                                dst[:, m * T + 512 * g: m * T + 512 * (g + 1)],
                                pst[:, 512 * (g % 4):512 * (g % 4 + 1)],
                                AF.Identity, bias=bcol[:, m:m + 1], scale=1.0),
                                waits=[("t", ctx.val("t")),
                                       ("v_misc", ctx.val("v_misc"))], inc="s")
                elif wi == 2:
                    for t in range(NT):
                        for g2 in range(2):
                            for c in range(DC):
                                ctx.op("tensor", lambda e, t=t, g2=g2, c=c: e.matmul(
                                    pst[:, 512 * ((2 * t + g2) % 4):
                                        512 * ((2 * t + g2) % 4 + 1)],
                                    xnT_sb[:, c * T + 128 * t: c * T + 128 * (t + 1)],
                                    w_sb[:, c * D + 512 * g2: c * D + 512 * (g2 + 1)],
                                    start=(c == 0), stop=(c == DC - 1)),
                                    waits=([("dma_w", w_ld), ("dma", xn_ld),
                                            ("s", ctx.val("s")),
                                            ("v", ctx.val("v"))] if c == 0 else []),
                                    inc="t")
                            dst3 = vaug[:, 1040 * t + 520 * g2: 1040 * t + 520 * (g2 + 1)] \
                                .rearrange("p (h d) -> p h d", d=65)[:, :, 0:64]
                            ctx.op("vector", lambda e, t=t, g2=g2, dst3=dst3: e.scalar_tensor_tensor(
                                dst3,
                                pst[:, 512 * ((2 * t + g2) % 4):
                                    512 * ((2 * t + g2) % 4 + 1)], 1.0,
                                bvob[:, 512 * g2:512 * (g2 + 1)]
                                .rearrange("p (h d) -> p h d", d=64),
                                op0=ALU.mult, op1=ALU.add),
                                waits=[("t", ctx.val("t")),
                                       ("v_misc", ctx.val("v_misc"))], inc="v")
                    v_done = ctx.val("v")
                    for cc in range(NT):
                        ctx.dma("sync",
                                vtok_d[3 + 128 * cc:3 + 128 * (cc + 1), :]
                                .rearrange("p (h d) -> p h d", d=64),
                                vaug[:, 1040 * cc:1040 * (cc + 1)]
                                .rearrange("p (h d) -> p h d", d=65)[:, :, 0:64],
                                waits=[("v", v_done),
                                       ("dma_win", ctx.val("dma_win"))],
                                inc="dma_out")
                else:
                    wo_ld = ctx.val("dma_w")

            for t in range(NT):
                ev = {}
                w0 = ctx.val("v")
                s0 = ctx.val("s")
                for c in range(DC):
                    ctx.op("tensor", lambda e, t=t, c=c: e.transpose(
                        pT[c % 2][:], KT[:, c * T + 128 * t: c * T + 128 * (t + 1)],
                        idb[:]),
                        waits=([("v", w0), ("s", s0)] if c < 2 else
                               [("v", ev[c - 2])]),
                        inc="t")
                    ctx.op("vector", lambda e, c=c: e.tensor_copy(
                        qtok2[:, 128 * c:128 * (c + 1)], pT[c % 2][:]),
                        waits=[("t", ctx.val("t"))], inc="v")
                    ev[c] = ctx.val("v")
                ctx.dma("sync", ktok_d[3 + 128 * t:3 + 128 * (t + 1), :],
                        qtok2[:, 0:D],
                        waits=[("v", ctx.val("v")), ("dma_win", ctx.val("dma_win"))],
                        inc="dma_out")
                ctx.wait("tensor", "dma_out", ctx.val("dma_out"))
            ktok_done = ctx.val("dma_out")

            exp_prev = [ctx.val("s")]
            tvb_free = [ctx.val("v")]
            pT_free = [ctx.val("v")]
            qtok_free = [ctx.val("v")]
            for gq in range(nt // 2):
                tp0 = 2 * gq
                ev_s = {}
                pi = 0
                for t4 in range(2):
                    t = tp0 + t4
                    w0 = max(pT_free[0], qtok_free[0])
                    g0 = ctx.val("g")
                    for c in range(DC):
                        ctx.op("tensor", lambda e, t=t, c=c: e.transpose(
                            pT[c % 2][:], QT[:, c * T + 128 * t: c * T + 128 * (t + 1)],
                            idb[:]),
                            waits=([("v", w0), ("g", g0)] if pi < 2 else
                                   [("s", ev_s[pi - 2])]),
                            inc="t")
                        ctx.op("scalar", lambda e, c=c, t4=t4: e.copy(
                            qtok2[:, D * t4 + 128 * c: D * t4 + 128 * (c + 1)],
                            pT[c % 2][:]),
                            waits=[("t", ctx.val("t"))], inc="s")
                        ev_s[pi] = ctx.val("s")
                        pi += 1
                    for c in range(DC):
                        ctx.op("tensor", lambda e, c=c, t=t: e.matmul(
                            pqd[:, 0:144],
                            QT[:, c * T + 128 * t: c * T + 128 * (t + 1)],
                            tkTs[:, 144 * c:144 * (c + 1)],
                            start=(c == 0), stop=(c == DC - 1)),
                            waits=([("v", tvb_free[0])] if c == 0 else []),
                            inc="t")
                    ctx.op("scalar", lambda e, t4=t4: e.copy(
                        qdT2[:, 144 * t4:144 * (t4 + 1)], pqd[:, 0:144]),
                        waits=[("t", ctx.val("t"))], inc="s")
                    ctx.op("scalar", lambda e, t4=t4: e.activation(
                        Acol2[:, 16 * t4:16 * (t4 + 1)],
                        qdT2[:, 144 * t4:144 * (t4 + 1)]
                        .rearrange("p (h j) -> p h j", j=9)[:, :, 0:1].squeeze(2),
                        AF.Exp, scale=SC8), inc="s")
                    ctx.op("scalar", lambda e, t4=t4: e.activation(
                        Bcol2[:, 16 * t4:16 * (t4 + 1)],
                        qdT2[:, 144 * t4:144 * (t4 + 1)]
                        .rearrange("p (h j) -> p h j", j=9)[:, :, 8:9].squeeze(2),
                        AF.Exp, scale=SC8), inc="s")
                qtok_done_s = ctx.val("s")
                AB_done = ctx.val("s")
                pair_v = tvb_free[0]
                pair_g = ctx.val("g")
                exp_prev[0] = max(exp_prev[0], ctx.val("s"))

                bd_done = [0, 0]
                for t4 in range(2):
                    t = tp0 + t4
                    ksrc = bass.AP(ktok_d.tensor, (128 * t) * D, [[D, 128], [D, 7], [1, D]])
                    ctx.dma("gpsimd", win.rearrange("p (o d) -> p o d", d=D), ksrc,
                            waits=[("dma_out", ktok_done), ("v", ctx.val("v"))],
                            inc="dma_win")
                    kwin_ld = ctx.val("dma_win")
                    ctx.op("vector", lambda e, t4=t4: e.memset(
                        bd2[:, 128 * t4:128 * (t4 + 1)], -1.0e30),
                        waits=[("s", ctx.val("s"))], inc="v")
                    for o in range(7):
                        ctx.op("vector", lambda e, o=o, t4=t4: e.tensor_tensor(
                            prod[:], qtok2[:, D * t4:D * (t4 + 1)],
                            win[:, D * o:D * (o + 1)], op=ALU.mult),
                            waits=[("dma_win", kwin_ld), ("s", qtok_done_s),
                                   ("v", ctx.val("v"))], inc="v")
                        ctx.op("vector", lambda e, o=o, t4=t4: e.tensor_reduce(
                            bd2[:, 128 * t4:128 * (t4 + 1)]
                            .rearrange("p (h o) -> p h o", o=8)[:, :, o:o + 1].squeeze(2),
                            prod[:].rearrange("p (h d) -> p h d", d=64),
                            axis=AX.X, op=ALU.add),
                            waits=[("v", ctx.val("v"))], inc="v")
                    bd_done[t4] = ctx.val("v")
                    qtok_free[0] = ctx.val("v")

                exp_ends = {}
                evac_end = {-1: 0}
                fu_t_end = {-1: 0, -2: 0}
                g8_done = [0, 0]
                accb_done = [0, 0]

                def sched_st(h):
                    p0 = 64 * (h % 2)
                    mo = (h // 2) * T
                    base_s = ctx.val("s")
                    p1 = p1t[h % 2]
                    for c in range(NT):
                        if c == 0:
                            w = [("s", exp_prev[0]), ("g", pair_g), ("v", pair_v)]
                        elif c % 4 == 0:
                            w = [("s", base_s + c // 4)]
                        else:
                            w = []
                        ctx.op("tensor", lambda e, h=h, c=c, p0=p0, mo=mo: e.matmul(
                            pst[:, 512 * (c % 4):512 * (c % 4) + 256],
                            KT[p0:p0 + 64, mo + 128 * c: mo + 128 * (c + 1)],
                            QT[p0:p0 + 64, mo + 256 * gq: mo + 256 * (gq + 1)],
                            start=True, stop=True),
                            waits=w, inc="t")
                        if c % 4 == 3:
                            ctx.op("scalar", lambda e, c=c, p1=p1: e.activation(
                                p1[:, 256 * (c - 3):256 * (c + 1)],
                                pst[:].rearrange("p (i x) -> p i x", x=512)
                                [:, :, 0:256],
                                AF.Exp, scale=SC8),
                                waits=[("t", ctx.val("t"))], inc="s")
                    exp_ends[h] = ctx.val("s")
                    exp_prev[0] = exp_ends[h]

                def sched_fu(h):
                    p1 = p1t[h % 2]
                    swait = max(exp_ends[h], evac_end[h - 1])
                    for t4 in range(2):
                        t = tp0 + t4
                        ctx.op("gpsimd", lambda e, t=t, t4=t4, p1=p1: e.affine_select(
                            msk1[:], p1[:, 256 * t + 128 * t4: 256 * t + 128 * (t4 + 1)],
                            pattern=[[-1, 128]], compare_op=ALU.is_ge, fill=0.0,
                            base=-4, channel_multiplier=1),
                            waits=[("s", exp_ends[h]),
                                   ("t", fu_t_end[h - 1] if t4 == 0 else ctx.val("t"))],
                            inc="g")
                        if t < NT - 1:
                            ctx.op("gpsimd", lambda e, t=t, t4=t4, p1=p1: e.affine_select(
                                msk2[:], p1[:, 256 * (t + 1) + 128 * t4:
                                            256 * (t + 1) + 128 * (t4 + 1)],
                                pattern=[[-1, 128]], compare_op=ALU.is_ge, fill=0.0,
                                base=124, channel_multiplier=1), inc="g")
                        msk_done = ctx.val("g")
                        for c in range(NT):
                            ctx.op("tensor", lambda e, c=c, h=h, t4=t4, p1=p1: e.matmul(
                                pF[:, 65 * t4:65 * (t4 + 1)],
                                p1[:, 256 * c + 128 * t4: 256 * c + 128 * (t4 + 1)],
                                vaug[:, 1040 * c + 65 * h: 1040 * c + 65 * (h + 1)],
                                start=(c == 0), stop=(c == NT - 1)),
                                waits=([("s", swait), ("g", pair_g)]
                                       if c == 0 else []),
                                inc="t")
                        nU = (1 if t == NT - 1 else 2) + max(0, NT - (t + 2))
                        iU = 0
                        ctx.op("tensor", lambda e, h=h, t=t, t4=t4, nU=nU: e.matmul(
                            pU[:, 65 * t4:65 * (t4 + 1)], msk1[:],
                            vaug[:, 1040 * t + 65 * h: 1040 * t + 65 * (h + 1)],
                            start=True, stop=(nU == 1)),
                            waits=[("g", msk_done), ("s", swait)], inc="t")
                        iU += 1
                        if t < NT - 1:
                            ctx.op("tensor", lambda e, h=h, t=t, t4=t4, nU=nU, iU=iU: e.matmul(
                                pU[:, 65 * t4:65 * (t4 + 1)], msk2[:],
                                vaug[:, 1040 * (t + 1) + 65 * h: 1040 * (t + 1) + 65 * (h + 1)],
                                start=False, stop=(iU == nU - 1)), inc="t")
                            iU += 1
                        for c in range(t + 2, NT):
                            ctx.op("tensor", lambda e, c=c, h=h, t4=t4, iU=iU, nU=nU, p1=p1: e.matmul(
                                pU[:, 65 * t4:65 * (t4 + 1)],
                                p1[:, 256 * c + 128 * t4: 256 * c + 128 * (t4 + 1)],
                                vaug[:, 1040 * c + 65 * h: 1040 * c + 65 * (h + 1)],
                                start=False, stop=(iU == nU - 1)), inc="t")
                            iU += 1
                    fu_t_end[h] = ctx.val("t")
                    ctx.op("scalar", lambda e, h=h: e.copy(
                        Fsb2[:].rearrange("p (q x) -> p q x", x=1040)[:, :, 65 * h:65 * (h + 1)],
                        pF.rearrange("p (q x) -> p q x", x=65)),
                        waits=[("t", ctx.val("t"))], inc="s")
                    ctx.op("scalar", lambda e, h=h: e.copy(
                        Usb2[:].rearrange("p (q x) -> p q x", x=1040)[:, :, 65 * h:65 * (h + 1)],
                        pU.rearrange("p (q x) -> p q x", x=65)), inc="s")
                    evac_end[h] = ctx.val("s")

                def sched_band2(t4):
                    t = tp0 + t4
                    bslice = bd2[:, 128 * t4:128 * (t4 + 1)]
                    qdT = qdT2[:, 144 * t4:144 * (t4 + 1)]
                    ctx.op("vector", lambda e: e.memset(warg[:], -1.0e30),
                           waits=[("s", ctx.val("s")), ("v", bd_done[t4])], inc="v")
                    ctx.op("vector", lambda e, bslice=bslice, qdT=qdT: e.tensor_tensor(
                        warg[:].rearrange("p (h o) -> p h o", o=8)[:, :, 0:7],
                        bslice.rearrange("p (h o) -> p h o", o=8)[:, :, 0:7],
                        qdT.rearrange("p (h j) -> p h j", j=9)[:, :, 1:8],
                        op=ALU.add), waits=[("v", ctx.val("v"))], inc="v")
                    warg_done = ctx.val("v")
                    ctx.op("scalar", lambda e, t4=t4: e.activation(
                        w8_2[:, 128 * t4:128 * (t4 + 1)], warg[:], AF.Exp, scale=SC8),
                        waits=[("v", warg_done)], inc="s")
                    ctx.op("scalar", lambda e, t4=t4, bslice=bslice: e.activation(
                        p1d8[:], bslice, AF.Exp, scale=SC8), inc="s")
                    exp8_done = ctx.val("s")
                    if t == 0 or t == NT - 1:
                        ei = 0 if t == 0 else 1
                        for buf in (w8_2[:, 128 * t4:128 * (t4 + 1)], p1d8[:]):
                            ctx.op("vector", lambda e, buf=buf, ei=ei: e.tensor_tensor(
                                buf.rearrange("p (h o) -> p h o", o=8),
                                buf.rearrange("p (h o) -> p h o", o=8),
                                emsks[:, 8 * ei:8 * (ei + 1)].unsqueeze(1)
                                .broadcast_to([128, NH, 8]), op=ALU.mult),
                                waits=[("s", exp8_done)], inc="v")
                    ctx.op("vector", lambda e, t4=t4: e.tensor_reduce(
                        Ssum2[:, 64 * t4:64 * t4 + 16],
                        w8_2[:, 128 * t4:128 * (t4 + 1)]
                        .rearrange("p (h o) -> p h o", o=8),
                        axis=AX.X, op=ALU.add),
                        waits=[("s", exp8_done), ("v", ctx.val("v"))], inc="v")
                    ctx.op("vector", lambda e, t4=t4: e.tensor_reduce(
                        Ssum2[:, 64 * t4 + 16:64 * t4 + 32],
                        p1d8[:].rearrange("p (h o) -> p h o", o=8),
                        axis=AX.X, op=ALU.add),
                        waits=[("v", ctx.val("v"))], inc="v")
                    ctx.op("vector", lambda e, t4=t4: e.tensor_copy(
                        Ab[:], Acol2[:, 16 * t4:16 * (t4 + 1)]),
                        waits=[("s", AB_done), ("v", ctx.val("v"))], inc="v")
                    ctx.op("vector", lambda e, t4=t4: e.tensor_tensor(
                        tg8_2[:, 128 * t4:128 * (t4 + 1)]
                        .rearrange("p (h o) -> p h o", o=8),
                        p1d8[:].rearrange("p (h o) -> p h o", o=8),
                        Ab[:].unsqueeze(2).broadcast_to([128, NH, 8]), op=ALU.mult),
                        waits=[("v", ctx.val("v"))], inc="v")
                    ctx.op("vector", lambda e, t4=t4: e.tensor_tensor(
                        tg8_2[:, 128 * t4:128 * (t4 + 1)],
                        w8_2[:, 128 * t4:128 * (t4 + 1)],
                        tg8_2[:, 128 * t4:128 * (t4 + 1)], op=ALU.subtract),
                        waits=[("v", ctx.val("v"))], inc="v")
                    g8_done[t4] = ctx.val("v")

                def sched_band3(t4):
                    t = tp0 + t4
                    vsrc = bass.AP(vtok_d.tensor, (128 * t) * D, [[D, 128], [D, 7], [1, D]])
                    ctx.dma("gpsimd", win.rearrange("p (o d) -> p o d", d=D), vsrc,
                            waits=[("v", g8_done[t4]), ("v", ctx.val("v")),
                                   ("dma_out", ctx.val("dma_out"))],
                            inc="dma_win")
                    vwin_ld = ctx.val("dma_win")
                    for o in range(7):
                        ctx.op("vector", lambda e, o=o, t4=t4: e.tensor_tensor(
                            prod[:].rearrange("p (h d) -> p h d", d=64),
                            win[:, D * o:D * (o + 1)].rearrange("p (h d) -> p h d", d=64),
                            tg8_2[:, 128 * t4:128 * (t4 + 1)]
                            .rearrange("p (h o) -> p h o", o=8)[:, :, o:o + 1]
                            .broadcast_to([128, NH, 64]), op=ALU.mult),
                            waits=[("dma_win", vwin_ld), ("v", ctx.val("v"))], inc="v")
                        if o == 0:
                            ctx.op("vector", lambda e, t4=t4: e.tensor_copy(
                                accb2[:, D * t4:D * (t4 + 1)], prod[:]),
                                waits=[("v", ctx.val("v"))], inc="v")
                        else:
                            ctx.op("vector", lambda e, t4=t4: e.tensor_tensor(
                                accb2[:, D * t4:D * (t4 + 1)],
                                accb2[:, D * t4:D * (t4 + 1)], prod[:], op=ALU.add),
                                waits=[("v", ctx.val("v"))], inc="v")
                    accb_done[t4] = ctx.val("v")

                for h in range(nh):
                    sched_st(h)
                    if h == 6:
                        sched_band2(0)
                        sched_band2(1)
                    if h == 9:
                        sched_band3(0)
                        sched_band3(1)
                    if h > 0:
                        sched_fu(h - 1)
                sched_fu(nh - 1)
                FU_done = ctx.val("s")

                for t4 in range(2):
                    t = tp0 + t4
                    qdT = qdT2[:, 144 * t4:144 * (t4 + 1)]
                    Acol = Acol2[:, 16 * t4:16 * (t4 + 1)]
                    Bcol = Bcol2[:, 16 * t4:16 * (t4 + 1)]
                    Fsb = Fsb2[:, 1040 * t4:1040 * (t4 + 1)]
                    Usb = Usb2[:, 1040 * t4:1040 * (t4 + 1)]
                    Ssum = Ssum2[:, 64 * t4:64 * (t4 + 1)]
                    accb = accb2[:, D * t4:D * (t4 + 1)]
                    ctx.op("tensor", lambda e, t4=t4: e.transpose(
                        pT[0][:], w8_2[:, 128 * t4:128 * (t4 + 1)], idb[:]),
                        waits=[("v", ctx.val("v"))], inc="t")
                    ctx.op("vector", lambda e: e.tensor_copy(w8T[:], pT[0][:]),
                           waits=[("t", ctx.val("t"))], inc="v")
                    w8T_done = ctx.val("v")
                    for gg in range(2):
                        ctx.op("tensor", lambda e, gg=gg: e.matmul(
                            pqd, w8T[:], tvbigs[:, 512 * gg:512 * (gg + 1)],
                            start=True, stop=True),
                            waits=[("v", w8T_done), ("v", ctx.val("v"))], inc="t")
                        ctx.op("vector", lambda e, gg=gg: e.tensor_copy(
                            tvb_sb[:, 512 * gg:512 * (gg + 1)], pqd),
                            waits=[("t", ctx.val("t"))], inc="v")
                    tvb_done = ctx.val("v")
                    tvb_free[0] = tvb_done
                    F3 = Fsb.rearrange("p (h x) -> p h x", x=65)
                    U3 = Usb.rearrange("p (h x) -> p h x", x=65)
                    ctx.op("vector", lambda e, F3=F3, U3=U3: e.tensor_tensor(
                        t0f[:].rearrange("p (h d) -> p h d", d=64),
                        F3[:, :, 0:64], U3[:, :, 0:64], op=ALU.subtract),
                        waits=[("s", FU_done), ("v", ctx.val("v")),
                               ("dma_out", ctx.val("dma_out"))], inc="v")
                    ctx.op("vector", lambda e, Acol=Acol: e.tensor_tensor(
                        numb[:].rearrange("p (h d) -> p h d", d=64),
                        t0f[:].rearrange("p (h d) -> p h d", d=64),
                        Acol.unsqueeze(2).broadcast_to([128, NH, 64]), op=ALU.mult),
                        waits=[("v", ctx.val("v")), ("s", AB_done)], inc="v")
                    ctx.op("vector", lambda e, U3=U3, Bcol=Bcol: e.tensor_tensor(
                        tb[:].rearrange("p (h d) -> p h d", d=64),
                        U3[:, :, 0:64],
                        Bcol.unsqueeze(2).broadcast_to([128, NH, 64]), op=ALU.mult),
                        waits=[("v", ctx.val("v"))], inc="v")
                    ctx.op("vector", lambda e: e.tensor_tensor(
                        numb[:], numb[:], tb[:], op=ALU.add),
                        waits=[("v", ctx.val("v"))], inc="v")
                    ctx.op("vector", lambda e, accb=accb, t4=t4: e.tensor_tensor(
                        numb[:], numb[:], accb, op=ALU.add),
                        waits=[("v", accb_done[t4])], inc="v")
                    ctx.op("vector", lambda e: e.tensor_tensor(
                        numb[:], numb[:], tvb_sb[:], op=ALU.add),
                        waits=[("v", tvb_done)], inc="v")
                    ctx.op("vector", lambda e, F3=F3, U3=U3, Ssum=Ssum: e.tensor_tensor(
                        Ssum[:, 32:48], F3[:, :, 64:65].squeeze(2),
                        U3[:, :, 64:65].squeeze(2), op=ALU.subtract),
                        waits=[("v", ctx.val("v")), ("s", FU_done)], inc="v")
                    ctx.op("vector", lambda e, Ssum=Ssum: e.tensor_tensor(
                        Ssum[:, 32:48], Ssum[:, 32:48], Ssum[:, 16:32],
                        op=ALU.subtract), waits=[("v", ctx.val("v"))], inc="v")
                    ctx.op("vector", lambda e, Ssum=Ssum, Acol=Acol: e.tensor_tensor(
                        den[:], Acol, Ssum[:, 32:48], op=ALU.mult),
                        waits=[("v", ctx.val("v"))], inc="v")
                    ctx.op("vector", lambda e, Ssum=Ssum, U3=U3, Bcol=Bcol: e.tensor_tensor(
                        Ssum[:, 48:64], Bcol, U3[:, :, 64:65].squeeze(2),
                        op=ALU.mult), waits=[("v", ctx.val("v")), ("s", AB_done)], inc="v")
                    ctx.op("vector", lambda e, Ssum=Ssum: e.tensor_tensor(
                        den[:], den[:], Ssum[:, 48:64], op=ALU.add),
                        waits=[("v", ctx.val("v"))], inc="v")
                    ctx.op("vector", lambda e, Ssum=Ssum: e.tensor_tensor(
                        den[:], den[:], Ssum[:, 0:16], op=ALU.add),
                        waits=[("v", ctx.val("v"))], inc="v")
                    ctx.op("vector", lambda e, Ssum=Ssum, Acol=Acol: e.tensor_tensor(
                        Ssum[:, 32:48], Acol, Ssum[:, 32:48], op=ALU.mult),
                        waits=[("v", ctx.val("v"))], inc="v")
                    ctx.op("vector", lambda e, Ssum=Ssum: e.tensor_tensor(
                        tb[:].rearrange("p (h d) -> p h d", d=64),
                        tv0b[:].rearrange("p (h d) -> p h d", d=64),
                        Ssum[:, 32:48].unsqueeze(2).broadcast_to([128, NH, 64]),
                        op=ALU.mult), waits=[("v", ctx.val("v"))], inc="v")
                    ctx.op("vector", lambda e: e.tensor_tensor(
                        numb[:], numb[:], tb[:], op=ALU.add), inc="v")
                    ctx.op("vector", lambda e, Ssum=Ssum: e.tensor_tensor(
                        tb[:].rearrange("p (h d) -> p h d", d=64),
                        tv8b[:].rearrange("p (h d) -> p h d", d=64),
                        Ssum[:, 48:64].unsqueeze(2).broadcast_to([128, NH, 64]),
                        op=ALU.mult), waits=[("v", ctx.val("v"))], inc="v")
                    ctx.op("vector", lambda e: e.tensor_tensor(
                        numb[:], numb[:], tb[:], op=ALU.add), inc="v")
                    ctx.op("vector", lambda e: e.reciprocal(rden[:], den[:]),
                           waits=[("v", ctx.val("v"))], inc="v")
                    ctx.op("vector", lambda e: e.tensor_tensor(
                        ao[:].rearrange("p (h d) -> p h d", d=64),
                        numb[:].rearrange("p (h d) -> p h d", d=64),
                        rden[:].unsqueeze(2).broadcast_to([128, NH, 64]), op=ALU.mult),
                        waits=[("v", ctx.val("v"))], inc="v")
                    ao_done = ctx.val("v")
                    ctx.dma("sync", xres[:, 0:D], x1[b, 128 * t:128 * (t + 1), :],
                            waits=[("v", ctx.val("v"))], inc="dma")
                    x_ld = ctx.val("dma")
                    ev = {}
                    w0 = ctx.val("v")
                    for c in range(DC):
                        ctx.op("tensor", lambda e, c=c: e.transpose(
                            pT[c % 2][:], ao[:, 128 * c:128 * (c + 1)], idb[:]),
                            waits=([("v", w0)] if c < 2 else [("v", ev[c - 2])]),
                            inc="t")
                        ctx.op("vector", lambda e, c=c: e.tensor_copy(
                            aoT[:, 128 * c:128 * (c + 1)], pT[c % 2][:]),
                            waits=[("t", ctx.val("t"))], inc="v")
                        ev[c] = ctx.val("v")
                    aoT_done = ctx.val("v")
                    pT_free[0] = aoT_done
                    op_free = ctx.val("v")
                    for g2 in range(2):
                        for c in range(DC):
                            ctx.op("tensor", lambda e, g2=g2, c=c: e.matmul(
                                pOP[:],
                                aoT[:, 128 * c:128 * (c + 1)],
                                w_sb[:, c * D + 512 * g2: c * D + 512 * (g2 + 1)],
                                start=(c == 0), stop=(c == DC - 1)),
                                waits=([("v", aoT_done if g2 == 0 else op_free),
                                        ("s", ctx.val("s")),
                                        ("dma_w", wo_ld)]
                                       if c == 0 else []),
                                inc="t")
                        ctx.op("vector", lambda e, g2=g2: e.tensor_tensor(
                            po[:, 512 * g2:512 * (g2 + 1)],
                            pOP[:],
                            bvob[:, D + 512 * g2: D + 512 * (g2 + 1)], op=ALU.add),
                            waits=[("t", ctx.val("t")), ("dma", x_ld),
                                   ("dma_out", ctx.val("dma_out"))], inc="v")
                        ctx.op("vector", lambda e, g2=g2: e.tensor_tensor(
                            po[:, 512 * g2:512 * (g2 + 1)],
                            po[:, 512 * g2:512 * (g2 + 1)],
                            xres[:, 512 * g2:512 * (g2 + 1)], op=ALU.add), inc="v")
                        op_free = ctx.val("v")
                    ctx.dma("sync", part2[b, 128 * t:128 * (t + 1), :], po[:],
                            waits=[("v", ctx.val("v"))], inc="dma_out")

        ctx.flush(block)


def l1att_consts(inputs):
    biases = np.concatenate([np.asarray(inputs["L1_attn_b" + n], np.float32)
                             for n in "qkvo"]).reshape(1, 4 * D)
    tk = np.asarray(inputs["L1_attn_table_k"], np.float32)
    tv = np.asarray(inputs["L1_attn_table_v"], np.float32)
    tkbig = np.zeros((D, NH * 9), np.float32)
    for h in range(NH):
        tkbig[64 * h:64 * (h + 1), 9 * h:9 * (h + 1)] = tk.T
    tvbig = np.zeros((128, D), np.float32)
    for h in range(NH):
        for o in range(7):
            tvbig[h * 8 + o, h * 64:(h + 1) * 64] = tv[o + 1]
    emsk = np.ones((2, 128, 8), np.float32)
    emsk[:, :, 7] = 0.0
    for p in range(128):
        for o in range(7):
            if p + (o - 3) < 0:
                emsk[0, p, o] = 0.0
            if (NT - 1) * 128 + p + (o - 3) > T - 1:
                emsk[1, p, o] = 0.0
    return {
        "wq": np.asarray(inputs["L1_attn_Wq"], np.float32),
        "wk": np.asarray(inputs["L1_attn_Wk"], np.float32),
        "wv": np.asarray(inputs["L1_attn_Wv"], np.float32),
        "wo": np.asarray(inputs["L1_attn_Wo"], np.float32),
        "biases": biases, "tkbig": tkbig, "tvbig": tvbig,
        "tv0r": np.tile(tv[0], NH).reshape(1, D),
        "tv8r": np.tile(tv[8], NH).reshape(1, D),
        "emsk": emsk, "zpad": np.zeros((4, D), np.float32),
        "idn": np.eye(128, dtype=np.float32),
        "ones1": np.ones((1, 128), np.float32),
    }


def emit_l1ffn(nc, ctx, block, io):
    xdT = io["xdT"]
    part2 = io["part2"]
    wdT_d = io["wdT_d"]
    bias = io["l1fb"]
    idn = io["idn"]
    ones1 = io["ones1"]
    out = io["out"]

    with contextlib.ExitStack() as es:
        sb = lambda *a: es.enter_context(nc.sbuf_tensor(*a))
        ps = lambda *a: es.enter_context(nc.psum_tensor(*a))
        idb = sb("idb", [128, 128], BF16)
        onesr = sb("onesr", [1, 128], F32)
        brow = sb("brow", [1, D], F32)
        bb = sb("bb", [128, D], F32)
        wdT = sb("wdT", [128, DC * D], BF16)
        xdT_sb = sb("xdT_sb", [128, DC * T], BF16)
        po = sb("po", [128, D], F32)
        pT0 = ps("pT0", [128, 128], BF16)
        pT1 = ps("pT1", [128, 128], BF16)
        pmm0 = ps("pmm0", [128, 512], F32)
        pmm1 = ps("pmm1", [128, 512], F32)
        pbc = ps("pbc", [128, 512], F32)
        pT = [pT0, pT1]
        pmm = [pmm0, pmm1]

        ctx.dma("gpsimd", idb[:], idn[:], inc="dmag")
        ctx.dma("sync", onesr[:], ones1[:], inc="dma")
        ctx.dma("sync", brow[:], bias[:], inc="dma")
        const_ld = ctx.val("dma")
        ctx.wait("tensor", "dma", const_ld)
        ctx.wait("tensor", "dmag", ctx.val("dmag"))
        ctx.wait("vector", "dma", const_ld)
        emit_bcast_row(ctx, brow[:], bb, onesr[:], pbc, D)

        ctx.dma("sync", wdT[:].rearrange("p (c q) -> p c q", q=D),
                wdT_d.rearrange("c p q -> p c q"), inc="dma")

        for b in range(BPC):
            ctx.dma("sync", xdT_sb[:].rearrange("p (c q) -> p c q", q=T),
                    xdT[b].rearrange("c p q -> p c q"),
                    waits=[("t", ctx.val("t"))], inc="dma")
            xd_ld = ctx.val("dma")
            for t in range(NT):
                for g2 in range(2):
                    for c in range(DC):
                        ctx.op("tensor", lambda e, t=t, g2=g2, c=c: e.matmul(
                            pmm[g2][:],
                            xdT_sb[:, c * T + 128 * t: c * T + 128 * (t + 1)],
                            wdT[:, c * D + 512 * g2: c * D + 512 * (g2 + 1)],
                            start=(c == 0), stop=(c == DC - 1)),
                            waits=([("dma", xd_ld),
                                    ("v", ctx.val("v"))] if c == 0 else []),
                            inc="t")
                ctx.dma("sync", po[:], part2[b, 128 * t:128 * (t + 1), :],
                        waits=[("dma_out", ctx.val("dma_out"))], inc="dma")
                p_ld = ctx.val("dma")
                for g2 in range(2):
                    ctx.op("vector", lambda e, g2=g2: e.tensor_tensor(
                        po[:, 512 * g2:512 * (g2 + 1)],
                        po[:, 512 * g2:512 * (g2 + 1)],
                        pmm[g2][:], op=ALU.add),
                        waits=[("t", ctx.val("t")), ("dma", p_ld)], inc="v")
                    ctx.op("vector", lambda e, g2=g2: e.tensor_tensor(
                        po[:, 512 * g2:512 * (g2 + 1)],
                        po[:, 512 * g2:512 * (g2 + 1)],
                        bb[:, 512 * g2:512 * (g2 + 1)], op=ALU.add), inc="v")
                ctx.dma("sync", out[b, 128 * t:128 * (t + 1), :], po[:],
                        waits=[("v", ctx.val("v"))], inc="dma_out")

        ctx.flush(block)


ALL_SEMS = ("dma", "dmag", "dma_out", "dma_w", "dma_win", "dma_wt",
            "t", "v", "s", "g", "t_wt", "v_wt", "t_misc", "v_misc")

IO_SPEC = {
    "x0": ([BPC, T, D], F32),
    "l0wg": ([D, NE], F32),
    "l0wq": ([D, D], F32), "l0wk": ([D, D], F32),
    "l0wv": ([D, D], F32), "l0wo": ([D, D], F32),
    "biases0": ([1, 4 * D], F32),
    "W1": ([NE, D, HMoE], F32), "b1": ([NE, HMoE], F32),
    "W2": ([NE, HMoE, D], F32), "b2": ([NE, D], F32),
    "l1wq": ([D, D], F32), "l1wk": ([D, D], F32),
    "l1wv": ([D, D], F32), "l1wo": ([D, D], F32),
    "biases1": ([1, 4 * D], F32),
    "tkbig": ([D, NH * 9], F32), "tvbig": ([128, D], F32),
    "tv0r": ([1, D], F32), "tv8r": ([1, D], F32),
    "emsk": ([2, 128, 8], F32), "zpad": ([4, D], F32),
    "l1fw": ([D, D], F32), "l1fb": ([1, D], F32),
    "idn": ([128, 128], F32), "ones1": ([1, 128], F32),
    "xnT": ([BPC, DC, 128, T], BF16),
    "scores": ([BPC, 128, NT, NE], F32),
    "part1": ([BPC, T, D], F32),
    "x1": ([BPC, T, D], F32),
    "xn1T": ([BPC, DC, 128, T], BF16),
    "xdT": ([BPC, DC, 128, T], BF16),
    "wdT_d": ([DC, 128, D], BF16),
    "part2": ([BPC, T, D], F32),
    "ktok_d": ([T + 6, D], BF16),
    "vtok_d": ([T + 6, D], BF16),
    "out": ([BPC, T, D], F32),
}

PHASES = {
    "l0ln": (["x0", "l0wg"], ["xnT", "scores"], []),
    "l0att": (["x0", "xnT", "l0wq", "l0wk", "l0wv", "l0wo", "biases0",
               "idn", "ones1"], ["part1"], []),
    "l0moe": (["xnT", "scores", "part1", "W1", "b1", "W2", "b2",
               "idn", "ones1"], ["x1"], []),
    "l1ln": (["x1", "l1fw", "idn"], ["xn1T", "xdT", "wdT_d"], []),
    "l1att": (["x1", "xn1T", "l1wq", "l1wk", "l1wv", "l1wo", "biases1",
               "tkbig", "tvbig", "tv0r", "tv8r", "emsk", "zpad",
               "idn", "ones1"], ["part2"], ["ktok_d", "vtok_d"]),
    "l1ffn": (["xdT", "part2", "wdT_d", "l1fb", "idn", "ones1"], ["out"], []),
}


def _declare(nc, names, kind):
    return {n: nc.dram_tensor(n, IO_SPEC[n][0], IO_SPEC[n][1], kind=kind).ap()
            for n in names}


def build_single(phase):
    ins, outs, internals = PHASES[phase]
    nc = bass.Bass("TRN2", target_bir_lowering=False)
    io = {}
    io.update(_declare(nc, ins, "ExternalInput"))
    io.update(_declare(nc, outs, "ExternalOutput"))
    io.update(_declare(nc, internals, "Internal"))
    with nc.Block() as block:
        ctx = Ctx(nc)
        for sname in ALL_SEMS:
            ctx.sem(sname)
        EMITTERS[phase](nc, ctx, block, io)
        ctx.wait("sync", "dma_out")
        ctx.flush(block)
    return nc


FUSED_ORDER = ["l0ln", "l0att", "l0moe", "l1ln", "l1att", "l1ffn"]

MODULE_INS = ["x0", "l0wg", "l0wq", "l0wk", "l0wv", "l0wo", "biases0",
              "W1", "b1", "W2", "b2",
              "l1wq", "l1wk", "l1wv", "l1wo", "biases1",
              "tkbig", "tvbig", "tv0r", "tv8r", "emsk", "zpad",
              "l1fw", "l1fb", "idn", "ones1"]
HANDOFFS = ["xnT", "scores", "part1", "x1", "xn1T", "xdT", "wdT_d", "part2",
            "ktok_d", "vtok_d"]


class _NcProxy:

    def __init__(self, nc, prefix):
        self._nc = nc
        self._p = prefix

    def sbuf_tensor(self, name, *a, **k):
        return self._nc.sbuf_tensor(self._p + name, *a, **k)

    def psum_tensor(self, name, *a, **k):
        return self._nc.psum_tensor(self._p + name, *a, **k)

    def __getattr__(self, item):
        return getattr(self._nc, item)


def build_fused():
    nc = bass.Bass("TRN2", target_bir_lowering=False)
    io = {}
    io.update(_declare(nc, MODULE_INS, "ExternalInput"))
    io.update(_declare(nc, ["out"], "ExternalOutput"))
    io.update(_declare(nc, HANDOFFS, "Internal"))
    with nc.Block() as block:
        ctx = Ctx(nc)
        for sname in ALL_SEMS:
            ctx.sem(sname)
        for i, phase in enumerate(FUSED_ORDER):
            if i > 0:
                for eng in ENGS:
                    for sname in ALL_SEMS:
                        ctx.wait(eng, sname)
            EMITTERS[phase](_NcProxy(nc, phase + "_"), ctx, block, io)
        ctx.wait("sync", "dma_out")
        ctx.flush(block)
    return nc


def _shard_inputs(inputs):
    x = np.ascontiguousarray(inputs["L0_attn_x"])
    return [np.ascontiguousarray(x[c * BPC:(c + 1) * BPC]) for c in range(NCORES)]


def module_consts(inputs):
    d = l1att_consts(inputs)
    consts = {
        "l0wg": np.ascontiguousarray(np.asarray(inputs["L0_ffn_Wg"], np.float32)),
        "l0wq": np.asarray(inputs["L0_attn_wq"], np.float32),
        "l0wk": np.asarray(inputs["L0_attn_wk"], np.float32),
        "l0wv": np.asarray(inputs["L0_attn_wv"], np.float32),
        "l0wo": np.asarray(inputs["L0_attn_wo"], np.float32),
        "biases0": np.stack([np.asarray(inputs["L0_attn_b" + n], np.float32)
                             for n in "qkvo"]).reshape(1, 4 * D),
        "W1": np.asarray(inputs["L0_ffn_W1"], np.float32),
        "b1": np.asarray(inputs["L0_ffn_b1"], np.float32),
        "W2": np.asarray(inputs["L0_ffn_W2"], np.float32),
        "b2": np.asarray(inputs["L0_ffn_b2"], np.float32),
        "l1wq": d["wq"], "l1wk": d["wk"], "l1wv": d["wv"], "l1wo": d["wo"],
        "biases1": d["biases"],
        "tkbig": d["tkbig"], "tvbig": d["tvbig"],
        "tv0r": d["tv0r"], "tv8r": d["tv8r"],
        "emsk": d["emsk"], "zpad": d["zpad"],
        "l1fw": np.asarray(inputs["L1_ffn_weight"], np.float32),
        "l1fb": np.asarray(inputs["L1_ffn_bias"], np.float32).reshape(1, D),
        "idn": d["idn"], "ones1": d["ones1"],
    }
    return consts


def kernel(**inputs):
    nc = build_fused()
    shards = _shard_inputs(inputs)
    consts = module_consts(inputs)
    in_maps = [dict(consts, x0=shards[c]) for c in range(NCORES)]
    res = run_bass_kernel_spmd(nc, in_maps, core_ids=list(range(NCORES)))
    return np.concatenate([np.asarray(r["out"]) for r in res.results], axis=0)


def kernel_unfused(**inputs):
    res1 = run_l0ln(inputs)
    xnT = [np.asarray(r["xnT"]) for r in res1]
    scores = [np.asarray(r["scores"]) for r in res1]
    res2 = run_l0att(inputs, xnT)
    part1 = [np.asarray(r["part1"]) for r in res2]
    res3 = run_l0moe(inputs, xnT, scores, part1)
    x1 = [np.asarray(r["x1"]) for r in res3]
    res4 = run_l1ln(x1)
    xn1T = [np.asarray(r["xn1T"]) for r in res4]
    xdT = [np.asarray(r["xdT"]) for r in res4]
    res5 = run_l1att(inputs, x1, xn1T)
    part2 = [np.asarray(r["part2"]) for r in res5]
    res6 = run_l1ffn(inputs, xdT, part2)
    return np.concatenate([np.asarray(r["out"]) for r in res6], axis=0)


def run_l0ln(inputs):
    nc = build_single("l0ln")
    shards = _shard_inputs(inputs)
    wg = np.ascontiguousarray(np.asarray(inputs["L0_ffn_Wg"], np.float32))
    in_maps = [{"x0": shards[c], "l0wg": wg} for c in range(NCORES)]
    return run_bass_kernel_spmd(nc, in_maps, core_ids=list(range(NCORES))).results


def run_l0att(inputs, xnT_np):
    nc = build_single("l0att")
    shards = _shard_inputs(inputs)
    cc = module_consts(inputs)
    keys = ["l0wq", "l0wk", "l0wv", "l0wo", "biases0", "idn", "ones1"]
    in_maps = [dict({k: cc[k] for k in keys}, x0=shards[c], xnT=xnT_np[c])
               for c in range(NCORES)]
    return run_bass_kernel_spmd(nc, in_maps, core_ids=list(range(NCORES))).results


def run_l0moe(inputs, xnT_np, scores_np, part1_np):
    nc = build_single("l0moe")
    cc = module_consts(inputs)
    keys = ["W1", "b1", "W2", "b2", "idn", "ones1"]
    in_maps = [dict({k: cc[k] for k in keys}, xnT=xnT_np[c], scores=scores_np[c],
                    part1=part1_np[c]) for c in range(NCORES)]
    return run_bass_kernel_spmd(nc, in_maps, core_ids=list(range(NCORES))).results


def run_l1ln(x1_np):
    nc = build_single("l1ln")
    idn = np.eye(128, dtype=np.float32)
    in_maps = [{"x1": x1_np[c], "idn": idn} for c in range(NCORES)]
    return run_bass_kernel_spmd(nc, in_maps, core_ids=list(range(NCORES))).results


def run_l1att(inputs, x1_np, xn1T_np):
    nc = build_single("l1att")
    cc = module_consts(inputs)
    keys = ["l1wq", "l1wk", "l1wv", "l1wo", "biases1", "tkbig", "tvbig",
            "tv0r", "tv8r", "emsk", "zpad", "idn", "ones1"]
    in_maps = [dict({k: cc[k] for k in keys}, x1=x1_np[c], xn1T=xn1T_np[c])
               for c in range(NCORES)]
    return run_bass_kernel_spmd(nc, in_maps, core_ids=list(range(NCORES))).results


def run_l1ffn(inputs, xdT_np, part2_np):
    nc = build_single("l1ffn")
    cc = module_consts(inputs)
    keys = ["l1fw", "l1fb", "idn", "ones1"]
    in_maps = [dict({k: cc[k] for k in keys}, xdT=xdT_np[c], part2=part2_np[c])
               for c in range(NCORES)]
    return run_bass_kernel_spmd(nc, in_maps, core_ids=list(range(NCORES))).results


EMITTERS = {
    "l0ln": emit_l0ln, "l0att": emit_l0att, "l0moe": emit_l0moe,
    "l1ln": emit_l1ln, "l1att": emit_l1att, "l1ffn": emit_l1ffn,
}
